# version 03
# baseline (speedup 1.0000x reference)
import numpy as np
import ml_dtypes
from contextlib import ExitStack

import concourse.bass as bass
import concourse.mybir as mybir
from concourse.bass_utils import run_bass_kernel_spmd
from concourse.alu_op_type import AluOpType as OP

F32 = mybir.dt.float32
BF16 = mybir.dt.bfloat16
U32 = mybir.dt.uint32
AF = mybir.ActivationFunctionType

S, D, NT = 4096, 1024, 32
NH = 16
PKH, DI, NE = 8, 2048, 16384
EPS = 1e-5


class Ctr:
    def __init__(self, es, nc, name):
        self.h = es.enter_context(nc.semaphore(name))
        self.n = 0
        self.maxw = 0
        self.seen = {}

    def bump(self, ins, k=1):
        ins.then_inc(self.h, k)
        self.n += k
        return self.n

    def wait(self, eng, v=None):
        v = self.n if v is None else v
        if v <= 0:
            return
        key = id(eng)
        if self.seen.get(key, -1) >= v:
            return
        self.seen[key] = v
        self.maxw = max(self.maxw, v)
        eng.wait_ge(self.h, v)


def build_nc(phases=("A", "B1", "C", "B2"), taps=()):
    nc = bass.Bass("TRN2", target_bir_lowering=False)
    es = ExitStack()
    tap_set = set(taps)

    dt = nc.dram_tensor
    x_h = dt("x", [S, D], F32, kind="ExternalInput").ap()
    gamma_row_h = dt("gamma_row", [1, D], F32, kind="ExternalInput").ap()
    beta_row_h = dt("beta_row", [1, D], F32, kind="ExternalInput").ap()
    wqkv_h = dt("W_qkv", [3 * D, D], F32, kind="ExternalInput").ap()
    bqk_h = dt("bqk_r", [128, NH], F32, kind="ExternalInput").ap()
    bv_h = dt("bv_r", [64, NH], F32, kind="ExternalInput").ap()
    wouta_h = dt("W_out_a", [D, D], F32, kind="ExternalInput").ap()
    bouta_h = dt("b_out_a", [1, D], F32, kind="ExternalInput").ap()
    qkvw_h = dt("attb_qkv_w", [D, D], F32, kind="ExternalInput").ap()
    tempc_h = dt("attb_tempc", [128, 8], F32, kind="ExternalInput").ap()
    outw_h = dt("attb_out_w", [D, D], F32, kind="ExternalInput").ap()
    outb_h = dt("attb_out_b", [1, D], F32, kind="ExternalInput").ap()
    wq_h = dt("ffn_Wq", [8 * D, D], F32, kind="ExternalInput").ap()
    keys_h = dt("ffn_keys", [PKH, 128, 2, 512], F32, kind="ExternalInput").ap()
    win_h = dt("ffn_W_in", [DI, D], F32, kind="ExternalInput").ap()
    wout_h = dt("ffn_W_out", [D, DI], F32, kind="ExternalInput").ap()
    tbl = {"ina": dt("ffn_in_a", [NE, D], F32, kind="ExternalInput").ap(),
           "inb": dt("ffn_in_b", [NE, DI], F32, kind="ExternalInput").ap(),
           "outa": dt("ffn_out_a", [NE, DI], F32, kind="ExternalInput").ap(),
           "outb": dt("ffn_out_b", [NE, D], F32, kind="ExternalInput").ap()}
    eye_h = dt("c_eye_bf", [128, 128], BF16, kind="ExternalInput").ap()
    eyef_h = dt("c_eye_f", [128, 128], F32, kind="ExternalInput").ap()
    ones_r_h = dt("c_ones_1x128_bf", [1, 128], BF16, kind="ExternalInput").ap()
    ones_rf_h = dt("c_ones_1x128_f", [1, 128], F32, kind="ExternalInput").ap()
    ones_c_h = dt("c_ones_128x1_bf", [128, 1], BF16, kind="ExternalInput").ap()
    emat_h = dt("c_emat_bf", [16, D], BF16, kind="ExternalInput").ap()
    eye16f_h = dt("c_eye16_f", [16, 16], F32, kind="ExternalInput").ap()

    out_h = dt("y", [S, D], F32, kind="ExternalOutput").ap()
    ya_h = dt("ya", [S, D], F32, kind="Internal").ap()
    yb_h = dt("yb", [S, D], F32, kind="Internal").ap()
    ctx_h = dt("ctxs", [128, 8, S], BF16,
               kind="ExternalOutput" if "ctx" in tap_set else "Internal").ap()
    tap_hbm = {t: dt(f"tap_{t}", [S, D], F32, kind="ExternalOutput").ap()
               for t in tap_set if t.startswith("y")}
    if "xn" in tap_set:
        tap_hbm["xn"] = dt("tap_xn", [128, 8, S], F32, kind="ExternalOutput").ap()
    if "idx" in tap_set:
        tap_hbm["idx"] = dt("tap_idx", [128, PKH, NT], U32, kind="ExternalOutput").ap()
        tap_hbm["gate"] = dt("tap_gate", [128, PKH, NT], F32, kind="ExternalOutput").ap()
    if "qkv0" in tap_set:
        tap_hbm["q0"] = dt("tap_q0", [64, S], F32, kind="ExternalOutput").ap()
        tap_hbm["k0"] = dt("tap_k0", [64, S], F32, kind="ExternalOutput").ap()
        tap_hbm["v0"] = dt("tap_v0", [64, S], F32, kind="ExternalOutput").ap()

    ec = es.enter_context
    st = nc.sbuf_tensor

    eye = ec(st("eye", [128, 128], BF16))
    eyef = ec(st("eyef", [128, 128], F32))
    eye16f = ec(st("eye16f", [16, 16], F32))
    ones_r = ec(st("ones_r", [1, 128], BF16))
    ones_rf = ec(st("ones_rf", [1, 128], F32))
    ones_c = ec(st("ones_c", [128, 1], BF16))
    emat = ec(st("emat", [16, D], BF16))
    xT = ec(st("xT", [128, 8, S], BF16))
    rsc = ec(st("rsc", [128, NT], F32))
    rbi = ec(st("rbi", [128, NT], F32))
    st6 = ec(st("st6", [128, 12], F32))
    aggr = ec(st("aggr", [128, 2], F32))
    ta = ec(st("ta", [128, 1], F32))
    tb = ec(st("tb", [128, 1], F32))
    tc = ec(st("tc", [128, 1], F32))
    xtile = ec(st("xtile", [128, 2, D], F32))
    ytile = ec(st("ytile", [128, D], F32))
    ntile = ec(st("ntile", [128, D], BF16))
    wstage = ec(st("wstage", [128, 2, D], BF16))

    ps_all = ec(nc.psum_tensor("ps_all", [128, 6, 512], F32))
    ps_c = ec(nc.psum_tensor("ps_c", [128, 512], F32))
    ps_tp = ec(nc.psum_tensor("ps_tp", [128, 512], F32))
    ps_tp_bf = ps_tp.bitcast(BF16)

    sx = Ctr(es, nc, "sx")
    sw = Ctr(es, nc, "sw")
    sg = Ctr(es, nc, "sg")
    so = Ctr(es, nc, "so")
    sm = Ctr(es, nc, "sm")
    ss = Ctr(es, nc, "ss")
    sv = Ctr(es, nc, "sv")
    scn = Ctr(es, nc, "scn")
    sco = Ctr(es, nc, "sco")

    block = es.enter_context(nc.Block())
    TE, VE, SE, SY, GE = (block.tensor, block.vector, block.scalar,
                          block.sync, block.gpsimd)

    def barrier():
        snap = [(c, c.n) for c in (sx, sw, sg, so, sm, ss, sv, scn, sco)]
        for sec in (TE, VE, SE, SY, GE):
            @sec
            def _(e):
                for c, v in snap:
                    c.wait(e, v)

    ytg = {"ss": 0, "so": 0}
    _scopes = {}
    tp_ss = [0]
    tp_sv = [0]
    wst_rd = [0, 0]

    def load_T(src_h, rows, cols, write_evac):
        ncc = cols // 128
        for j in range(rows // 128):
            buf = j % 2

            @GE
            def _(g):
                sm.wait(g, wst_rd[buf])
                g.dma_start(wstage[:, buf, :cols],
                            src_h[j * 128:(j + 1) * 128, :]).then_inc(sw.h, 16)
            sw.n += 16
            ldv = sw.n

            @TE
            def _(t):
                sw.wait(t, ldv)
                sv.wait(t, tp_sv[0])
                ss.wait(t, tp_ss[0])
                for cc in range(ncc):
                    ins = t.transpose(ps_tp_bf[:, cc * 128:(cc + 1) * 128],
                                      wstage[:, buf, cc * 128:(cc + 1) * 128],
                                      eye[:])
                    if cc == ncc - 1:
                        sm.bump(ins)
            wst_rd[buf] = sm.n
            write_evac(j, sm.n)
            tp_sv[0] = sv.n

    def simple_evacT(dstT):
        def f(j, tpv):
            @VE
            def _(v):
                sm.wait(v, tpv)
                sv.bump(v.tensor_copy(
                    dstT[:, :, j * 128:(j + 1) * 128],
                    ps_tp_bf.rearrange("p (c t) -> p c t", c=8)))
        return f

    def ln_chain(xin, out_bf, gb=None, z_out=None, store_affine=None,
                 wait_before=None):
        @VE
        def _(v):
            for c, val in (wait_before or []):
                c.wait(v, val)
            v.bn_stats(st6[:, 0:6], xin[:, 0:512])
            sv.bump(v.bn_stats(st6[:, 6:12], xin[:, 512:1024]))
        vst = sv.n

        @VE
        def _(v):
            sv.wait(v, vst)
            sv.bump(v.bn_aggr(aggr[:], st6[:]))
        vag = sv.n

        @VE
        def _(v):
            sv.wait(v, vag)
            sv.bump(v.tensor_scalar(ta[:], aggr[:, 1:2], EPS, None, OP.add))
        vta = sv.n

        @SE
        def _(s):
            sv.wait(s, vta)
            ss.bump(s.activation(tb[:], ta[:], AF.Ln))
        sln = ss.n

        @SE
        def _(s):
            ss.wait(s, sln)
            ss.bump(s.activation(tb[:], tb[:], AF.Exp, scale=-0.5))
        srs = ss.n

        @VE
        def _(v):
            ss.wait(v, srs)
            sv.bump(v.tensor_scalar(tc[:], aggr[:, 0:1], tb[:], -1.0,
                                    OP.mult, OP.mult))
        vtc = sv.n

        if store_affine is not None:
            sca, bia = store_affine

            @VE
            def _(v):
                sv.wait(v, vtc)
                v.tensor_scalar(sca, tb[:], 1.0, None, OP.add)
                sv.bump(v.tensor_scalar(bia, tc[:], 1.0, None, OP.mult))

        if gb is not None:
            gbc, bbc, zn0, z2 = gb
            tgt = zn0
        else:
            tgt = z_out if z_out is not None else out_bf

        @SE
        def _(s):
            sv.wait(s, vtc)
            ss.bump(s.activation(tgt[:], xin[:], AF.Identity,
                                 scale=tb[:], bias=tc[:]))
        szn = ss.n

        if gb is not None:
            @VE
            def _(v):
                ss.wait(v, szn)
                v.scalar_tensor_tensor(z2[:], zn0[:], 1.0, gbc[:],
                                       OP.mult, OP.mult)
                sv.bump(v.tensor_tensor(out_bf[:], z2[:], bbc[:], OP.add))
        return szn, ss.n, sv.n

    def tile_to_xT(i, src_bf, src_ctr, src_v):
        @TE
        def _(t):
            src_ctr.wait(t, src_v)
            sv.wait(t, tp_sv[0])
            ss.wait(t, tp_ss[0])
            for cc in range(8):
                ins = t.transpose(ps_tp_bf[:, cc * 128:(cc + 1) * 128],
                                  src_bf[:, cc * 128:(cc + 1) * 128], eye[:])
                if cc == 7:
                    sm.bump(ins)
        tpv = sm.n

        @VE
        def _(v):
            sm.wait(v, tpv)
            sv.bump(v.tensor_copy(
                xT[:, :, i * 128:(i + 1) * 128],
                ps_tp_bf.rearrange("p (c t) -> p c t", c=8)))
        tp_sv[0] = sv.n
        return sv.n

    def finish_tile(i, mmv, xld, buf, dst, affine_z, next_ln, tap, xguard):
        if affine_z is not None:
            @SE
            def _(s):
                sx.wait(s, xld)
                ss.bump(s.activation(affine_z[:], xtile[:, buf, :],
                                     AF.Identity, scale=rsc[:, i:i + 1],
                                     bias=rbi[:, i:i + 1]))
            rv = (ss, ss.n)
            xguard[i] = (ss, ss.n)
        else:
            rv = (sx, xld)

        yss, yso = ytg["ss"], ytg["so"]

        @VE
        def _(v):
            sm.wait(v, mmv)
            rv[0].wait(v, rv[1])
            ss.wait(v, yss)
            so.wait(v, yso)
            src = affine_z[:] if affine_z is not None else xtile[:, buf, :]
            sv.bump(v.tensor_tensor(
                ytile[:], src,
                ps_all[:, 4:6, :].rearrange("p a b -> p (a b)"), OP.add))
        yv = sv.n
        if affine_z is None:
            xguard[i] = (sv, yv)

        @SY
        def _(s):
            sv.wait(s, yv)
            s.dma_start(dst[i * 128:(i + 1) * 128, :],
                        ytile[:]).then_inc(so.h, 16)
            if tap is not None and tap in tap_hbm:
                s.dma_start(tap_hbm[tap][i * 128:(i + 1) * 128, :],
                            ytile[:]).then_inc(so.h, 16)
        so.n += 16 * (2 if (tap in tap_hbm) else 1)
        ytg["so"] = so.n

        if next_ln:
            szn, sdv, vdv = ln_chain(ytile, ntile, wait_before=[(sv, yv)])
            tile_to_xT(i, ntile, ss, sdv)
            ytg["ss"] = szn
        return yv

    @SY
    def _(s):
        for dsts, srcs in ((eye, eye_h), (eyef, eyef_h), (eye16f, eye16f_h),
                           (ones_r, ones_r_h), (ones_rf, ones_rf_h),
                           (ones_c, ones_c_h), (emat, emat_h)):
            s.dma_start(dsts[:], srcs).then_inc(sx.h, 16)
    sx.n += 7 * 16
    barrier()

    def phase_a():
        with ExitStack() as phx:
            pec = phx.enter_context
            gr_row = pec(st("a_gr", [1, D], F32))
            br_row = pec(st("a_br", [1, D], F32))
            gbc = pec(st("a_gbc", [128, D], F32))
            bbc = pec(st("a_bbc", [128, D], F32))
            ztile = pec(st("a_ztile", [128, D], F32))
            zn0 = pec(st("a_zn0", [128, D], F32))
            z2 = pec(st("a_z2", [128, D], F32))
            wqkvT = pec(st("a_wqkvT", [128, 8, 768], BF16))
            bqk_s = pec(st("a_bqk", [128, NH], F32))
            bv_s = pec(st("a_bv", [64, NH], F32))
            woutT = pec(st("a_woutT", [128, 8, D], BF16))
            bouta_s = pec(st("a_bouta", [1, D], BF16))
            qT = pec(st("a_qT", [64, S], BF16))
            kT = pec(st("a_kT", [64, S], BF16))
            vT = pec(st("a_vT", [64, S], BF16))
            vb = pec(st("a_vb", [128, NT, 66], BF16))
            pT = pec(st("a_pT", [128, 6, 512], BF16))
            recb = pec(st("a_recb", [1, 512], BF16))
            recf = pec(st("a_recf", [64, 512], F32))
            cstg = pec(st("a_cstg", [64, 512], BF16))
            ctile = pec(st("a_ctile", [128, 2, 8, 128], BF16))

            @SY
            def _(s):
                s.dma_start(bqk_s[:], bqk_h).then_inc(sx.h, 16)
                s.dma_start(bv_s[:], bv_h).then_inc(sx.h, 16)
                s.dma_start(gr_row[:], gamma_row_h).then_inc(sx.h, 16)
                s.dma_start(br_row[:], beta_row_h).then_inc(sx.h, 16)
            sx.n += 64
            gbrow = sx.n

            @GE
            def _(g):
                g.dma_start(bouta_s[:], bouta_h).then_inc(sw.h, 16)
            sw.n += 16

            @TE
            def _(t):
                sx.wait(t, gbrow)
                for half in range(2):
                    t.matmul(ps_all[:, 2 * half, :], ones_rf[:],
                             gr_row[:, half * 512:(half + 1) * 512],
                             start=True, stop=True)
                    ins = t.matmul(ps_all[:, 2 * half + 1, :], ones_rf[:],
                                   br_row[:, half * 512:(half + 1) * 512],
                                   start=True, stop=True)
                    if half == 1:
                        sm.bump(ins)
            gmmv = sm.n

            @VE
            def _(v):
                sm.wait(v, gmmv)
                v.tensor_copy(gbc[:, 0:512], ps_all[:, 0, :])
                v.tensor_copy(bbc[:, 0:512], ps_all[:, 1, :])
                v.tensor_copy(gbc[:, 512:1024], ps_all[:, 2, :])
                sv.bump(v.tensor_copy(bbc[:, 512:1024], ps_all[:, 3, :]))
            gb_done = sv.n

            load_T(wouta_h, D, D, simple_evacT(woutT))

            @VE
            def _(v):
                v.memset(vb[:, :, 64:66], 0.0)
                sv.bump(v.memset(vb[:, :, 64:65], 1.0))
            vb_ones = sv.n
            barrier()

            _scopes["A1"] = nc.enter_named_scope("A1", False)[0]
            zdone = {}
            for i in range(NT):
                buf = i % 2

                @SY
                def _(s):
                    if i >= 2:
                        ss.wait(s, zdone[i - 2])
                    s.dma_start(xtile[:, buf, :],
                                x_h[i * 128:(i + 1) * 128, :]).then_inc(sx.h, 16)
                sx.n += 16
                xld = sx.n
                szn, _s, _v = ln_chain(
                    xtile[:, buf, :], None, z_out=ztile,
                    store_affine=(rsc[:, i:i + 1], rbi[:, i:i + 1]),
                    wait_before=[(sx, xld)])
                zdone[i] = szn
                szn2, sdv, vdv = ln_chain(ztile, ntile,
                                          gb=(gbc, bbc, zn0, z2),
                                          wait_before=[(ss, szn)])
                tile_to_xT(i, ntile, sv, vdv)
            barrier()
            if "xn" in tap_set:
                @GE
                def _(g):
                    g.dma_start(tap_hbm["xn"], xT[:]).then_inc(sw.h, 16)
                sw.n += 16
                barrier()
            nc.leave_named_scope("A1", _scopes["A1"], False)
            _scopes["A2"] = nc.enter_named_scope("A2", False)[0]

            for quarter in range(4):
                for band in range(3):
                    row0 = band * 1024 + quarter * 256

                    def wq_evac(j, tpv, band=band):
                        @VE
                        def _(v):
                            sm.wait(v, tpv)
                            sv.bump(v.tensor_copy(
                                wqkvT[:, :, band * 256 + j * 128:
                                      band * 256 + (j + 1) * 128],
                                ps_tp_bf.rearrange("p (c t) -> p c t", c=8)))
                    load_T(wqkv_h[row0:row0 + 256, :], 256, D, wq_evac)
                wq_done = sv.n
                for hh in range(4):
                    h = quarter * 4 + hh
                    a2_head(h, hh, wqkvT, bqk_s, bv_s, qT, kT, vT, vb, pT,
                            recb, recf, cstg, wq_done, vb_ones)
            barrier()
            nc.leave_named_scope("A2", _scopes["A2"], False)
            _scopes["A3"] = nc.enter_named_scope("A3", False)[0]

            xguard = {}
            cld = {}
            for i in range(NT):
                buf = i % 2
                cbuf = i % 2
                prev = a3_ctile_used.get(i - 2, 0)

                @SY
                def _(s):
                    sm.wait(s, prev)
                    s.dma_start(ctile[:, cbuf, :, :],
                                ctx_h[:, :, i * 128:(i + 1) * 128]).then_inc(sx.h, 16)
                sx.n += 16
                cldv = sx.n

                @SY
                def _(s):
                    g = xguard.get(i - 2)
                    if g is not None:
                        g[0].wait(s, g[1])
                    s.dma_start(xtile[:, buf, :],
                                x_h[i * 128:(i + 1) * 128, :]).then_inc(sx.h, 16)
                sx.n += 16
                xld = sx.n

                evv = a3_psfree[0]

                @TE
                def _(t):
                    sv.wait(t, evv)
                    sx.wait(t, cldv)
                    for ob in range(2):
                        for cc in range(8):
                            t.matmul(ps_all[:, 4 + ob, :],
                                     ctile[:, cbuf, cc, :],
                                     woutT[:, cc, ob * 512:(ob + 1) * 512],
                                     start=(cc == 0), stop=False)
                        ins = t.matmul(ps_all[:, 4 + ob, :], ones_r[:],
                                       bouta_s[:, ob * 512:(ob + 1) * 512],
                                       start=False, stop=True)
                        if ob == 1:
                            sm.bump(ins)
                mmv = sm.n
                a3_ctile_used[i] = mmv
                yv = finish_tile(i, mmv, xld, buf, ya_h, ztile, True, "y1",
                                 xguard)
                a3_psfree[0] = yv
            barrier()
            nc.leave_named_scope("A3", _scopes["A3"], False)

    a3_ctile_used = {}
    a3_psfree = [0]

    def a2_head(h, hh, wqkvT, bqk_s, bv_s, qT, kT, vT, vb, pT, recb, recf,
                cstg, wq_done, vb_ones):
        w3 = wqkvT.rearrange("p c (b o) -> p c b o", b=3)
        for tbk in range(8):
            bk = (tbk % 2) * 2
            evv = ss.n

            @TE
            def _(t):
                sv.wait(t, wq_done)
                ss.wait(t, evv)
                for cc in range(8):
                    t.matmul(ps_all[0:64, bk, :],
                             w3[:, cc, 0, hh * 64:(hh + 1) * 64],
                             xT[:, cc, tbk * 512:(tbk + 1) * 512],
                             start=(cc == 0), stop=(cc == 7))
                for cc in range(8):
                    t.matmul(ps_all[64:128, bk, :],
                             w3[:, cc, 1, hh * 64:(hh + 1) * 64],
                             xT[:, cc, tbk * 512:(tbk + 1) * 512],
                             start=(cc == 0), stop=(cc == 7),
                             tile_position=(0, 64))
                for cc in range(8):
                    ins = t.matmul(ps_all[0:64, bk + 1, :],
                                   w3[:, cc, 2, hh * 64:(hh + 1) * 64],
                                   xT[:, cc, tbk * 512:(tbk + 1) * 512],
                                   start=(cc == 0), stop=(cc == 7))
                    if cc == 7:
                        sm.bump(ins)
            mmv = sm.n

            @SE
            def _(s):
                sm.wait(s, mmv)
                sl = slice(tbk * 512, (tbk + 1) * 512)
                s.activation(qT[:, sl], ps_all[0:64, bk, :], AF.Identity,
                             bias=bqk_s[0:64, h:h + 1])
                s.activation(kT[:, sl], ps_all[64:128, bk, :], AF.Identity,
                             bias=bqk_s[64:128, h:h + 1])
                ss.bump(s.activation(vT[:, sl], ps_all[0:64, bk + 1, :],
                                     AF.Identity, bias=bv_s[:, h:h + 1]))
        vT_done = ss.n
        if h == 0 and "q0" in tap_hbm:
            @GE
            def _(g):
                ss.wait(g, vT_done)
                g.dma_start(tap_hbm["q0"], qT[:]).then_inc(sw.h, 16)
                g.dma_start(tap_hbm["k0"], kT[:]).then_inc(sw.h, 16)
                g.dma_start(tap_hbm["v0"], vT[:]).then_inc(sw.h, 16)
            sw.n += 48
            barrier()

        for grp in range(4):
            @TE
            def _(t):
                ss.wait(t, vT_done)
                ss.wait(t, tp_ss[0])
                sv.wait(t, tp_sv[0])
                sv.wait(t, vb_ones)
                for k in range(8):
                    kc = grp * 8 + k
                    ins = t.transpose(ps_tp_bf[:, k * 64:(k + 1) * 64],
                                      vT[:, kc * 128:(kc + 1) * 128],
                                      eye[0:64, 0:64])
                    if k == 7:
                        sm.bump(ins)
            tpv = sm.n

            @VE
            def _(v):
                sm.wait(v, tpv)
                sv.bump(v.tensor_copy(
                    vb[:, grp * 8:(grp + 1) * 8, 0:64],
                    ps_tp_bf[:, 0:512].rearrange("p (c t) -> p c t", c=8)))
            tp_sv[0] = sv.n
        vb_done = sv.n

        groups = [list(range(g * 3, min(32, g * 3 + 3))) for g in range(11)]
        ng = len(groups)
        for qb in range(8):
            qsl = slice(qb * 512, (qb + 1) * 512)
            edone = {}
            cdone = {}
            sdone = {}

            def emit_s(g):
                chunks = groups[g]
                bs = (g % 2) * 3

                @TE
                def _(t):
                    if g >= 2:
                        ss.wait(t, edone[g - 2])
                    sv.wait(t, vb_done)
                    for idx, kc in enumerate(chunks):
                        ins = t.matmul(ps_all[:, bs + idx, :],
                                       kT[:, kc * 128:(kc + 1) * 128],
                                       qT[:, qsl], start=True, stop=True)
                        if idx == len(chunks) - 1:
                            sm.bump(ins)
                sdone[g] = sm.n

            def emit_e(g):
                chunks = groups[g]
                bs = (g % 2) * 3

                @SE
                def _(s):
                    sm.wait(s, sdone[g])
                    if g >= 2:
                        sm.wait(s, cdone[g - 2])
                    ss.bump(s.activation(pT[:, bs:bs + len(chunks), :],
                                         ps_all[:, bs:bs + len(chunks), :],
                                         AF.Exp, scale=0.125))
                edone[g] = ss.n

            def emit_c(g):
                chunks = groups[g]
                bs = (g % 2) * 3

                @TE
                def _(t):
                    ss.wait(t, edone[g])
                    if g == 0:
                        sv.wait(t, a2_ctx_free[0])
                    for idx, kc in enumerate(chunks):
                        ins = t.matmul(ps_c[0:65, :],
                                       vb[:, kc, 0:65],
                                       pT[:, bs + idx, :],
                                       start=(kc == 0), stop=(kc == 31))
                        if idx == len(chunks) - 1:
                            sm.bump(ins)
                cdone[g] = sm.n

            emit_s(0)
            emit_e(0)
            for g in range(1, ng):
                emit_s(g)
                emit_e(g)
                emit_c(g - 1)
            emit_c(ng - 1)
            ctx_done = sm.n

            @VE
            def _(v):
                sm.wait(v, ctx_done)
                with nc.allow_low_precision(reason="bf16 softmax recip bcast"):
                    sv.bump(v.reciprocal(recb[:], ps_c[64:65, :]))
            rcv = sv.n

            @TE
            def _(t):
                sv.wait(t, rcv)
                ss.wait(t, tp_ss[0])
                sv.wait(t, tp_sv[0])
                sm.bump(t.matmul(ps_tp[0:64, :], ones_r[0:1, 0:64], recb[:],
                                 start=True, stop=True))
            bcv = sm.n

            @SE
            def _(s):
                sm.wait(s, bcv)
                sv.wait(s, a2_ctx_free[0])
                ss.bump(s.copy(recf[:], ps_tp[0:64, :]))
            a2_recf_free[0] = ss.n
            tp_ss[0] = ss.n
            rfv = ss.n

            @VE
            def _(v):
                ss.wait(v, rfv)
                so.wait(v, a2_cstg_free[0])
                sv.bump(v.tensor_tensor(cstg[:], ps_c[0:64, :], recf[:],
                                        OP.mult))
            csv = sv.n
            a2_ctx_free[0] = csv

            @SY
            def _(s):
                sv.wait(s, csv)
                s.dma_start(ctx_h[(h % 2) * 64:(h % 2) * 64 + 64, h // 2, qsl],
                            cstg[:]).then_inc(so.h, 16)
            so.n += 16
            a2_cstg_free[0] = so.n

    a2_ctx_free = [0]
    a2_cstg_free = [0]
    a2_recf_free = [0]

    def phase_b(resid_src, dst, next_ln, tap, tag):
        with ExitStack() as phx:
            pec = phx.enter_context
            winT = pec(st("b" + tag + "_winT", [128, 8, DI], BF16))
            woutfT = pec(st("b" + tag + "_woutfT", [128, 16, D], BF16))
            sxv = pec(st("b" + tag + "_sx", [128, PKH, NT], F32))
            ixv = pec(st("b" + tag + "_ix", [128, PKH, NT], U32))
            idxv = pec(st("b" + tag + "_idx", [128, PKH, NT], U32))
            gatev = pec(st("b" + tag + "_gate", [128, PKH, NT], F32))
            g_ina = pec(st("b" + tag + "_gina", [128, D], F32))
            g_inb = pec(st("b" + tag + "_ginb", [128, DI], F32))
            g_outa = pec(st("b" + tag + "_gouta", [128, DI], F32))
            g_outb = pec(st("b" + tag + "_goutb", [128, D], F32))
            inb_bf = pec(st("b" + tag + "_inb_bf", [128, 1, DI], BF16))
            outb_bf = pec(st("b" + tag + "_outb_bf", [128, 1, D], BF16))
            diag = pec(st("b" + tag + "_diag", [128, 2, 128], BF16))
            diag2 = pec(st("b" + tag + "_diag2", [128, 2, 128], BF16))
            xftile = pec(st("b" + tag + "_xf", [128, D], BF16))
            linb = pec(st("b" + tag + "_lin", [128, PKH], F32))
            loutb = pec(st("b" + tag + "_lout", [128, PKH], F32))
            b1_scope = ExitStack()
            b1c = b1_scope.enter_context
            wqT = b1c(st("b" + tag + "_wqT", [128, 8, 512], BF16))
            qTb = b1c(st("b" + tag + "_qT", [128, 4, 512], F32))
            keysT = b1c(st("b" + tag + "_keysT", [128, 4, 128], F32))
            simt = b1c(st("b" + tag + "_simt", [128, 128], F32))
            mv8 = b1c(st("b" + tag + "_mv8", [128, 8], F32))
            mi8 = b1c(st("b" + tag + "_mi8", [128, 8], U32))

            load_T(win_h, DI, D, simple_evacT(winT))

            def woutf_evac(base):
                def f(j, tpv):
                    @VE
                    def _(v):
                        sm.wait(v, tpv)
                        sv.bump(v.tensor_copy(
                            woutfT[:, base:base + 8, j * 128:(j + 1) * 128],
                            ps_tp_bf.rearrange("p (c t) -> p c t", c=8)))
                return f
            load_T(wout_h[:, 0:1024], D, 1024, woutf_evac(0))
            load_T(wout_h[:, 1024:2048], D, 1024, woutf_evac(8))
            barrier()

            _scopes["B1q" + tag] = nc.enter_named_scope("B1q" + tag, False)[0]
            bank_free = [ss.n] * 4
            sim_free = sv.n
            key_free = sm.n
            for ph_i in range(16):
                p_, h_ = ph_i // 8, ph_i % 8
                row0 = (p_ * 8 + h_) * 512
                load_T(wq_h[row0:row0 + 512, :], 512, D, simple_evacT(wqT))
                wqv = sv.n

                kstage = qTb[:, 0, :]

                @SY
                def _(s):
                    sm.wait(s, key_free)
                    ss.wait(s, b1_qtb_writes[0])
                    s.dma_start(kstage,
                                keys_h[h_, :, p_, :]).then_inc(sx.h, 16)
                sx.n += 16
                kldv = sx.n

                @TE
                def _(t):
                    sx.wait(t, kldv)
                    sv.wait(t, tp_sv[0])
                    ss.wait(t, tp_ss[0])
                    for m in range(4):
                        ins = t.transpose(ps_tp[:, m * 128:(m + 1) * 128],
                                          kstage[:, m * 128:(m + 1) * 128],
                                          eyef[:])
                        if m == 3:
                            sm.bump(ins)
                ktp = sm.n

                @VE
                def _(v):
                    sm.wait(v, ktp)
                    sv.bump(v.tensor_copy(
                        keysT[:, :, :],
                        ps_tp[:, 0:512].rearrange("p (c t) -> p c t", c=4)))
                tp_sv[0] = sv.n
                kv = sv.n

                for tbk in range(8):
                    simv_prev = sm.n
                    for m in range(4):
                        bk = m

                        @TE
                        def _(t):
                            sv.wait(t, wqv)
                            ss.wait(t, bank_free[bk])
                            for cc in range(8):
                                ins = t.matmul(
                                    ps_all[:, bk, :],
                                    wqT[:, cc, m * 128:(m + 1) * 128],
                                    xT[:, cc, tbk * 512:(tbk + 1) * 512],
                                    start=(cc == 0), stop=(cc == 7))
                                if cc == 7:
                                    sm.bump(ins)
                        mmv = sm.n

                        @SE
                        def _(s):
                            sm.wait(s, mmv)
                            sm.wait(s, qtb_free[0])
                            sm.wait(s, ktp)
                            ss.bump(s.copy(qTb[:, m, :], ps_all[:, bk, :]))
                        bank_free[bk] = ss.n
                        b1_qtb_writes[0] = ss.n
                    qdv = ss.n

                    for q in range(4):
                        i = tbk * 4 + q

                        @TE
                        def _(t):
                            ss.wait(t, qdv)
                            sv.wait(t, sim_free)
                            sv.wait(t, kv)
                            for m in range(4):
                                ins = t.matmul(ps_c[:, 0:128],
                                               qTb[:, m, q * 128:(q + 1) * 128],
                                               keysT[:, m, :],
                                               start=(m == 0), stop=(m == 3))
                                if m == 3:
                                    sm.bump(ins)
                        smv = sm.n
                        if q == 3:
                            qtb_free[0] = smv
                        key_free = smv

                        @VE
                        def _(v):
                            sm.wait(v, smv)
                            sv.bump(v.tensor_copy(simt[:], ps_c[:, 0:128]))
                        sim_free = sv.n
                        cpv = sv.n

                        @VE
                        def _(v):
                            sv.wait(v, cpv)
                            sv.bump(v.max(mv8[:], simt[:]))
                        mxv = sv.n

                        @VE
                        def _(v):
                            sv.wait(v, mxv)
                            sv.bump(v.max_index(mi8[:], mv8[:], simt[:]))
                        miv = sv.n

                        if p_ == 0:
                            @VE
                            def _(v):
                                sv.wait(v, miv)
                                v.tensor_copy(sxv[:, h_, i:i + 1], mv8[:, 0:1])
                                sv.bump(v.tensor_copy(ixv[:, h_, i:i + 1],
                                                      mi8[:, 0:1]))
                        else:
                            @VE
                            def _(v):
                                sv.wait(v, miv)
                                v.scalar_tensor_tensor(
                                    idxv[:, h_, i:i + 1], ixv[:, h_, i:i + 1],
                                    128.0, mi8[:, 0:1], OP.mult, OP.add)
                                sv.bump(v.tensor_scalar(
                                    gatev[:, h_, i:i + 1],
                                    sxv[:, h_, i:i + 1], mv8[:, 0:1], 0.0,
                                    OP.add, OP.max))
            nc.leave_named_scope("B1q" + tag, _scopes["B1q" + tag], False)
            barrier()
            if "idx" in tap_set and tag == "1":
                @GE
                def _(g):
                    g.dma_start(tap_hbm["idx"], idxv[:]).then_inc(sw.h, 16)
                    g.dma_start(tap_hbm["gate"], gatev[:]).then_inc(sw.h, 16)
                sw.n += 32
                barrier()
            b1_scope.close()
            _scopes["B3_" + tag] = nc.enter_named_scope("B3_" + tag, False)[0]
            prod = pec(st("b" + tag + "_prod", [128, DI], BF16))
            hidbf = pec(st("b" + tag + "_hidbf", [128, DI], BF16))
            hT = pec(st("b" + tag + "_hT", [128, 16, 128], BF16))

            scn0 = scn.n
            sco0 = sco.n
            sg0 = sg.n
            sg_in_base = {}
            sg_out_base = {}
            sgc = [sg.n]
            GQD = 4

            def emit_in(i):
                sg_in_base[i] = sgc[0]

                @GE
                def _(g):
                    for h_ in range(PKH):
                        ghc = i * PKH + h_
                        if ghc >= 1:
                            scn.wait(g, scn0 + 2 * ghc)
                        sg.wait(g, sg.n - (GQD - 1) * 16)
                        g.indirect_dma_start(
                            g_ina[:, :], None, tbl["ina"],
                            bass.IndirectOffsetOnAxis(idxv[:, h_, i:i + 1], 0)
                        ).then_inc(sg.h, 16)
                        sg.n += 16
                        sg.wait(g, sg.n - (GQD - 1) * 16)
                        g.indirect_dma_start(
                            g_inb[:, :], None, tbl["inb"],
                            bass.IndirectOffsetOnAxis(idxv[:, h_, i:i + 1], 0)
                        ).then_inc(sg.h, 16)
                        sg.n += 16
                sgc[0] = sg.n

            def emit_out(i):
                sg_out_base[i] = sgc[0]

                @GE
                def _(g):
                    for h_ in range(PKH):
                        ghc = i * PKH + h_
                        if ghc >= 1:
                            sco.wait(g, sco0 + 2 * ghc)
                        sg.wait(g, sg.n - (GQD - 1) * 16)
                        g.indirect_dma_start(
                            g_outa[:, :], None, tbl["outa"],
                            bass.IndirectOffsetOnAxis(idxv[:, h_, i:i + 1], 0)
                        ).then_inc(sg.h, 16)
                        sg.n += 16
                        sg.wait(g, sg.n - (GQD - 1) * 16)
                        g.indirect_dma_start(
                            g_outb[:, :], None, tbl["outb"],
                            bass.IndirectOffsetOnAxis(idxv[:, h_, i:i + 1], 0)
                        ).then_inc(sg.h, 16)
                        sg.n += 16
                sgc[0] = sg.n

            for i in range(NT):
                emit_in(i)
                emit_out(i)

            def gwait(ghc, k):
                i, h_ = divmod(ghc, PKH)
                if k < 2:
                    return sg_in_base[i] + (2 * h_ + k + 1) * 16
                return sg_out_base[i] + (2 * h_ + (k - 2) + 1) * 16

            hid_free = ss.n
            out_free = sv.n
            xguard = {}
            b3_xf_rd = [0]
            last_dmm = {0: 0, 1: 0}
            last_dmm2 = {0: 0, 1: 0}
            last_ibmm = {0: 0, 1: 0}
            last_obmm = {0: 0, 1: 0}
            for i in range(NT):
                buf = i % 2

                @TE
                def _(t):
                    sv.wait(t, tp_sv[0])
                    ss.wait(t, tp_ss[0])
                    sv.wait(t, b3_xf_rd[0])
                    for cc in range(8):
                        ins = t.transpose(ps_tp_bf[:, cc * 128:(cc + 1) * 128],
                                          xT[:, cc, i * 128:(i + 1) * 128],
                                          eye[:])
                        if cc == 7:
                            sm.bump(ins)
                tpv = sm.n

                @VE
                def _(v):
                    sm.wait(v, tpv)
                    sv.bump(v.tensor_copy(xftile[:], ps_tp_bf[:, 0:1024]))
                tp_sv[0] = sv.n
                xfv = sv.n

                @TE
                def _(t):
                    ss.wait(t, hid_free)
                    for nb in range(4):
                        for cc in range(8):
                            ins = t.matmul(ps_all[:, nb, :],
                                           xT[:, cc, i * 128:(i + 1) * 128],
                                           winT[:, cc, nb * 512:(nb + 1) * 512],
                                           start=(cc == 0), stop=False)
                        if nb == 3:
                            sm.bump(ins)
                winv = sm.n

                for h_ in range(PKH):
                    ghc = i * PKH + h_
                    par = h_ % 2

                    @VE
                    def _(v):
                        sg.wait(v, gwait(ghc, 0))
                        sv.wait(v, xfv)
                        sv.bump(v.scalar_tensor_tensor(
                            prod[:, 0:1024], g_ina[:, :], 1.0, xftile[:],
                            OP.mult, OP.mult,
                            accum_out=linb[:, h_:h_ + 1]))
                        scn.bump(v.nop())
                    linv = sv.n
                    if h_ == PKH - 1:
                        b3_xf_rd[0] = sv.n

                    @VE
                    def _(v):
                        sv.wait(v, linv)
                        sm.wait(v, last_dmm[par])
                        sv.bump(v.tensor_scalar(diag[:, par, :], eye[:],
                                                linb[:, h_:h_ + 1],
                                                gatev[:, h_, i:i + 1],
                                                OP.mult, OP.mult))
                    dgv = sv.n

                    @SE
                    def _(s):
                        sg.wait(s, gwait(ghc, 1))
                        sm.wait(s, last_ibmm[0])
                        ss.bump(s.copy(inb_bf[:, 0, :], g_inb[:, :]))
                        scn.bump(s.nop())
                    cbv = ss.n

                    @TE
                    def _(t):
                        sv.wait(t, dgv)
                        ss.wait(t, cbv)
                        sm.wait(t, winv)
                        for nb in range(4):
                            ins = t.matmul(
                                ps_all[:, nb, :], diag[:, par, :],
                                inb_bf[:, 0, nb * 512:(nb + 1) * 512],
                                start=False, stop=(h_ == PKH - 1))
                            if nb == 3:
                                sm.bump(ins)
                    last_dmm[par] = sm.n
                    last_ibmm[0] = sm.n
                hidv = sm.n

                @SE
                def _(s):
                    sm.wait(s, hidv)
                    ss.bump(s.activation(
                        hidbf[:],
                        ps_all[:, 0:4, :].rearrange("p a b -> p (a b)"),
                        AF.Gelu))
                hid_free = ss.n
                gfv = ss.n

                for grp in range(2):
                    @TE
                    def _(t):
                        ss.wait(t, gfv)
                        sv.wait(t, tp_sv[0])
                        ss.wait(t, tp_ss[0])
                        for k in range(8):
                            cc = grp * 8 + k
                            ins = t.transpose(
                                ps_tp_bf[:, k * 128:(k + 1) * 128],
                                hidbf[:, cc * 128:(cc + 1) * 128], eye[:])
                            if k == 7:
                                sm.bump(ins)
                    tpv = sm.n

                    @VE
                    def _(v):
                        sm.wait(v, tpv)
                        sv.bump(v.tensor_copy(
                            hT[:, grp * 8:(grp + 1) * 8, :],
                            ps_tp_bf.rearrange("p (c t) -> p c t", c=8)))
                    tp_sv[0] = sv.n
                hTv = sv.n

                @TE
                def _(t):
                    sv.wait(t, hTv)
                    sv.wait(t, out_free)
                    for ob in range(2):
                        for cc in range(16):
                            ins = t.matmul(
                                ps_all[:, 4 + ob, :], hT[:, cc, :],
                                woutfT[:, cc, ob * 512:(ob + 1) * 512],
                                start=(cc == 0), stop=False)
                        if ob == 1:
                            sm.bump(ins)
                woutv = sm.n

                for h_ in range(PKH):
                    ghc = i * PKH + h_
                    par = h_ % 2

                    @VE
                    def _(v):
                        sg.wait(v, gwait(ghc, 2))
                        ss.wait(v, gfv)
                        sv.bump(v.scalar_tensor_tensor(
                            prod[:], g_outa[:, :], 1.0, hidbf[:],
                            OP.mult, OP.mult,
                            accum_out=loutb[:, h_:h_ + 1]))
                        sco.bump(v.nop())
                    lov = sv.n

                    @VE
                    def _(v):
                        sv.wait(v, lov)
                        sm.wait(v, last_dmm2[par])
                        sv.bump(v.tensor_scalar(diag2[:, par, :], eye[:],
                                                loutb[:, h_:h_ + 1],
                                                gatev[:, h_, i:i + 1],
                                                OP.mult, OP.mult))
                    dg2 = sv.n

                    @SE
                    def _(s):
                        sg.wait(s, gwait(ghc, 3))
                        sm.wait(s, last_obmm[0])
                        ss.bump(s.copy(outb_bf[:, 0, :], g_outb[:, :]))
                        sco.bump(s.nop())
                    cb2 = ss.n

                    @TE
                    def _(t):
                        sv.wait(t, dg2)
                        ss.wait(t, cb2)
                        sm.wait(t, woutv)
                        for ob in range(2):
                            ins = t.matmul(
                                ps_all[:, 4 + ob, :], diag2[:, par, :],
                                outb_bf[:, 0, ob * 512:(ob + 1) * 512],
                                start=False, stop=(h_ == PKH - 1))
                            if ob == 1:
                                sm.bump(ins)
                    last_dmm2[par] = sm.n
                    last_obmm[0] = sm.n
                outv = sm.n

                @SY
                def _(s):
                    g = xguard.get(i - 2)
                    if g is not None:
                        g[0].wait(s, g[1])
                    s.dma_start(xtile[:, buf, :],
                                resid_src[i * 128:(i + 1) * 128, :]).then_inc(sx.h, 16)
                sx.n += 16
                xld = sx.n
                yv = finish_tile(i, outv, xld, buf, dst, None, next_ln, tap,
                                 xguard)
                out_free = yv
            barrier()
            nc.leave_named_scope("B3_" + tag, _scopes["B3_" + tag], False)

    qtb_free = [0]
    b1_qtb_writes = [0]

    def phase_c():
        with ExitStack() as phx:
            pec = phx.enter_context
            outb_s = pec(st("c_outb", [1, D], BF16))
            tempc = pec(st("c_tempc", [128, 8], F32))
            wT = pec(st("c_wT", [128, 8, S], BF16))
            norm2p = pec(st("c_norm2p", [128, 8, 8], F32))
            norm2 = pec(st("c_norm2", [128, 8], F32))
            gvec = pec(st("c_g", [128, 8], F32))
            gmat = pec(st("c_gmat", [128, 8, 16], BF16))
            sqsc = pec(st("c_sqsc", [128, 2, 512], BF16))
            sts = pec(st("c_sts", [16, 512], F32))
            sti = pec(st("c_sti", [128, 16], F32))
            e16 = pec(st("c_e16", [128, 16], F32))
            esc = pec(st("c_esc", [128, 16], F32))
            pibf = pec(st("c_pibf", [128, NT, 16], BF16))
            piT = pec(st("c_piT", [16, S], BF16))
            zrow = pec(st("c_zrow", [16, 1], F32))
            zrowT = pec(st("c_zrowT", [1, 16], F32))
            zc = pec(st("c_zc", [128, 8], F32))
            dotsc = pec(st("c_dotsc", [128, 8], F32))
            attnc = pec(st("c_attnc", [128, 8], F32))
            wtile = pec(st("c_wtile", [128, D], BF16))
            w2tile = pec(st("c_w2tile", [128, D], BF16))
            tmp1 = pec(st("c_tmp1", [128, 512], F32))
            opT = pec(st("c_opT", [128, 8, 512], BF16))

            @SY
            def _(s):
                s.dma_start(tempc[:], tempc_h).then_inc(sx.h, 16)
            sx.n += 16
            tcv = sx.n

            @GE
            def _(g):
                g.dma_start(outb_s[:], outb_h).then_inc(sw.h, 16)
            sw.n += 16

            with ExitStack() as ph2:
                qkvwT = ph2.enter_context(st("c_qkvwT", [128, 8, D], BF16))
                load_T(qkvw_h, D, D, simple_evacT(qkvwT))
                barrier()
                bank_free = [ss.n] * 4
                for tbk in range(8):
                    for oc in range(8):
                        bk = (tbk * 8 + oc) % 4
                        evv = bank_free[bk]

                        @TE
                        def _(t):
                            ss.wait(t, evv)
                            for cc in range(8):
                                ins = t.matmul(
                                    ps_all[:, bk, :],
                                    qkvwT[:, cc, oc * 128:(oc + 1) * 128],
                                    xT[:, cc, tbk * 512:(tbk + 1) * 512],
                                    start=(cc == 0), stop=(cc == 7))
                                if cc == 7:
                                    sm.bump(ins)
                        mmv = sm.n

                        @SE
                        def _(s):
                            sm.wait(s, mmv)
                            ss.bump(s.copy(
                                wT[:, oc, tbk * 512:(tbk + 1) * 512],
                                ps_all[:, bk, :]))
                        bank_free[bk] = ss.n
                barrier()
            wTv = ss.n

            outwT = pec(st("c_outwT", [128, 8, D], BF16))
            load_T(outw_h, D, D, simple_evacT(outwT))

            for cc in range(8):
                for blk in range(8):
                    par = blk % 2

                    @SE
                    def _(s):
                        ss.wait(s, wTv)
                        sv.wait(s, c_sq_free[0])
                        ss.bump(s.activation(
                            sqsc[:, par, :],
                            wT[:, cc, blk * 512:(blk + 1) * 512], AF.Square,
                            accum_out=norm2p[:, cc, blk:blk + 1]))
            nv = ss.n

            @VE
            def _(v):
                ss.wait(v, nv)
                sv.bump(v.tensor_reduce(norm2[:], norm2p[:],
                                        mybir.AxisListType.X, OP.add))
            n2r = sv.n

            @VE
            def _(v):
                sv.wait(v, n2r)
                sv.bump(v.tensor_scalar(norm2[:], norm2[:], 1e-24, None,
                                        OP.max))
            n2v = sv.n

            @VE
            def _(v):
                sv.wait(v, n2v)
                sv.bump(v.reciprocal(gvec[:], norm2[:]))
            rgv = sv.n

            @VE
            def _(v):
                sv.wait(v, rgv)
                sx.wait(v, tcv)
                sv.bump(v.tensor_tensor(gvec[:], gvec[:], tempc[:], OP.mult))
            gv = sv.n

            @VE
            def _(v):
                sv.wait(v, gv)
                sv.bump(v.memset(gmat[:], 0.0))
            gm0 = sv.n

            @VE
            def _(v):
                sv.wait(v, gm0)
                for j in range(8):
                    v.tensor_copy(gmat[0:64, j, 2 * j:2 * j + 1],
                                  gvec[0:64, j:j + 1])
                    ins = v.tensor_copy(gmat[64:128, j, 2 * j + 1:2 * j + 2],
                                        gvec[64:128, j:j + 1])
                sv.bump(ins)
            gmv = sv.n

            pst_free = sv.n
            for tbk in range(8):
                for cc in range(8):
                    par = cc % 2

                    k_glob = tbk * 8 + cc

                    @SE
                    def _(s):
                        sv.wait(s, gmv)
                        if k_glob >= 2:
                            sm.wait(s, sq_hist[k_glob - 2])
                        ss.bump(s.activation(
                            sqsc[:, par, :],
                            wT[:, cc, tbk * 512:(tbk + 1) * 512], AF.Square))
                    sqv = ss.n

                    @TE
                    def _(t):
                        ss.wait(t, sqv)
                        if tbk > 0 and cc == 0:
                            ss.wait(t, c_sts_done[0])
                        ins = t.matmul(ps_c[0:16, :], gmat[:, cc, :],
                                       sqsc[:, par, :],
                                       start=(cc == 0), stop=(cc == 7))
                        sm.bump(ins)
                    sq_hist.append(sm.n)
                stv = sm.n

                @SE
                def _(s):
                    sm.wait(s, stv)
                    ss.bump(s.copy(sts[:], ps_c[0:16, :]))
                c_sts_done[0] = ss.n
                stsv = ss.n

                for q in range(4):
                    i = tbk * 4 + q

                    @TE
                    def _(t):
                        ss.wait(t, stsv)
                        sv.wait(t, tp_sv[0])
                        ss.wait(t, tp_ss[0])
                        sm.bump(t.transpose(ps_tp[:, 0:16],
                                            sts[:, q * 128:(q + 1) * 128],
                                            eye16f[:]))
                    tpv = sm.n

                    @VE
                    def _(v):
                        sm.wait(v, tpv)
                        sv.bump(v.tensor_copy(sti[:], ps_tp[:, 0:16]))
                    tp_sv[0] = sv.n
                    stiv = sv.n

                    @SE
                    def _(s):
                        sv.wait(s, stiv)
                        ss.bump(s.activation(e16[:], sti[:], AF.Exp))
                    ev = ss.n

                    @VE
                    def _(v):
                        ss.wait(v, ev)
                        sv.bump(v.tensor_scalar(esc[:], e16[:], 1.0, None,
                                                OP.mult, OP.add,
                                                accum_out=tb[:]))
                    sev = sv.n

                    @VE
                    def _(v):
                        sv.wait(v, sev)
                        sv.bump(v.reciprocal(tc[:], tb[:]))
                    rcv = sv.n

                    @VE
                    def _(v):
                        sv.wait(v, rcv)
                        sv.bump(v.tensor_scalar(pibf[:, i, :], e16[:], tc[:],
                                                None, OP.mult))
                    piv = sv.n

                    @TE
                    def _(t):
                        sv.wait(t, piv)
                        sv.wait(t, tp_sv[0])
                        ss.wait(t, tp_ss[0])
                        sm.bump(t.transpose(ps_tp.bitcast(BF16)[0:16, 0:128],
                                            pibf[:, i, :], eye[:]))
                    ptv = sm.n

                    @VE
                    def _(v):
                        sm.wait(v, ptv)
                        sv.bump(v.tensor_copy(piT[:, i * 128:(i + 1) * 128],
                                              ps_tp.bitcast(BF16)[0:16, 0:128]))
                    tp_sv[0] = sv.n
                    pst_free = sv.n

            @TE
            def _(t):
                sv.wait(t, pst_free)
                ins = None
                for i in range(NT):
                    ins = t.matmul(ps_c[0:16, 0:1], pibf[:, i, :], ones_c[:],
                                   start=(i == 0), stop=(i == NT - 1))
                sm.bump(ins)
            zmv = sm.n

            @SE
            def _(s):
                sm.wait(s, zmv)
                ss.bump(s.copy(zrow[:], ps_c[0:16, 0:1]))
            zev = ss.n

            @VE
            def _(v):
                ss.wait(v, zev)
                sv.bump(v.tensor_scalar(zrow[:], zrow[:], 1e-8, None, OP.add))
            zadd = sv.n

            @TE
            def _(t):
                sv.wait(t, zadd)
                sv.wait(t, tp_sv[0])
                ss.wait(t, tp_ss[0])
                sm.bump(t.transpose(ps_tp[0:1, 0:16], zrow[:], eye16f[:]))
            ztv = sm.n

            @SE
            def _(s):
                sm.wait(s, ztv)
                ss.bump(s.copy(zrowT[:], ps_tp[0:1, 0:16]))
            tp_ss[0] = ss.n
            zrv = ss.n

            @TE
            def _(t):
                ss.wait(t, zrv)
                sv.wait(t, tp_sv[0])
                sm.bump(t.matmul(ps_tp[:, 0:16], ones_rf[:], zrowT[:],
                                 start=True, stop=True))
            zbv = sm.n

            @VE
            def _(v):
                sm.wait(v, zbv)
                for j in range(8):
                    v.tensor_copy(zc[0:64, j:j + 1],
                                  ps_tp[0:64, 2 * j:2 * j + 1])
                    ins = v.tensor_copy(zc[64:128, j:j + 1],
                                        ps_tp[64:128, 2 * j + 1:2 * j + 2])
                sv.bump(ins)
            tp_sv[0] = sv.n
            zcv = sv.n

            @VE
            def _(v):
                sv.wait(v, zcv)
                sv.bump(v.reciprocal(zc[:], zc[:]))
            zrcv = sv.n

            c_wtile_rd = [0]
            for i in range(NT):
                @TE
                def _(t):
                    sv.wait(t, tp_sv[0])
                    ss.wait(t, tp_ss[0])
                    ss.wait(t, c_wtile_rd[0])
                    if i == 0:
                        sv.wait(t, zrcv)
                    for cc in range(8):
                        ins = t.transpose(ps_tp_bf[:, cc * 128:(cc + 1) * 128],
                                          wT[:, cc, i * 128:(i + 1) * 128],
                                          eye[:])
                        if cc == 7:
                            sm.bump(ins)
                tpv = sm.n

                @VE
                def _(v):
                    sm.wait(v, tpv)
                    sv.bump(v.tensor_copy(wtile[:], ps_tp_bf[:, 0:1024]))
                tp_sv[0] = sv.n
                wtv = sv.n

                @SE
                def _(s):
                    sv.wait(s, wtv)
                    ss.bump(s.activation(w2tile[:], wtile[:], AF.Square))
                w2v = ss.n
                c_wtile_rd[0] = ss.n

                @TE
                def _(t):
                    ss.wait(t, w2v)
                    for cc in range(8):
                        ins = t.matmul(ps_c[:, cc * 16:(cc + 1) * 16],
                                       w2tile[:, cc * 128:(cc + 1) * 128],
                                       pibf[:, i, :],
                                       start=(i == 0), stop=(i == NT - 1))
                        if cc == 7:
                            sm.bump(ins)
            dmv = sm.n

            @VE
            def _(v):
                sm.wait(v, dmv)
                for j in range(8):
                    v.tensor_copy(dotsc[0:64, j:j + 1],
                                  ps_c[0:64, j * 16 + 2 * j:j * 16 + 2 * j + 1])
                    ins = v.tensor_copy(
                        dotsc[64:128, j:j + 1],
                        ps_c[64:128, j * 16 + 2 * j + 1:j * 16 + 2 * j + 2])
                sv.bump(ins)
            dcv = sv.n

            @VE
            def _(v):
                sv.wait(v, dcv)
                sv.bump(v.tensor_tensor(attnc[:], dotsc[:], zc[:], OP.mult))
            acv0 = sv.n

            @VE
            def _(v):
                sv.wait(v, acv0)
                sv.bump(v.tensor_scalar(attnc[:], attnc[:], 1.0, None,
                                        OP.add))
            acv = sv.n

            @VE
            def _(v):
                sv.wait(v, acv)
                sv.bump(v.reciprocal(attnc[:], attnc[:]))
            afv = sv.n

            pp_free = sv.n
            xguard = {}
            for tbk in range(8):
                for cc in range(8):
                    evv = pp_free

                    @TE
                    def _(t):
                        sv.wait(t, afv)
                        sv.wait(t, evv)
                        sm.bump(t.matmul(ps_c[:, 0:512],
                                         emat[0:16, cc * 128:(cc + 1) * 128],
                                         piT[:, tbk * 512:(tbk + 1) * 512],
                                         start=True, stop=True))
                    pev = sm.n

                    @VE
                    def _(v):
                        sm.wait(v, pev)
                        v.tensor_scalar(tmp1[:],
                                        wT[:, cc, tbk * 512:(tbk + 1) * 512],
                                        attnc[:, cc:cc + 1], -1.0,
                                        OP.mult, OP.mult)
                        sv.bump(v.tensor_tensor(opT[:, cc, :], tmp1[:],
                                                ps_c[:, 0:512], OP.mult))
                    pp_free = sv.n
                opv = sv.n

                for q in range(4):
                    i = tbk * 4 + q
                    buf = i % 2

                    @SY
                    def _(s):
                        g = xguard.get(i - 2)
                        if g is not None:
                            g[0].wait(s, g[1])
                        s.dma_start(xtile[:, buf, :],
                                    yb_h[i * 128:(i + 1) * 128, :]).then_inc(sx.h, 16)
                    sx.n += 16
                    xld = sx.n
                    oevv = c_out_free[0]

                    @TE
                    def _(t):
                        sv.wait(t, opv)
                        sv.wait(t, oevv)
                        for ob in range(2):
                            for cc in range(8):
                                t.matmul(ps_all[:, 4 + ob, :],
                                         opT[:, cc, q * 128:(q + 1) * 128],
                                         outwT[:, cc, ob * 512:(ob + 1) * 512],
                                         start=(cc == 0), stop=False)
                            ins = t.matmul(ps_all[:, 4 + ob, :], ones_r[:],
                                           outb_s[:, ob * 512:(ob + 1) * 512],
                                           start=False, stop=True)
                            if ob == 1:
                                sm.bump(ins)
                    mmv = sm.n
                    yv = finish_tile(i, mmv, xld, buf, ya_h, None, True,
                                     "y3", xguard)
                    c_out_free[0] = yv
            barrier()

    c_out_free = [0]
    c_sq_free = [0]
    c_sts_done = [0]
    sq_hist = []

    if "A" in phases:
        with nc.named_scope("phaseA"):
            phase_a()
    if "B1" in phases:
        with nc.named_scope("phaseB1"):
            phase_b(resid_src=ya_h, dst=yb_h, next_ln=("C" in phases), tap="y2", tag="1")
    if "C" in phases:
        with nc.named_scope("phaseC"):
            phase_c()
    if "B2" in phases:
        with nc.named_scope("phaseB2"):
            phase_b(resid_src=ya_h, dst=out_h, next_ln=False, tap=None, tag="2")

    @SY
    def _(s):
        so.wait(s, so.n)
        sg.wait(s, sg.n)
        sw.wait(s, sw.n)
        sx.wait(s, sx.n)

    es.close()
    return nc


def _consts():
    c = {}
    c["c_eye_bf"] = np.eye(128, dtype=ml_dtypes.bfloat16)
    c["c_eye_f"] = np.eye(128, dtype=np.float32)
    c["c_eye16_f"] = np.eye(16, dtype=np.float32)
    c["c_ones_1x128_bf"] = np.ones((1, 128), dtype=ml_dtypes.bfloat16)
    c["c_ones_1x128_f"] = np.ones((1, 128), dtype=np.float32)
    c["c_ones_128x1_bf"] = np.ones((128, 1), dtype=ml_dtypes.bfloat16)
    em = np.zeros((16, D), dtype=ml_dtypes.bfloat16)
    for h in range(16):
        em[h, h * 64:(h + 1) * 64] = 1
    c["c_emat_bf"] = em
    return c


def _in_map(core, inputs):
    m = {}
    m["x"] = inputs["L0_attn_a_hidden_states"][core]
    m["gamma_row"] = inputs["L0_attn_a_ln_gamma"].reshape(1, D)
    m["beta_row"] = inputs["L0_attn_a_ln_beta"].reshape(1, D)
    m["W_qkv"] = inputs["L0_attn_a_W_qkv"]
    b = inputs["L0_attn_a_b_qkv"]
    bqk = np.zeros((128, NH), np.float32)
    bv = np.zeros((64, NH), np.float32)
    for h in range(NH):
        bqk[0:64, h] = b[h * 64:(h + 1) * 64]
        bqk[64:128, h] = b[1024 + h * 64:1024 + (h + 1) * 64]
        bv[:, h] = b[2048 + h * 64:2048 + (h + 1) * 64]
    m["bqk_r"] = bqk
    m["bv_r"] = bv
    m["W_out_a"] = inputs["L0_attn_a_W_out"]
    m["b_out_a"] = inputs["L0_attn_a_b_out"].reshape(1, D)
    m["attb_qkv_w"] = inputs["L0_attn_b_qkv_w"]
    t = np.asarray(inputs["L0_attn_b_temp"]).reshape(16)
    tcm = np.zeros((128, 8), np.float32)
    for j in range(8):
        tcm[0:64, j] = t[2 * j]
        tcm[64:128, j] = t[2 * j + 1]
    m["attb_tempc"] = tcm
    m["attb_out_w"] = inputs["L0_attn_b_out_w"]
    m["attb_out_b"] = inputs["L0_attn_b_out_b"].reshape(1, D)
    m["ffn_Wq"] = inputs["L0_ffn_Wq"]
    m["ffn_keys"] = inputs["L0_ffn_keys"]
    m["ffn_W_in"] = inputs["L0_ffn_W_in"]
    m["ffn_W_out"] = inputs["L0_ffn_W_out"]
    m["ffn_in_a"] = inputs["L0_ffn_in_a"]
    m["ffn_in_b"] = inputs["L0_ffn_in_b"]
    m["ffn_out_a"] = inputs["L0_ffn_out_a"]
    m["ffn_out_b"] = inputs["L0_ffn_out_b"]
    m.update(_consts())
    return {k: np.ascontiguousarray(v) for k, v in m.items()}


def kernel(**inputs):
    nc = build_nc()
    in_maps = [_in_map(c, inputs) for c in range(8)]
    res = run_bass_kernel_spmd(nc, in_maps, core_ids=list(range(8)))
    out = np.stack([np.asarray(res.results[c]["y"]) for c in range(8)], axis=0)
    return out.astype(np.float32)


# version 04
# speedup vs baseline: 1.1352x; 1.1352x over previous
import numpy as np
import ml_dtypes
from contextlib import ExitStack

import concourse.bass as bass
import concourse.mybir as mybir
from concourse.bass_utils import run_bass_kernel_spmd
from concourse.alu_op_type import AluOpType as OP

F32 = mybir.dt.float32
BF16 = mybir.dt.bfloat16
U32 = mybir.dt.uint32
AF = mybir.ActivationFunctionType

S, D, NT = 4096, 1024, 32
NH = 16
PKH, DI, NE = 8, 2048, 16384
EPS = 1e-5


class Ctr:
    def __init__(self, es, nc, name):
        self.h = es.enter_context(nc.semaphore(name))
        self.n = 0
        self.maxw = 0
        self.seen = {}

    def bump(self, ins, k=1):
        ins.then_inc(self.h, k)
        self.n += k
        return self.n

    def wait(self, eng, v=None):
        v = self.n if v is None else v
        if v <= 0:
            return
        key = id(eng)
        if self.seen.get(key, -1) >= v:
            return
        self.seen[key] = v
        self.maxw = max(self.maxw, v)
        eng.wait_ge(self.h, v)


def build_nc(phases=("A", "B1", "C", "B2"), taps=()):
    nc = bass.Bass("TRN2", target_bir_lowering=False)
    es = ExitStack()
    tap_set = set(taps)

    dt = nc.dram_tensor
    x_h = dt("x", [S, D], F32, kind="ExternalInput").ap()
    gamma_row_h = dt("gamma_row", [1, D], F32, kind="ExternalInput").ap()
    beta_row_h = dt("beta_row", [1, D], F32, kind="ExternalInput").ap()
    wqkv_h = dt("W_qkv", [3 * D, D], F32, kind="ExternalInput").ap()
    bqk_h = dt("bqk_r", [128, NH], F32, kind="ExternalInput").ap()
    bv_h = dt("bv_r", [64, NH], F32, kind="ExternalInput").ap()
    wouta_h = dt("W_out_a", [D, D], F32, kind="ExternalInput").ap()
    bouta_h = dt("b_out_a", [1, D], F32, kind="ExternalInput").ap()
    qkvw_h = dt("attb_qkv_w", [D, D], F32, kind="ExternalInput").ap()
    tempc_h = dt("attb_tempc", [128, 8], F32, kind="ExternalInput").ap()
    outw_h = dt("attb_out_w", [D, D], F32, kind="ExternalInput").ap()
    outb_h = dt("attb_out_b", [1, D], F32, kind="ExternalInput").ap()
    wq_h = dt("ffn_Wq", [8 * D, D], F32, kind="ExternalInput").ap()
    keys_h = dt("ffn_keys", [PKH, 128, 2, 512], F32, kind="ExternalInput").ap()
    win_h = dt("ffn_W_in", [DI, D], F32, kind="ExternalInput").ap()
    wout_h = dt("ffn_W_out", [D, DI], F32, kind="ExternalInput").ap()
    tbl = {"ina": dt("ffn_in_a", [NE, D], F32, kind="ExternalInput").ap(),
           "inb": dt("ffn_in_b", [NE, DI], F32, kind="ExternalInput").ap(),
           "outa": dt("ffn_out_a", [NE, DI], F32, kind="ExternalInput").ap(),
           "outb": dt("ffn_out_b", [NE, D], F32, kind="ExternalInput").ap()}
    eye_h = dt("c_eye_bf", [128, 128], BF16, kind="ExternalInput").ap()
    eyef_h = dt("c_eye_f", [128, 128], F32, kind="ExternalInput").ap()
    ones_r_h = dt("c_ones_1x128_bf", [1, 128], BF16, kind="ExternalInput").ap()
    ones_rf_h = dt("c_ones_1x128_f", [1, 128], F32, kind="ExternalInput").ap()
    ones_c_h = dt("c_ones_128x1_bf", [128, 1], BF16, kind="ExternalInput").ap()
    emat_h = dt("c_emat_bf", [16, D], BF16, kind="ExternalInput").ap()
    eye16f_h = dt("c_eye16_f", [16, 16], F32, kind="ExternalInput").ap()

    out_h = dt("y", [S, D], F32, kind="ExternalOutput").ap()
    ya_h = dt("ya", [S, D], F32, kind="Internal").ap()
    yb_h = dt("yb", [S, D], F32, kind="Internal").ap()
    ctx_h = dt("ctxs", [128, 8, S], BF16,
               kind="ExternalOutput" if "ctx" in tap_set else "Internal").ap()
    tap_hbm = {t: dt(f"tap_{t}", [S, D], F32, kind="ExternalOutput").ap()
               for t in tap_set if t.startswith("y")}
    if "xn" in tap_set:
        tap_hbm["xn"] = dt("tap_xn", [128, 8, S], F32, kind="ExternalOutput").ap()
    if "idx" in tap_set:
        tap_hbm["idx"] = dt("tap_idx", [128, PKH, NT], U32, kind="ExternalOutput").ap()
        tap_hbm["gate"] = dt("tap_gate", [128, PKH, NT], F32, kind="ExternalOutput").ap()
    if "qkv0" in tap_set:
        tap_hbm["q0"] = dt("tap_q0", [64, S], F32, kind="ExternalOutput").ap()
        tap_hbm["k0"] = dt("tap_k0", [64, S], F32, kind="ExternalOutput").ap()
        tap_hbm["v0"] = dt("tap_v0", [64, S], F32, kind="ExternalOutput").ap()

    ec = es.enter_context
    st = nc.sbuf_tensor

    eye = ec(st("eye", [128, 128], BF16))
    eyef = ec(st("eyef", [128, 128], F32))
    eye16f = ec(st("eye16f", [16, 16], F32))
    ones_r = ec(st("ones_r", [1, 128], BF16))
    ones_rf = ec(st("ones_rf", [1, 128], F32))
    ones_c = ec(st("ones_c", [128, 1], BF16))
    emat = ec(st("emat", [16, D], BF16))
    xT = ec(st("xT", [128, 8, S], BF16))
    rsc = ec(st("rsc", [128, NT], F32))
    rbi = ec(st("rbi", [128, NT], F32))
    st6 = ec(st("st6", [128, 12], F32))
    aggr = ec(st("aggr", [128, 2], F32))
    ta = ec(st("ta", [128, 1], F32))
    tb = ec(st("tb", [128, 1], F32))
    tc = ec(st("tc", [128, 1], F32))
    xtile = ec(st("xtile", [128, 2, D], F32))
    ytile = ec(st("ytile", [128, D], F32))
    ntile = ec(st("ntile", [128, D], BF16))
    wstage = ec(st("wstage", [128, 2, D], BF16))

    ps_all = ec(nc.psum_tensor("ps_all", [128, 6, 512], F32))
    ps_c = ec(nc.psum_tensor("ps_c", [128, 512], F32))
    ps_tp = ec(nc.psum_tensor("ps_tp", [128, 512], F32))
    ps_tp_bf = ps_tp.bitcast(BF16)

    sx = Ctr(es, nc, "sx")
    sw = Ctr(es, nc, "sw")
    sg = Ctr(es, nc, "sg")
    so = Ctr(es, nc, "so")
    sm = Ctr(es, nc, "sm")
    ss = Ctr(es, nc, "ss")
    sv = Ctr(es, nc, "sv")
    scn = Ctr(es, nc, "scn")
    sco = Ctr(es, nc, "sco")

    block = es.enter_context(nc.Block())
    TE, VE, SE, SY, GE = (block.tensor, block.vector, block.scalar,
                          block.sync, block.gpsimd)

    def barrier():
        snap = [(c, c.n) for c in (sx, sw, sg, so, sm, ss, sv, scn, sco)]
        for sec in (TE, VE, SE, SY, GE):
            @sec
            def _(e):
                for c, v in snap:
                    c.wait(e, v)

    ytg = {"ss": 0, "so": 0}
    _scopes = {}
    tp_ss = [0]
    tp_sv = [0]
    wst_rd = [0, 0]

    def load_T(src_h, rows, cols, write_evac):
        ncc = cols // 128
        for j in range(rows // 128):
            buf = j % 2

            @GE
            def _(g):
                sm.wait(g, wst_rd[buf])
                g.dma_start(wstage[:, buf, :cols],
                            src_h[j * 128:(j + 1) * 128, :]).then_inc(sw.h, 16)
            sw.n += 16
            ldv = sw.n

            @TE
            def _(t):
                sw.wait(t, ldv)
                sv.wait(t, tp_sv[0])
                ss.wait(t, tp_ss[0])
                for cc in range(ncc):
                    ins = t.transpose(ps_tp_bf[:, cc * 128:(cc + 1) * 128],
                                      wstage[:, buf, cc * 128:(cc + 1) * 128],
                                      eye[:])
                    if cc == ncc - 1:
                        sm.bump(ins)
            wst_rd[buf] = sm.n
            write_evac(j, sm.n)
            tp_sv[0] = sv.n

    def simple_evacT(dstT):
        def f(j, tpv):
            @VE
            def _(v):
                sm.wait(v, tpv)
                sv.bump(v.tensor_copy(
                    dstT[:, :, j * 128:(j + 1) * 128],
                    ps_tp_bf.rearrange("p (c t) -> p c t", c=8)))
        return f

    def ln_chain(xin, out_bf, gb=None, z_out=None, store_affine=None,
                 wait_before=None):
        @VE
        def _(v):
            for c, val in (wait_before or []):
                c.wait(v, val)
            v.bn_stats(st6[:, 0:6], xin[:, 0:512])
            sv.bump(v.bn_stats(st6[:, 6:12], xin[:, 512:1024]))
        vst = sv.n

        @VE
        def _(v):
            sv.wait(v, vst)
            sv.bump(v.bn_aggr(aggr[:], st6[:]))
        vag = sv.n

        @VE
        def _(v):
            sv.wait(v, vag)
            sv.bump(v.tensor_scalar(ta[:], aggr[:, 1:2], EPS, None, OP.add))
        vta = sv.n

        @SE
        def _(s):
            sv.wait(s, vta)
            ss.bump(s.activation(tb[:], ta[:], AF.Ln))
        sln = ss.n

        @SE
        def _(s):
            ss.wait(s, sln)
            ss.bump(s.activation(tb[:], tb[:], AF.Exp, scale=-0.5))
        srs = ss.n

        @VE
        def _(v):
            ss.wait(v, srs)
            sv.bump(v.tensor_scalar(tc[:], aggr[:, 0:1], tb[:], -1.0,
                                    OP.mult, OP.mult))
        vtc = sv.n

        if store_affine is not None:
            sca, bia = store_affine

            @VE
            def _(v):
                sv.wait(v, vtc)
                v.tensor_scalar(sca, tb[:], 1.0, None, OP.add)
                sv.bump(v.tensor_scalar(bia, tc[:], 1.0, None, OP.mult))

        if gb is not None:
            gbc, bbc, zn0, z2 = gb
            tgt = zn0
        else:
            tgt = z_out if z_out is not None else out_bf

        @SE
        def _(s):
            sv.wait(s, vtc)
            ss.bump(s.activation(tgt[:], xin[:], AF.Identity,
                                 scale=tb[:], bias=tc[:]))
        szn = ss.n

        if gb is not None:
            @VE
            def _(v):
                ss.wait(v, szn)
                v.scalar_tensor_tensor(z2[:], zn0[:], 1.0, gbc[:],
                                       OP.mult, OP.mult)
                sv.bump(v.tensor_tensor(out_bf[:], z2[:], bbc[:], OP.add))
        return szn, ss.n, sv.n

    def tile_to_xT(i, src_bf, src_ctr, src_v):
        @TE
        def _(t):
            src_ctr.wait(t, src_v)
            sv.wait(t, tp_sv[0])
            ss.wait(t, tp_ss[0])
            for cc in range(8):
                ins = t.transpose(ps_tp_bf[:, cc * 128:(cc + 1) * 128],
                                  src_bf[:, cc * 128:(cc + 1) * 128], eye[:])
                if cc == 7:
                    sm.bump(ins)
        tpv = sm.n

        @VE
        def _(v):
            sm.wait(v, tpv)
            sv.bump(v.tensor_copy(
                xT[:, :, i * 128:(i + 1) * 128],
                ps_tp_bf.rearrange("p (c t) -> p c t", c=8)))
        tp_sv[0] = sv.n
        return sv.n

    def finish_tile(i, mmv, xld, buf, dst, affine_z, next_ln, tap, xguard):
        if affine_z is not None:
            @SE
            def _(s):
                sx.wait(s, xld)
                ss.bump(s.activation(affine_z[:], xtile[:, buf, :],
                                     AF.Identity, scale=rsc[:, i:i + 1],
                                     bias=rbi[:, i:i + 1]))
            rv = (ss, ss.n)
            xguard[i] = (ss, ss.n)
        else:
            rv = (sx, xld)

        yss, yso = ytg["ss"], ytg["so"]

        @VE
        def _(v):
            sm.wait(v, mmv)
            rv[0].wait(v, rv[1])
            ss.wait(v, yss)
            so.wait(v, yso)
            src = affine_z[:] if affine_z is not None else xtile[:, buf, :]
            sv.bump(v.tensor_tensor(
                ytile[:], src,
                ps_all[:, 4:6, :].rearrange("p a b -> p (a b)"), OP.add))
        yv = sv.n
        if affine_z is None:
            xguard[i] = (sv, yv)

        @SY
        def _(s):
            sv.wait(s, yv)
            s.dma_start(dst[i * 128:(i + 1) * 128, :],
                        ytile[:]).then_inc(so.h, 16)
            if tap is not None and tap in tap_hbm:
                s.dma_start(tap_hbm[tap][i * 128:(i + 1) * 128, :],
                            ytile[:]).then_inc(so.h, 16)
        so.n += 16 * (2 if (tap in tap_hbm) else 1)
        ytg["so"] = so.n

        if next_ln:
            szn, sdv, vdv = ln_chain(ytile, ntile, wait_before=[(sv, yv)])
            tile_to_xT(i, ntile, ss, sdv)
            ytg["ss"] = szn
        return yv

    @SY
    def _(s):
        for dsts, srcs in ((eye, eye_h), (eyef, eyef_h), (eye16f, eye16f_h),
                           (ones_r, ones_r_h), (ones_rf, ones_rf_h),
                           (ones_c, ones_c_h), (emat, emat_h)):
            s.dma_start(dsts[:], srcs).then_inc(sx.h, 16)
    sx.n += 7 * 16
    barrier()

    def phase_a():
        with ExitStack() as phx:
            pec = phx.enter_context
            gr_row = pec(st("a_gr", [1, D], F32))
            br_row = pec(st("a_br", [1, D], F32))
            gbc = pec(st("a_gbc", [128, D], F32))
            bbc = pec(st("a_bbc", [128, D], F32))
            ztile = pec(st("a_ztile", [128, D], F32))
            zn0 = pec(st("a_zn0", [128, D], F32))
            z2 = pec(st("a_z2", [128, D], F32))
            wqkvT = pec(st("a_wqkvT", [128, 8, 768], BF16))
            bqk_s = pec(st("a_bqk", [128, NH], F32))
            bv_s = pec(st("a_bv", [64, NH], F32))
            woutT = pec(st("a_woutT", [128, 8, D], BF16))
            bouta_s = pec(st("a_bouta", [1, D], BF16))
            qT = pec(st("a_qT", [64, S], BF16))
            kT = pec(st("a_kT", [64, S], BF16))
            vT = pec(st("a_vT", [64, S], BF16))
            vb = pec(st("a_vb", [128, NT, 66], BF16))
            pT = pec(st("a_pT", [128, 6, 512], BF16))
            recb = pec(st("a_recb", [1, 512], BF16))
            recf = pec(st("a_recf", [64, 512], F32))
            cstg = pec(st("a_cstg", [64, 512], BF16))
            ctile = pec(st("a_ctile", [128, 2, 8, 128], BF16))

            @SY
            def _(s):
                s.dma_start(bqk_s[:], bqk_h).then_inc(sx.h, 16)
                s.dma_start(bv_s[:], bv_h).then_inc(sx.h, 16)
                s.dma_start(gr_row[:], gamma_row_h).then_inc(sx.h, 16)
                s.dma_start(br_row[:], beta_row_h).then_inc(sx.h, 16)
            sx.n += 64
            gbrow = sx.n

            @GE
            def _(g):
                g.dma_start(bouta_s[:], bouta_h).then_inc(sw.h, 16)
            sw.n += 16

            @TE
            def _(t):
                sx.wait(t, gbrow)
                for half in range(2):
                    t.matmul(ps_all[:, 2 * half, :], ones_rf[:],
                             gr_row[:, half * 512:(half + 1) * 512],
                             start=True, stop=True)
                    ins = t.matmul(ps_all[:, 2 * half + 1, :], ones_rf[:],
                                   br_row[:, half * 512:(half + 1) * 512],
                                   start=True, stop=True)
                    if half == 1:
                        sm.bump(ins)
            gmmv = sm.n

            @VE
            def _(v):
                sm.wait(v, gmmv)
                v.tensor_copy(gbc[:, 0:512], ps_all[:, 0, :])
                v.tensor_copy(bbc[:, 0:512], ps_all[:, 1, :])
                v.tensor_copy(gbc[:, 512:1024], ps_all[:, 2, :])
                sv.bump(v.tensor_copy(bbc[:, 512:1024], ps_all[:, 3, :]))
            gb_done = sv.n

            load_T(wouta_h, D, D, simple_evacT(woutT))

            @VE
            def _(v):
                v.memset(vb[:, :, 64:66], 0.0)
                sv.bump(v.memset(vb[:, :, 64:65], 1.0))
            vb_ones = sv.n
            barrier()

            _scopes["A1"] = nc.enter_named_scope("A1", False)[0]
            zdone = {}
            for i in range(NT):
                buf = i % 2

                @SY
                def _(s):
                    if i >= 2:
                        ss.wait(s, zdone[i - 2])
                    s.dma_start(xtile[:, buf, :],
                                x_h[i * 128:(i + 1) * 128, :]).then_inc(sx.h, 16)
                sx.n += 16
                xld = sx.n
                szn, _s, _v = ln_chain(
                    xtile[:, buf, :], None, z_out=ztile,
                    store_affine=(rsc[:, i:i + 1], rbi[:, i:i + 1]),
                    wait_before=[(sx, xld)])
                zdone[i] = szn
                szn2, sdv, vdv = ln_chain(ztile, ntile,
                                          gb=(gbc, bbc, zn0, z2),
                                          wait_before=[(ss, szn)])
                tile_to_xT(i, ntile, sv, vdv)
            barrier()
            if "xn" in tap_set:
                @GE
                def _(g):
                    g.dma_start(tap_hbm["xn"], xT[:]).then_inc(sw.h, 16)
                sw.n += 16
                barrier()
            nc.leave_named_scope("A1", _scopes["A1"], False)
            _scopes["A2"] = nc.enter_named_scope("A2", False)[0]

            for quarter in range(4):
                for band in range(3):
                    row0 = band * 1024 + quarter * 256

                    def wq_evac(j, tpv, band=band):
                        @VE
                        def _(v):
                            sm.wait(v, tpv)
                            sv.bump(v.tensor_copy(
                                wqkvT[:, :, band * 256 + j * 128:
                                      band * 256 + (j + 1) * 128],
                                ps_tp_bf.rearrange("p (c t) -> p c t", c=8)))
                    load_T(wqkv_h[row0:row0 + 256, :], 256, D, wq_evac)
                wq_done = sv.n
                for hh in range(4):
                    h = quarter * 4 + hh
                    a2_head(h, hh, wqkvT, bqk_s, bv_s, qT, kT, vT, vb, pT,
                            recb, recf, cstg, wq_done, vb_ones)
            barrier()
            nc.leave_named_scope("A2", _scopes["A2"], False)
            _scopes["A3"] = nc.enter_named_scope("A3", False)[0]

            xguard = {}
            cld = {}
            for i in range(NT):
                buf = i % 2
                cbuf = i % 2
                prev = a3_ctile_used.get(i - 2, 0)

                @SY
                def _(s):
                    sm.wait(s, prev)
                    s.dma_start(ctile[:, cbuf, :, :],
                                ctx_h[:, :, i * 128:(i + 1) * 128]).then_inc(sx.h, 16)
                sx.n += 16
                cldv = sx.n

                @SY
                def _(s):
                    g = xguard.get(i - 2)
                    if g is not None:
                        g[0].wait(s, g[1])
                    s.dma_start(xtile[:, buf, :],
                                x_h[i * 128:(i + 1) * 128, :]).then_inc(sx.h, 16)
                sx.n += 16
                xld = sx.n

                evv = a3_psfree[0]

                @TE
                def _(t):
                    sv.wait(t, evv)
                    sx.wait(t, cldv)
                    for ob in range(2):
                        for cc in range(8):
                            t.matmul(ps_all[:, 4 + ob, :],
                                     ctile[:, cbuf, cc, :],
                                     woutT[:, cc, ob * 512:(ob + 1) * 512],
                                     start=(cc == 0), stop=False)
                        ins = t.matmul(ps_all[:, 4 + ob, :], ones_r[:],
                                       bouta_s[:, ob * 512:(ob + 1) * 512],
                                       start=False, stop=True)
                        if ob == 1:
                            sm.bump(ins)
                mmv = sm.n
                a3_ctile_used[i] = mmv
                yv = finish_tile(i, mmv, xld, buf, ya_h, ztile, True, "y1",
                                 xguard)
                a3_psfree[0] = yv
            barrier()
            nc.leave_named_scope("A3", _scopes["A3"], False)

    a3_ctile_used = {}
    a3_psfree = [0]

    def a2_head(h, hh, wqkvT, bqk_s, bv_s, qT, kT, vT, vb, pT, recb, recf,
                cstg, wq_done, vb_ones):
        w3 = wqkvT.rearrange("p c (b o) -> p c b o", b=3)
        for tbk in range(8):
            bk = (tbk % 2) * 2
            evv = sv.n

            @TE
            def _(t):
                sv.wait(t, wq_done)
                sv.wait(t, evv)
                for cc in range(8):
                    t.matmul(ps_all[0:64, bk, :],
                             w3[:, cc, 0, hh * 64:(hh + 1) * 64],
                             xT[:, cc, tbk * 512:(tbk + 1) * 512],
                             start=(cc == 0), stop=(cc == 7))
                for cc in range(8):
                    t.matmul(ps_all[64:128, bk, :],
                             w3[:, cc, 1, hh * 64:(hh + 1) * 64],
                             xT[:, cc, tbk * 512:(tbk + 1) * 512],
                             start=(cc == 0), stop=(cc == 7),
                             tile_position=(0, 64))
                for cc in range(8):
                    ins = t.matmul(ps_all[0:64, bk + 1, :],
                                   w3[:, cc, 2, hh * 64:(hh + 1) * 64],
                                   xT[:, cc, tbk * 512:(tbk + 1) * 512],
                                   start=(cc == 0), stop=(cc == 7))
                    if cc == 7:
                        sm.bump(ins)
            mmv = sm.n

            @VE
            def _(v):
                sm.wait(v, mmv)
                sl = slice(tbk * 512, (tbk + 1) * 512)
                v.tensor_scalar(qT[:, sl], ps_all[0:64, bk, :],
                                bqk_s[0:64, h:h + 1], None, OP.add)
                v.tensor_scalar(kT[:, sl], ps_all[64:128, bk, :],
                                bqk_s[64:128, h:h + 1], None, OP.add)
                sv.bump(v.tensor_scalar(vT[:, sl], ps_all[0:64, bk + 1, :],
                                        bv_s[:, h:h + 1], None, OP.add))
        vT_done = sv.n
        if h == 0 and "q0" in tap_hbm:
            @GE
            def _(g):
                ss.wait(g, vT_done)
                g.dma_start(tap_hbm["q0"], qT[:]).then_inc(sw.h, 16)
                g.dma_start(tap_hbm["k0"], kT[:]).then_inc(sw.h, 16)
                g.dma_start(tap_hbm["v0"], vT[:]).then_inc(sw.h, 16)
            sw.n += 48
            barrier()

        for grp in range(4):
            @TE
            def _(t):
                sv.wait(t, vT_done)
                ss.wait(t, tp_ss[0])
                sv.wait(t, tp_sv[0])
                sv.wait(t, vb_ones)
                for k in range(8):
                    kc = grp * 8 + k
                    ins = t.transpose(ps_tp_bf[:, k * 64:(k + 1) * 64],
                                      vT[:, kc * 128:(kc + 1) * 128],
                                      eye[0:64, 0:64])
                    if k == 7:
                        sm.bump(ins)
            tpv = sm.n

            @VE
            def _(v):
                sm.wait(v, tpv)
                sv.bump(v.tensor_copy(
                    vb[:, grp * 8:(grp + 1) * 8, 0:64],
                    ps_tp_bf[:, 0:512].rearrange("p (c t) -> p c t", c=8)))
            tp_sv[0] = sv.n
        vb_done = sv.n

        groups = [list(range(g * 3, min(32, g * 3 + 3))) for g in range(11)]
        ng = len(groups)
        for qb in range(8):
            qsl = slice(qb * 512, (qb + 1) * 512)
            edone = {}
            cdone = {}
            sdone = {}

            def emit_s(g):
                chunks = groups[g]
                bs = (g % 2) * 3

                @TE
                def _(t):
                    if g >= 2:
                        ss.wait(t, edone[g - 2])
                    sv.wait(t, vb_done)
                    for idx, kc in enumerate(chunks):
                        ins = t.matmul(ps_all[:, bs + idx, :],
                                       kT[:, kc * 128:(kc + 1) * 128],
                                       qT[:, qsl], start=True, stop=True)
                        if idx == len(chunks) - 1:
                            sm.bump(ins)
                sdone[g] = sm.n

            def emit_e(g):
                chunks = groups[g]
                bs = (g % 2) * 3

                @SE
                def _(s):
                    sm.wait(s, sdone[g])
                    if g >= 2:
                        sm.wait(s, cdone[g - 2])
                    ss.bump(s.activation(pT[:, bs:bs + len(chunks), :],
                                         ps_all[:, bs:bs + len(chunks), :],
                                         AF.Exp, scale=0.125))
                edone[g] = ss.n

            def emit_c(g):
                chunks = groups[g]
                bs = (g % 2) * 3

                @TE
                def _(t):
                    ss.wait(t, edone[g])
                    if g == 0:
                        sv.wait(t, a2_ctx_free[0])
                    for idx, kc in enumerate(chunks):
                        ins = t.matmul(ps_c[0:65, :],
                                       vb[:, kc, 0:65],
                                       pT[:, bs + idx, :],
                                       start=(kc == 0), stop=(kc == 31))
                        if idx == len(chunks) - 1:
                            sm.bump(ins)
                cdone[g] = sm.n

            emit_s(0)
            emit_e(0)
            for g in range(1, ng):
                emit_s(g)
                emit_e(g)
                emit_c(g - 1)
            emit_c(ng - 1)
            ctx_done = sm.n

            @VE
            def _(v):
                sm.wait(v, ctx_done)
                with nc.allow_low_precision(reason="bf16 softmax recip bcast"):
                    sv.bump(v.reciprocal(recb[:], ps_c[64:65, :]))
            rcv = sv.n

            @TE
            def _(t):
                sv.wait(t, rcv)
                ss.wait(t, tp_ss[0])
                sv.wait(t, tp_sv[0])
                sm.bump(t.matmul(ps_tp[0:64, :], ones_r[0:1, 0:64], recb[:],
                                 start=True, stop=True))
            bcv = sm.n

            @SE
            def _(s):
                sm.wait(s, bcv)
                sv.wait(s, a2_ctx_free[0])
                ss.bump(s.copy(recf[:], ps_tp[0:64, :]))
            a2_recf_free[0] = ss.n
            tp_ss[0] = ss.n
            rfv = ss.n

            @VE
            def _(v):
                ss.wait(v, rfv)
                so.wait(v, a2_cstg_free[0])
                sv.bump(v.tensor_tensor(cstg[:], ps_c[0:64, :], recf[:],
                                        OP.mult))
            csv = sv.n
            a2_ctx_free[0] = csv

            @SY
            def _(s):
                sv.wait(s, csv)
                s.dma_start(ctx_h[(h % 2) * 64:(h % 2) * 64 + 64, h // 2, qsl],
                            cstg[:]).then_inc(so.h, 16)
            so.n += 16
            a2_cstg_free[0] = so.n

    a2_ctx_free = [0]
    a2_cstg_free = [0]
    a2_recf_free = [0]

    def phase_b(resid_src, dst, next_ln, tap, tag):
        with ExitStack() as phx:
            pec = phx.enter_context
            winT = pec(st("b" + tag + "_winT", [128, 8, DI], BF16))
            woutfT = pec(st("b" + tag + "_woutfT", [128, 16, D], BF16))
            sxv = pec(st("b" + tag + "_sx", [128, PKH, NT], F32))
            ixv = pec(st("b" + tag + "_ix", [128, PKH, NT], U32))
            idxv = pec(st("b" + tag + "_idx", [128, PKH, NT], U32))
            gatev = pec(st("b" + tag + "_gate", [128, PKH, NT], F32))
            g_ina = pec(st("b" + tag + "_gina", [128, D], F32))
            g_inb = pec(st("b" + tag + "_ginb", [128, 2, DI], F32))
            g_outa = pec(st("b" + tag + "_gouta", [128, DI], F32))
            g_outb = pec(st("b" + tag + "_goutb", [128, D], F32))
            xftile = pec(st("b" + tag + "_xf", [128, D], BF16))
            linb = pec(st("b" + tag + "_lin", [128, PKH], F32))
            loutb = pec(st("b" + tag + "_lout", [128, PKH], F32))
            b1_scope = ExitStack()
            b1c = b1_scope.enter_context
            wqT = b1c(st("b" + tag + "_wqT", [128, 8, 512], BF16))
            qTb = b1c(st("b" + tag + "_qT", [128, 4, 512], F32))
            keysT = b1c(st("b" + tag + "_keysT", [128, 4, 128], F32))
            simt = b1c(st("b" + tag + "_simt", [128, 128], F32))
            mv8 = b1c(st("b" + tag + "_mv8", [128, 8], F32))
            mi8 = b1c(st("b" + tag + "_mi8", [128, 8], U32))

            load_T(win_h, DI, D, simple_evacT(winT))

            def woutf_evac(base):
                def f(j, tpv):
                    @VE
                    def _(v):
                        sm.wait(v, tpv)
                        sv.bump(v.tensor_copy(
                            woutfT[:, base:base + 8, j * 128:(j + 1) * 128],
                            ps_tp_bf.rearrange("p (c t) -> p c t", c=8)))
                return f
            load_T(wout_h[:, 0:1024], D, 1024, woutf_evac(0))
            load_T(wout_h[:, 1024:2048], D, 1024, woutf_evac(8))
            barrier()

            _scopes["B1q" + tag] = nc.enter_named_scope("B1q" + tag, False)[0]
            bank_free = [sv.n] * 4
            sim_free = sv.n
            key_free = sm.n
            for ph_i in range(16):
                p_, h_ = ph_i // 8, ph_i % 8
                row0 = (p_ * 8 + h_) * 512
                load_T(wq_h[row0:row0 + 512, :], 512, D, simple_evacT(wqT))
                wqv = sv.n

                kstage = qTb[:, 0, :]

                @SY
                def _(s):
                    sm.wait(s, key_free)
                    sv.wait(s, b1_qtb_writes[0])
                    s.dma_start(kstage,
                                keys_h[h_, :, p_, :]).then_inc(sx.h, 16)
                sx.n += 16
                kldv = sx.n

                @TE
                def _(t):
                    sx.wait(t, kldv)
                    sv.wait(t, tp_sv[0])
                    ss.wait(t, tp_ss[0])
                    for m in range(4):
                        ins = t.transpose(ps_tp[:, m * 128:(m + 1) * 128],
                                          kstage[:, m * 128:(m + 1) * 128],
                                          eyef[:])
                        if m == 3:
                            sm.bump(ins)
                ktp = sm.n

                @VE
                def _(v):
                    sm.wait(v, ktp)
                    sv.bump(v.tensor_copy(
                        keysT[:, :, :],
                        ps_tp[:, 0:512].rearrange("p (c t) -> p c t", c=4)))
                tp_sv[0] = sv.n
                kv = sv.n

                for tbk in range(8):
                    for m in range(4):
                        bk = m

                        @TE
                        def _(t):
                            sv.wait(t, wqv)
                            sv.wait(t, bank_free[bk])
                            for cc in range(8):
                                ins = t.matmul(
                                    ps_all[:, bk, :],
                                    wqT[:, cc, m * 128:(m + 1) * 128],
                                    xT[:, cc, tbk * 512:(tbk + 1) * 512],
                                    start=(cc == 0), stop=(cc == 7))
                                if cc == 7:
                                    sm.bump(ins)
                        mmv = sm.n

                        @VE
                        def _(v):
                            sm.wait(v, mmv)
                            sm.wait(v, qtb_free[0])
                            sm.wait(v, ktp)
                            sv.bump(v.tensor_copy(qTb[:, m, :],
                                                  ps_all[:, bk, :]))
                        bank_free[bk] = sv.n
                        b1_qtb_writes[0] = sv.n
                    qdv = ss.n

                    for q in range(4):
                        i = tbk * 4 + q

                        @TE
                        def _(t):
                            ss.wait(t, qdv)
                            sv.wait(t, sim_free)
                            sv.wait(t, kv)
                            for m in range(4):
                                ins = t.matmul(ps_c[:, 0:128],
                                               qTb[:, m, q * 128:(q + 1) * 128],
                                               keysT[:, m, :],
                                               start=(m == 0), stop=(m == 3))
                                if m == 3:
                                    sm.bump(ins)
                        smv = sm.n
                        if q == 3:
                            qtb_free[0] = smv
                        key_free = smv

                        @VE
                        def _(v):
                            sm.wait(v, smv)
                            sv.bump(v.tensor_copy(simt[:], ps_c[:, 0:128]))
                        sim_free = sv.n
                        cpv = sv.n

                        @VE
                        def _(v):
                            sv.wait(v, cpv)
                            sv.bump(v.max(mv8[:], simt[:]))
                        mxv = sv.n

                        @VE
                        def _(v):
                            sv.wait(v, mxv)
                            sv.bump(v.max_index(mi8[:], mv8[:], simt[:]))
                        miv = sv.n

                        if p_ == 0:
                            @VE
                            def _(v):
                                sv.wait(v, miv)
                                v.tensor_copy(sxv[:, h_, i:i + 1], mv8[:, 0:1])
                                sv.bump(v.tensor_copy(ixv[:, h_, i:i + 1],
                                                      mi8[:, 0:1]))
                        else:
                            @VE
                            def _(v):
                                sv.wait(v, miv)
                                v.scalar_tensor_tensor(
                                    idxv[:, h_, i:i + 1], ixv[:, h_, i:i + 1],
                                    128.0, mi8[:, 0:1], OP.mult, OP.add)
                                sv.bump(v.tensor_scalar(
                                    gatev[:, h_, i:i + 1],
                                    sxv[:, h_, i:i + 1], mv8[:, 0:1], 0.0,
                                    OP.add, OP.max))
            nc.leave_named_scope("B1q" + tag, _scopes["B1q" + tag], False)
            barrier()
            if "idx" in tap_set and tag == "1":
                @GE
                def _(g):
                    g.dma_start(tap_hbm["idx"], idxv[:]).then_inc(sw.h, 16)
                    g.dma_start(tap_hbm["gate"], gatev[:]).then_inc(sw.h, 16)
                sw.n += 32
                barrier()
            b1_scope.close()
            _scopes["B3_" + tag] = nc.enter_named_scope("B3_" + tag, False)[0]
            prod = pec(st("b" + tag + "_prod", [128, D], BF16))
            hidbf = pec(st("b" + tag + "_hidbf", [128, DI], BF16))
            hT = pec(st("b" + tag + "_hT", [128, 16, 128], BF16))
            hacc = pec(st("b" + tag + "_hacc", [128, DI], F32))
            lp = pec(st("b" + tag + "_lp", [128, PKH], F32))
            lq = pec(st("b" + tag + "_lq", [128, PKH], F32))

            scn0 = scn.n
            sco0 = sco.n
            sg0 = sg.n
            sg_in_base = {}
            sg_out_base = {}
            sgc = [sg.n]
            GQD = 4

            def emit_in(i):
                sg_in_base[i] = sgc[0]

                @GE
                def _(g):
                    for h_ in range(PKH):
                        ghc = i * PKH + h_
                        if ghc >= 1:
                            scn.wait(g, scn0 + max(0, 2 * ghc - 1))
                        sg.wait(g, sg.n - (GQD - 1) * 16)
                        g.indirect_dma_start(
                            g_ina[:, :], None, tbl["ina"],
                            bass.IndirectOffsetOnAxis(idxv[:, h_, i:i + 1], 0)
                        ).then_inc(sg.h, 16)
                        sg.n += 16
                        if ghc >= 2:
                            scn.wait(g, scn0 + 2 * ghc - 2)
                        sg.wait(g, sg.n - (GQD - 1) * 16)
                        g.indirect_dma_start(
                            g_inb[:, ghc % 2, :], None, tbl["inb"],
                            bass.IndirectOffsetOnAxis(idxv[:, h_, i:i + 1], 0)
                        ).then_inc(sg.h, 16)
                        sg.n += 16
                sgc[0] = sg.n

            def emit_out(i):
                sg_out_base[i] = sgc[0]

                @GE
                def _(g):
                    for h_ in range(PKH):
                        ghc = i * PKH + h_
                        if ghc >= 1:
                            sco.wait(g, sco0 + 2 * ghc)
                        sg.wait(g, sg.n - (GQD - 1) * 16)
                        g.indirect_dma_start(
                            g_outa[:, :], None, tbl["outa"],
                            bass.IndirectOffsetOnAxis(idxv[:, h_, i:i + 1], 0)
                        ).then_inc(sg.h, 16)
                        sg.n += 16
                        sg.wait(g, sg.n - (GQD - 1) * 16)
                        g.indirect_dma_start(
                            g_outb[:, :], None, tbl["outb"],
                            bass.IndirectOffsetOnAxis(idxv[:, h_, i:i + 1], 0)
                        ).then_inc(sg.h, 16)
                        sg.n += 16
                sgc[0] = sg.n

            for i in range(NT):
                emit_in(i)
                emit_out(i)

            def gwait(ghc, k):
                i, h_ = divmod(ghc, PKH)
                if k < 2:
                    return sg_in_base[i] + (2 * h_ + k + 1) * 16
                return sg_out_base[i] + (2 * h_ + (k - 2) + 1) * 16

            hid_free = sv.n
            out_free = sv.n
            xguard = {}
            b3_xf_rd = [0]
            for i in range(NT):
                buf = i % 2

                @TE
                def _(t):
                    sv.wait(t, tp_sv[0])
                    ss.wait(t, tp_ss[0])
                    sv.wait(t, b3_xf_rd[0])
                    for cc in range(8):
                        ins = t.transpose(ps_tp_bf[:, cc * 128:(cc + 1) * 128],
                                          xT[:, cc, i * 128:(i + 1) * 128],
                                          eye[:])
                        if cc == 7:
                            sm.bump(ins)
                tpv = sm.n

                @VE
                def _(v):
                    sm.wait(v, tpv)
                    sv.bump(v.tensor_copy(xftile[:], ps_tp_bf[:, 0:1024]))
                tp_sv[0] = sv.n
                xfv = sv.n

                @TE
                def _(t):
                    sv.wait(t, hid_free)
                    for nb in range(4):
                        for cc in range(8):
                            ins = t.matmul(ps_all[:, nb, :],
                                           xT[:, cc, i * 128:(i + 1) * 128],
                                           winT[:, cc, nb * 512:(nb + 1) * 512],
                                           start=(cc == 0), stop=(cc == 7))
                        if nb == 3:
                            sm.bump(ins)
                winv = sm.n

                for h_ in range(PKH):
                    ghc = i * PKH + h_

                    @VE
                    def _(v):
                        sg.wait(v, gwait(ghc, 0))
                        sv.wait(v, xfv)
                        v.scalar_tensor_tensor(
                            prod[:, 0:1024], g_ina[:, :], 1.0, xftile[:],
                            OP.mult, OP.mult,
                            accum_out=linb[:, h_:h_ + 1])
                        sv.bump(v.nop())
                        scn.bump(v.nop())
                    linv = sv.n
                    if h_ == PKH - 1:
                        b3_xf_rd[0] = sv.n

                    @VE
                    def _(v):
                        sv.wait(v, linv)
                        sv.bump(v.tensor_scalar(lp[:, h_:h_ + 1],
                                                linb[:, h_:h_ + 1],
                                                gatev[:, h_, i:i + 1], None,
                                                OP.mult))
                    lpv = sv.n

                    @VE
                    def _(v):
                        sv.wait(v, lpv)
                        sg.wait(v, gwait(ghc, 1))
                        if h_ == 0:
                            sm.wait(v, winv)
                            v.scalar_tensor_tensor(
                                hacc[:], g_inb[:, ghc % 2, :],
                                lp[:, h_:h_ + 1],
                                ps_all[:, 0:4, :].rearrange("p a b -> p (a b)"),
                                OP.mult, OP.add)
                        else:
                            v.scalar_tensor_tensor(
                                hacc[:], g_inb[:, ghc % 2, :],
                                lp[:, h_:h_ + 1], hacc[:],
                                OP.mult, OP.add)
                        sv.bump(v.nop())
                        scn.bump(v.nop())
                    if h_ == 0:
                        hid_free = sv.n
                hidv = sv.n

                @SE
                def _(s):
                    sv.wait(s, hidv)
                    ss.bump(s.activation(hidbf[:], hacc[:], AF.Gelu))
                gfv = ss.n

                for grp in range(2):
                    @TE
                    def _(t):
                        ss.wait(t, gfv)
                        sv.wait(t, tp_sv[0])
                        ss.wait(t, tp_ss[0])
                        for k in range(8):
                            cc = grp * 8 + k
                            ins = t.transpose(
                                ps_tp_bf[:, k * 128:(k + 1) * 128],
                                hidbf[:, cc * 128:(cc + 1) * 128], eye[:])
                            if k == 7:
                                sm.bump(ins)
                    tpv = sm.n

                    @VE
                    def _(v):
                        sm.wait(v, tpv)
                        sv.bump(v.tensor_copy(
                            hT[:, grp * 8:(grp + 1) * 8, :],
                            ps_tp_bf.rearrange("p (c t) -> p c t", c=8)))
                    tp_sv[0] = sv.n
                hTv = sv.n

                @TE
                def _(t):
                    sv.wait(t, hTv)
                    sv.wait(t, out_free)
                    for ob in range(2):
                        for cc in range(16):
                            ins = t.matmul(
                                ps_all[:, 4 + ob, :], hT[:, cc, :],
                                woutfT[:, cc, ob * 512:(ob + 1) * 512],
                                start=(cc == 0), stop=(cc == 15))
                        if ob == 1:
                            sm.bump(ins)
                woutv = sm.n

                @SY
                def _(s):
                    g = xguard.get(i - 2)
                    if g is not None:
                        g[0].wait(s, g[1])
                    s.dma_start(xtile[:, buf, :],
                                resid_src[i * 128:(i + 1) * 128, :]).then_inc(sx.h, 16)
                sx.n += 16
                xld = sx.n

                yss, yso = ytg["ss"], ytg["so"]
                for h_ in range(PKH):
                    ghc = i * PKH + h_

                    @VE
                    def _(v):
                        sg.wait(v, gwait(ghc, 2))
                        ss.wait(v, gfv)
                        v.scalar_tensor_tensor(
                            hacc.bitcast(BF16)[:, 0:DI], g_outa[:, :], 1.0,
                            hidbf[:], OP.mult, OP.mult,
                            accum_out=loutb[:, h_:h_ + 1])
                        sv.bump(v.nop())
                        sco.bump(v.nop())
                    lov = sv.n

                    @VE
                    def _(v):
                        sv.wait(v, lov)
                        sv.bump(v.tensor_scalar(lq[:, h_:h_ + 1],
                                                loutb[:, h_:h_ + 1],
                                                gatev[:, h_, i:i + 1], None,
                                                OP.mult))
                    lqv = sv.n

                    @VE
                    def _(v):
                        sv.wait(v, lqv)
                        sg.wait(v, gwait(ghc, 3))
                        if h_ == 0:
                            sm.wait(v, woutv)
                            ss.wait(v, yss)
                            so.wait(v, yso)
                            v.scalar_tensor_tensor(
                                ytile[:], g_outb[:, :], lq[:, h_:h_ + 1],
                                ps_all[:, 4:6, :].rearrange("p a b -> p (a b)"),
                                OP.mult, OP.add)
                        else:
                            v.scalar_tensor_tensor(
                                ytile[:], g_outb[:, :], lq[:, h_:h_ + 1],
                                ytile[:], OP.mult, OP.add)
                        sv.bump(v.nop())
                        sco.bump(v.nop())
                    if h_ == 0:
                        out_free = sv.n
                oaxv = sv.n

                @VE
                def _(v):
                    sv.wait(v, oaxv)
                    sx.wait(v, xld)
                    sv.bump(v.tensor_tensor(ytile[:], ytile[:],
                                            xtile[:, buf, :], OP.add))
                yv = sv.n
                xguard[i] = (sv, yv)

                @SY
                def _(s):
                    sv.wait(s, yv)
                    s.dma_start(dst[i * 128:(i + 1) * 128, :],
                                ytile[:]).then_inc(so.h, 16)
                    if tap is not None and tap in tap_hbm:
                        s.dma_start(tap_hbm[tap][i * 128:(i + 1) * 128, :],
                                    ytile[:]).then_inc(so.h, 16)
                so.n += 16 * (2 if (tap in tap_hbm) else 1)
                ytg["so"] = so.n

                if next_ln:
                    szn, sdv, vdv = ln_chain(ytile, ntile,
                                             wait_before=[(sv, yv)])
                    tile_to_xT(i, ntile, ss, sdv)
                    ytg["ss"] = szn
            barrier()
            nc.leave_named_scope("B3_" + tag, _scopes["B3_" + tag], False)

    qtb_free = [0]
    b1_qtb_writes = [0]

    def phase_c():
        with ExitStack() as phx:
            pec = phx.enter_context
            outb_s = pec(st("c_outb", [1, D], BF16))
            tempc = pec(st("c_tempc", [128, 8], F32))
            wT = pec(st("c_wT", [128, 8, S], BF16))
            norm2p = pec(st("c_norm2p", [128, 8, 8], F32))
            norm2 = pec(st("c_norm2", [128, 8], F32))
            gvec = pec(st("c_g", [128, 8], F32))
            gmat = pec(st("c_gmat", [128, 8, 16], BF16))
            sqsc = pec(st("c_sqsc", [128, 2, 512], BF16))
            sts = pec(st("c_sts", [16, 512], F32))
            sti = pec(st("c_sti", [128, 16], F32))
            e16 = pec(st("c_e16", [128, 16], F32))
            esc = pec(st("c_esc", [128, 16], F32))
            pibf = pec(st("c_pibf", [128, NT, 16], BF16))
            piT = pec(st("c_piT", [16, S], BF16))
            zrow = pec(st("c_zrow", [16, 1], F32))
            zrowT = pec(st("c_zrowT", [1, 16], F32))
            zc = pec(st("c_zc", [128, 8], F32))
            dotsc = pec(st("c_dotsc", [128, 8], F32))
            attnc = pec(st("c_attnc", [128, 8], F32))
            wtile = pec(st("c_wtile", [128, D], BF16))
            w2tile = pec(st("c_w2tile", [128, D], BF16))
            tmp1 = pec(st("c_tmp1", [128, 512], F32))
            opT = pec(st("c_opT", [128, 8, 512], BF16))

            @SY
            def _(s):
                s.dma_start(tempc[:], tempc_h).then_inc(sx.h, 16)
            sx.n += 16
            tcv = sx.n

            @GE
            def _(g):
                g.dma_start(outb_s[:], outb_h).then_inc(sw.h, 16)
            sw.n += 16

            with ExitStack() as ph2:
                qkvwT = ph2.enter_context(st("c_qkvwT", [128, 8, D], BF16))
                load_T(qkvw_h, D, D, simple_evacT(qkvwT))
                barrier()
                bank_free = [ss.n] * 4
                for tbk in range(8):
                    for oc in range(8):
                        bk = (tbk * 8 + oc) % 4
                        evv = bank_free[bk]

                        @TE
                        def _(t):
                            ss.wait(t, evv)
                            for cc in range(8):
                                ins = t.matmul(
                                    ps_all[:, bk, :],
                                    qkvwT[:, cc, oc * 128:(oc + 1) * 128],
                                    xT[:, cc, tbk * 512:(tbk + 1) * 512],
                                    start=(cc == 0), stop=(cc == 7))
                                if cc == 7:
                                    sm.bump(ins)
                        mmv = sm.n

                        @SE
                        def _(s):
                            sm.wait(s, mmv)
                            ss.bump(s.copy(
                                wT[:, oc, tbk * 512:(tbk + 1) * 512],
                                ps_all[:, bk, :]))
                        bank_free[bk] = ss.n
                barrier()
            wTv = ss.n

            outwT = pec(st("c_outwT", [128, 8, D], BF16))
            load_T(outw_h, D, D, simple_evacT(outwT))

            for cc in range(8):
                for blk in range(8):
                    par = blk % 2

                    @SE
                    def _(s):
                        ss.wait(s, wTv)
                        sv.wait(s, c_sq_free[0])
                        ss.bump(s.activation(
                            sqsc[:, par, :],
                            wT[:, cc, blk * 512:(blk + 1) * 512], AF.Square,
                            accum_out=norm2p[:, cc, blk:blk + 1]))
            nv = ss.n

            @VE
            def _(v):
                ss.wait(v, nv)
                sv.bump(v.tensor_reduce(norm2[:], norm2p[:],
                                        mybir.AxisListType.X, OP.add))
            n2r = sv.n

            @VE
            def _(v):
                sv.wait(v, n2r)
                sv.bump(v.tensor_scalar(norm2[:], norm2[:], 1e-24, None,
                                        OP.max))
            n2v = sv.n

            @VE
            def _(v):
                sv.wait(v, n2v)
                sv.bump(v.reciprocal(gvec[:], norm2[:]))
            rgv = sv.n

            @VE
            def _(v):
                sv.wait(v, rgv)
                sx.wait(v, tcv)
                sv.bump(v.tensor_tensor(gvec[:], gvec[:], tempc[:], OP.mult))
            gv = sv.n

            @VE
            def _(v):
                sv.wait(v, gv)
                sv.bump(v.memset(gmat[:], 0.0))
            gm0 = sv.n

            @VE
            def _(v):
                sv.wait(v, gm0)
                for j in range(8):
                    v.tensor_copy(gmat[0:64, j, 2 * j:2 * j + 1],
                                  gvec[0:64, j:j + 1])
                    ins = v.tensor_copy(gmat[64:128, j, 2 * j + 1:2 * j + 2],
                                        gvec[64:128, j:j + 1])
                sv.bump(ins)
            gmv = sv.n

            pst_free = sv.n
            for tbk in range(8):
                for cc in range(8):
                    par = cc % 2

                    k_glob = tbk * 8 + cc

                    @SE
                    def _(s):
                        sv.wait(s, gmv)
                        if k_glob >= 2:
                            sm.wait(s, sq_hist[k_glob - 2])
                        ss.bump(s.activation(
                            sqsc[:, par, :],
                            wT[:, cc, tbk * 512:(tbk + 1) * 512], AF.Square))
                    sqv = ss.n

                    @TE
                    def _(t):
                        ss.wait(t, sqv)
                        if tbk > 0 and cc == 0:
                            ss.wait(t, c_sts_done[0])
                        ins = t.matmul(ps_c[0:16, :], gmat[:, cc, :],
                                       sqsc[:, par, :],
                                       start=(cc == 0), stop=(cc == 7))
                        sm.bump(ins)
                    sq_hist.append(sm.n)
                stv = sm.n

                @SE
                def _(s):
                    sm.wait(s, stv)
                    ss.bump(s.copy(sts[:], ps_c[0:16, :]))
                c_sts_done[0] = ss.n
                stsv = ss.n

                for q in range(4):
                    i = tbk * 4 + q

                    @TE
                    def _(t):
                        ss.wait(t, stsv)
                        sv.wait(t, tp_sv[0])
                        ss.wait(t, tp_ss[0])
                        sm.bump(t.transpose(ps_tp[:, 0:16],
                                            sts[:, q * 128:(q + 1) * 128],
                                            eye16f[:]))
                    tpv = sm.n

                    @VE
                    def _(v):
                        sm.wait(v, tpv)
                        sv.bump(v.tensor_copy(sti[:], ps_tp[:, 0:16]))
                    tp_sv[0] = sv.n
                    stiv = sv.n

                    @SE
                    def _(s):
                        sv.wait(s, stiv)
                        ss.bump(s.activation(e16[:], sti[:], AF.Exp))
                    ev = ss.n

                    @VE
                    def _(v):
                        ss.wait(v, ev)
                        sv.bump(v.tensor_scalar(esc[:], e16[:], 1.0, None,
                                                OP.mult, OP.add,
                                                accum_out=tb[:]))
                    sev = sv.n

                    @VE
                    def _(v):
                        sv.wait(v, sev)
                        sv.bump(v.reciprocal(tc[:], tb[:]))
                    rcv = sv.n

                    @VE
                    def _(v):
                        sv.wait(v, rcv)
                        sv.bump(v.tensor_scalar(pibf[:, i, :], e16[:], tc[:],
                                                None, OP.mult))
                    piv = sv.n

                    @TE
                    def _(t):
                        sv.wait(t, piv)
                        sv.wait(t, tp_sv[0])
                        ss.wait(t, tp_ss[0])
                        sm.bump(t.transpose(ps_tp.bitcast(BF16)[0:16, 0:128],
                                            pibf[:, i, :], eye[:]))
                    ptv = sm.n

                    @VE
                    def _(v):
                        sm.wait(v, ptv)
                        sv.bump(v.tensor_copy(piT[:, i * 128:(i + 1) * 128],
                                              ps_tp.bitcast(BF16)[0:16, 0:128]))
                    tp_sv[0] = sv.n
                    pst_free = sv.n

            @TE
            def _(t):
                sv.wait(t, pst_free)
                ins = None
                for i in range(NT):
                    ins = t.matmul(ps_c[0:16, 0:1], pibf[:, i, :], ones_c[:],
                                   start=(i == 0), stop=(i == NT - 1))
                sm.bump(ins)
            zmv = sm.n

            @SE
            def _(s):
                sm.wait(s, zmv)
                ss.bump(s.copy(zrow[:], ps_c[0:16, 0:1]))
            zev = ss.n

            @VE
            def _(v):
                ss.wait(v, zev)
                sv.bump(v.tensor_scalar(zrow[:], zrow[:], 1e-8, None, OP.add))
            zadd = sv.n

            @TE
            def _(t):
                sv.wait(t, zadd)
                sv.wait(t, tp_sv[0])
                ss.wait(t, tp_ss[0])
                sm.bump(t.transpose(ps_tp[0:1, 0:16], zrow[:], eye16f[:]))
            ztv = sm.n

            @SE
            def _(s):
                sm.wait(s, ztv)
                ss.bump(s.copy(zrowT[:], ps_tp[0:1, 0:16]))
            tp_ss[0] = ss.n
            zrv = ss.n

            @TE
            def _(t):
                ss.wait(t, zrv)
                sv.wait(t, tp_sv[0])
                sm.bump(t.matmul(ps_tp[:, 0:16], ones_rf[:], zrowT[:],
                                 start=True, stop=True))
            zbv = sm.n

            @VE
            def _(v):
                sm.wait(v, zbv)
                for j in range(8):
                    v.tensor_copy(zc[0:64, j:j + 1],
                                  ps_tp[0:64, 2 * j:2 * j + 1])
                    ins = v.tensor_copy(zc[64:128, j:j + 1],
                                        ps_tp[64:128, 2 * j + 1:2 * j + 2])
                sv.bump(ins)
            tp_sv[0] = sv.n
            zcv = sv.n

            @VE
            def _(v):
                sv.wait(v, zcv)
                sv.bump(v.reciprocal(zc[:], zc[:]))
            zrcv = sv.n

            c_wtile_rd = [0]
            for i in range(NT):
                @TE
                def _(t):
                    sv.wait(t, tp_sv[0])
                    ss.wait(t, tp_ss[0])
                    ss.wait(t, c_wtile_rd[0])
                    if i == 0:
                        sv.wait(t, zrcv)
                    for cc in range(8):
                        ins = t.transpose(ps_tp_bf[:, cc * 128:(cc + 1) * 128],
                                          wT[:, cc, i * 128:(i + 1) * 128],
                                          eye[:])
                        if cc == 7:
                            sm.bump(ins)
                tpv = sm.n

                @VE
                def _(v):
                    sm.wait(v, tpv)
                    sv.bump(v.tensor_copy(wtile[:], ps_tp_bf[:, 0:1024]))
                tp_sv[0] = sv.n
                wtv = sv.n

                @SE
                def _(s):
                    sv.wait(s, wtv)
                    ss.bump(s.activation(w2tile[:], wtile[:], AF.Square))
                w2v = ss.n
                c_wtile_rd[0] = ss.n

                @TE
                def _(t):
                    ss.wait(t, w2v)
                    for cc in range(8):
                        ins = t.matmul(ps_c[:, cc * 16:(cc + 1) * 16],
                                       w2tile[:, cc * 128:(cc + 1) * 128],
                                       pibf[:, i, :],
                                       start=(i == 0), stop=(i == NT - 1))
                        if cc == 7:
                            sm.bump(ins)
            dmv = sm.n

            @VE
            def _(v):
                sm.wait(v, dmv)
                for j in range(8):
                    v.tensor_copy(dotsc[0:64, j:j + 1],
                                  ps_c[0:64, j * 16 + 2 * j:j * 16 + 2 * j + 1])
                    ins = v.tensor_copy(
                        dotsc[64:128, j:j + 1],
                        ps_c[64:128, j * 16 + 2 * j + 1:j * 16 + 2 * j + 2])
                sv.bump(ins)
            dcv = sv.n

            @VE
            def _(v):
                sv.wait(v, dcv)
                sv.bump(v.tensor_tensor(attnc[:], dotsc[:], zc[:], OP.mult))
            acv0 = sv.n

            @VE
            def _(v):
                sv.wait(v, acv0)
                sv.bump(v.tensor_scalar(attnc[:], attnc[:], 1.0, None,
                                        OP.add))
            acv = sv.n

            @VE
            def _(v):
                sv.wait(v, acv)
                sv.bump(v.reciprocal(attnc[:], attnc[:]))
            afv = sv.n

            pp_free = sv.n
            xguard = {}
            for tbk in range(8):
                for cc in range(8):
                    evv = pp_free

                    @TE
                    def _(t):
                        sv.wait(t, afv)
                        sv.wait(t, evv)
                        sm.bump(t.matmul(ps_c[:, 0:512],
                                         emat[0:16, cc * 128:(cc + 1) * 128],
                                         piT[:, tbk * 512:(tbk + 1) * 512],
                                         start=True, stop=True))
                    pev = sm.n

                    @VE
                    def _(v):
                        sm.wait(v, pev)
                        v.tensor_scalar(tmp1[:],
                                        wT[:, cc, tbk * 512:(tbk + 1) * 512],
                                        attnc[:, cc:cc + 1], -1.0,
                                        OP.mult, OP.mult)
                        sv.bump(v.tensor_tensor(opT[:, cc, :], tmp1[:],
                                                ps_c[:, 0:512], OP.mult))
                    pp_free = sv.n
                opv = sv.n

                for q in range(4):
                    i = tbk * 4 + q
                    buf = i % 2

                    @SY
                    def _(s):
                        g = xguard.get(i - 2)
                        if g is not None:
                            g[0].wait(s, g[1])
                        s.dma_start(xtile[:, buf, :],
                                    yb_h[i * 128:(i + 1) * 128, :]).then_inc(sx.h, 16)
                    sx.n += 16
                    xld = sx.n
                    oevv = c_out_free[0]

                    @TE
                    def _(t):
                        sv.wait(t, opv)
                        sv.wait(t, oevv)
                        for ob in range(2):
                            for cc in range(8):
                                t.matmul(ps_all[:, 4 + ob, :],
                                         opT[:, cc, q * 128:(q + 1) * 128],
                                         outwT[:, cc, ob * 512:(ob + 1) * 512],
                                         start=(cc == 0), stop=False)
                            ins = t.matmul(ps_all[:, 4 + ob, :], ones_r[:],
                                           outb_s[:, ob * 512:(ob + 1) * 512],
                                           start=False, stop=True)
                            if ob == 1:
                                sm.bump(ins)
                    mmv = sm.n
                    yv = finish_tile(i, mmv, xld, buf, ya_h, None, True,
                                     "y3", xguard)
                    c_out_free[0] = yv
            barrier()

    c_out_free = [0]
    c_sq_free = [0]
    c_sts_done = [0]
    sq_hist = []

    if "A" in phases:
        with nc.named_scope("phaseA"):
            phase_a()
    if "B1" in phases:
        with nc.named_scope("phaseB1"):
            phase_b(resid_src=ya_h, dst=yb_h, next_ln=("C" in phases), tap="y2", tag="1")
    if "C" in phases:
        with nc.named_scope("phaseC"):
            phase_c()
    if "B2" in phases:
        with nc.named_scope("phaseB2"):
            phase_b(resid_src=ya_h, dst=out_h, next_ln=False, tap=None, tag="2")

    @SY
    def _(s):
        so.wait(s, so.n)
        sg.wait(s, sg.n)
        sw.wait(s, sw.n)
        sx.wait(s, sx.n)

    es.close()
    return nc


def _consts():
    c = {}
    c["c_eye_bf"] = np.eye(128, dtype=ml_dtypes.bfloat16)
    c["c_eye_f"] = np.eye(128, dtype=np.float32)
    c["c_eye16_f"] = np.eye(16, dtype=np.float32)
    c["c_ones_1x128_bf"] = np.ones((1, 128), dtype=ml_dtypes.bfloat16)
    c["c_ones_1x128_f"] = np.ones((1, 128), dtype=np.float32)
    c["c_ones_128x1_bf"] = np.ones((128, 1), dtype=ml_dtypes.bfloat16)
    em = np.zeros((16, D), dtype=ml_dtypes.bfloat16)
    for h in range(16):
        em[h, h * 64:(h + 1) * 64] = 1
    c["c_emat_bf"] = em
    return c


def _in_map(core, inputs):
    m = {}
    m["x"] = inputs["L0_attn_a_hidden_states"][core]
    m["gamma_row"] = inputs["L0_attn_a_ln_gamma"].reshape(1, D)
    m["beta_row"] = inputs["L0_attn_a_ln_beta"].reshape(1, D)
    m["W_qkv"] = inputs["L0_attn_a_W_qkv"]
    b = inputs["L0_attn_a_b_qkv"]
    bqk = np.zeros((128, NH), np.float32)
    bv = np.zeros((64, NH), np.float32)
    for h in range(NH):
        bqk[0:64, h] = b[h * 64:(h + 1) * 64]
        bqk[64:128, h] = b[1024 + h * 64:1024 + (h + 1) * 64]
        bv[:, h] = b[2048 + h * 64:2048 + (h + 1) * 64]
    m["bqk_r"] = bqk
    m["bv_r"] = bv
    m["W_out_a"] = inputs["L0_attn_a_W_out"]
    m["b_out_a"] = inputs["L0_attn_a_b_out"].reshape(1, D)
    m["attb_qkv_w"] = inputs["L0_attn_b_qkv_w"]
    t = np.asarray(inputs["L0_attn_b_temp"]).reshape(16)
    tcm = np.zeros((128, 8), np.float32)
    for j in range(8):
        tcm[0:64, j] = t[2 * j]
        tcm[64:128, j] = t[2 * j + 1]
    m["attb_tempc"] = tcm
    m["attb_out_w"] = inputs["L0_attn_b_out_w"]
    m["attb_out_b"] = inputs["L0_attn_b_out_b"].reshape(1, D)
    m["ffn_Wq"] = inputs["L0_ffn_Wq"]
    m["ffn_keys"] = inputs["L0_ffn_keys"]
    m["ffn_W_in"] = inputs["L0_ffn_W_in"]
    m["ffn_W_out"] = inputs["L0_ffn_W_out"]
    m["ffn_in_a"] = inputs["L0_ffn_in_a"]
    m["ffn_in_b"] = inputs["L0_ffn_in_b"]
    m["ffn_out_a"] = inputs["L0_ffn_out_a"]
    m["ffn_out_b"] = inputs["L0_ffn_out_b"]
    m.update(_consts())
    return {k: np.ascontiguousarray(v) for k, v in m.items()}


def kernel(**inputs):
    nc = build_nc()
    in_maps = [_in_map(c, inputs) for c in range(8)]
    res = run_bass_kernel_spmd(nc, in_maps, core_ids=list(range(8)))
    out = np.stack([np.asarray(res.results[c]["y"]) for c in range(8)], axis=0)
    return out.astype(np.float32)


# version 06
# speedup vs baseline: 1.1500x; 1.0131x over previous
import numpy as np
import ml_dtypes
from contextlib import ExitStack

import concourse.bass as bass
import concourse.mybir as mybir
from concourse.bass_utils import run_bass_kernel_spmd
from concourse.alu_op_type import AluOpType as OP

F32 = mybir.dt.float32
BF16 = mybir.dt.bfloat16
U32 = mybir.dt.uint32
AF = mybir.ActivationFunctionType

S, D, NT = 4096, 1024, 32
NH = 16
PKH, DI, NE = 8, 2048, 16384
EPS = 1e-5


class Ctr:
    def __init__(self, es, nc, name):
        self.h = es.enter_context(nc.semaphore(name))
        self.n = 0
        self.maxw = 0
        self.seen = {}

    def bump(self, ins, k=1):
        ins.then_inc(self.h, k)
        self.n += k
        return self.n

    def wait(self, eng, v=None):
        v = self.n if v is None else v
        if v <= 0:
            return
        key = id(eng)
        if self.seen.get(key, -1) >= v:
            return
        self.seen[key] = v
        self.maxw = max(self.maxw, v)
        eng.wait_ge(self.h, v)


def build_nc(phases=("A", "B1", "C", "B2"), taps=()):
    nc = bass.Bass("TRN2", target_bir_lowering=False)
    es = ExitStack()
    tap_set = set(taps)

    dt = nc.dram_tensor
    x_h = dt("x", [S, D], F32, kind="ExternalInput").ap()
    gamma_row_h = dt("gamma_row", [1, D], F32, kind="ExternalInput").ap()
    beta_row_h = dt("beta_row", [1, D], F32, kind="ExternalInput").ap()
    wqkv_h = dt("W_qkv", [3 * D, D], F32, kind="ExternalInput").ap()
    bqk_h = dt("bqk_r", [128, NH], F32, kind="ExternalInput").ap()
    bv_h = dt("bv_r", [64, NH], F32, kind="ExternalInput").ap()
    wouta_h = dt("W_out_a", [D, D], F32, kind="ExternalInput").ap()
    bouta_h = dt("b_out_a", [1, D], F32, kind="ExternalInput").ap()
    qkvw_h = dt("attb_qkv_w", [D, D], F32, kind="ExternalInput").ap()
    tempc_h = dt("attb_tempc", [128, 8], F32, kind="ExternalInput").ap()
    outw_h = dt("attb_out_w", [D, D], F32, kind="ExternalInput").ap()
    outb_h = dt("attb_out_b", [1, D], F32, kind="ExternalInput").ap()
    wq_h = dt("ffn_Wq", [8 * D, D], F32, kind="ExternalInput").ap()
    keys_h = dt("ffn_keys", [PKH, 128, 2, 512], F32, kind="ExternalInput").ap()
    win_h = dt("ffn_W_in", [DI, D], F32, kind="ExternalInput").ap()
    wout_h = dt("ffn_W_out", [D, DI], F32, kind="ExternalInput").ap()
    tbl = {"ina": dt("ffn_in_a", [NE, D], F32, kind="ExternalInput").ap(),
           "inb": dt("ffn_in_b", [NE, DI], F32, kind="ExternalInput").ap(),
           "outa": dt("ffn_out_a", [NE, DI], F32, kind="ExternalInput").ap(),
           "outb": dt("ffn_out_b", [NE, D], F32, kind="ExternalInput").ap()}
    eye_h = dt("c_eye_bf", [128, 128], BF16, kind="ExternalInput").ap()
    eyef_h = dt("c_eye_f", [128, 128], F32, kind="ExternalInput").ap()
    ones_r_h = dt("c_ones_1x128_bf", [1, 128], BF16, kind="ExternalInput").ap()
    ones_rf_h = dt("c_ones_1x128_f", [1, 128], F32, kind="ExternalInput").ap()
    ones_c_h = dt("c_ones_128x1_bf", [128, 1], BF16, kind="ExternalInput").ap()
    emat_h = dt("c_emat_bf", [16, D], BF16, kind="ExternalInput").ap()
    eye16f_h = dt("c_eye16_f", [16, 16], F32, kind="ExternalInput").ap()

    out_h = dt("y", [S, D], F32, kind="ExternalOutput").ap()
    ya_h = dt("ya", [S, D], F32, kind="Internal").ap()
    yb_h = dt("yb", [S, D], F32, kind="Internal").ap()
    ctx_h = dt("ctxs", [128, 8, S], BF16,
               kind="ExternalOutput" if "ctx" in tap_set else "Internal").ap()
    tap_hbm = {t: dt(f"tap_{t}", [S, D], F32, kind="ExternalOutput").ap()
               for t in tap_set if t.startswith("y")}
    if "xn" in tap_set:
        tap_hbm["xn"] = dt("tap_xn", [128, 8, S], F32, kind="ExternalOutput").ap()
    if "idx" in tap_set:
        tap_hbm["idx"] = dt("tap_idx", [128, PKH, NT], U32, kind="ExternalOutput").ap()
        tap_hbm["gate"] = dt("tap_gate", [128, PKH, NT], F32, kind="ExternalOutput").ap()
    if "qkv0" in tap_set:
        tap_hbm["q0"] = dt("tap_q0", [64, S], F32, kind="ExternalOutput").ap()
        tap_hbm["k0"] = dt("tap_k0", [64, S], F32, kind="ExternalOutput").ap()
        tap_hbm["v0"] = dt("tap_v0", [64, S], F32, kind="ExternalOutput").ap()

    ec = es.enter_context
    st = nc.sbuf_tensor

    eye = ec(st("eye", [128, 128], BF16))
    eyef = ec(st("eyef", [128, 128], F32))
    eye16f = ec(st("eye16f", [16, 16], F32))
    ones_r = ec(st("ones_r", [1, 128], BF16))
    ones_rf = ec(st("ones_rf", [1, 128], F32))
    ones_c = ec(st("ones_c", [128, 1], BF16))
    emat = ec(st("emat", [16, D], BF16))
    xT = ec(st("xT", [128, 8, S], BF16))
    rsc = ec(st("rsc", [128, NT], F32))
    rbi = ec(st("rbi", [128, NT], F32))
    st6 = ec(st("st6", [128, 12], F32))
    aggr = ec(st("aggr", [128, 2], F32))
    ta = ec(st("ta", [128, 1], F32))
    tb = ec(st("tb", [128, 1], F32))
    tc = ec(st("tc", [128, 1], F32))
    xtile = ec(st("xtile", [128, 2, D], F32))
    ytile = ec(st("ytile", [128, D], F32))
    ntile = ec(st("ntile", [128, D], BF16))
    wstage = ec(st("wstage", [128, 2, D], BF16))

    ps_all = ec(nc.psum_tensor("ps_all", [128, 6, 512], F32))
    ps_c = ec(nc.psum_tensor("ps_c", [128, 512], F32))
    ps_tp = ec(nc.psum_tensor("ps_tp", [128, 512], F32))
    ps_tp_bf = ps_tp.bitcast(BF16)

    sx = Ctr(es, nc, "sx")
    sw = Ctr(es, nc, "sw")
    sg = Ctr(es, nc, "sg")
    so = Ctr(es, nc, "so")
    sm = Ctr(es, nc, "sm")
    ss = Ctr(es, nc, "ss")
    sv = Ctr(es, nc, "sv")
    scn = Ctr(es, nc, "scn")
    sco = Ctr(es, nc, "sco")

    block = es.enter_context(nc.Block())
    TE, VE, SE, SY, GE = (block.tensor, block.vector, block.scalar,
                          block.sync, block.gpsimd)

    def barrier():
        snap = [(c, c.n) for c in (sx, sw, sg, so, sm, ss, sv, scn, sco)]
        for sec in (TE, VE, SE, SY, GE):
            @sec
            def _(e):
                for c, v in snap:
                    c.wait(e, v)

    ytg = {"ss": 0, "so": 0}
    _scopes = {}
    tp_ss = [0]
    tp_sv = [0]
    wst_rd = [0, 0]

    def load_T(src_h, rows, cols, write_evac):
        ncc = cols // 128
        for j in range(rows // 128):
            buf = j % 2

            @GE
            def _(g):
                sm.wait(g, wst_rd[buf])
                g.dma_start(wstage[:, buf, :cols],
                            src_h[j * 128:(j + 1) * 128, :]).then_inc(sw.h, 16)
            sw.n += 16
            ldv = sw.n

            @TE
            def _(t):
                sw.wait(t, ldv)
                sv.wait(t, tp_sv[0])
                ss.wait(t, tp_ss[0])
                for cc in range(ncc):
                    ins = t.transpose(ps_tp_bf[:, cc * 128:(cc + 1) * 128],
                                      wstage[:, buf, cc * 128:(cc + 1) * 128],
                                      eye[:])
                    if cc == ncc - 1:
                        sm.bump(ins)
            wst_rd[buf] = sm.n
            write_evac(j, sm.n)
            tp_sv[0] = sv.n

    def simple_evacT(dstT):
        def f(j, tpv):
            @VE
            def _(v):
                sm.wait(v, tpv)
                sv.bump(v.tensor_copy(
                    dstT[:, :, j * 128:(j + 1) * 128],
                    ps_tp_bf.rearrange("p (c t) -> p c t", c=8)))
        return f

    def ln_chain(xin, out_bf, gb=None, z_out=None, store_affine=None,
                 wait_before=None):
        @VE
        def _(v):
            for c, val in (wait_before or []):
                c.wait(v, val)
            v.bn_stats(st6[:, 0:6], xin[:, 0:512])
            sv.bump(v.bn_stats(st6[:, 6:12], xin[:, 512:1024]))
        vst = sv.n

        @VE
        def _(v):
            sv.wait(v, vst)
            sv.bump(v.bn_aggr(aggr[:], st6[:]))
        vag = sv.n

        @VE
        def _(v):
            sv.wait(v, vag)
            sv.bump(v.tensor_scalar(ta[:], aggr[:, 1:2], EPS, None, OP.add))
        vta = sv.n

        @SE
        def _(s):
            sv.wait(s, vta)
            ss.bump(s.activation(tb[:], ta[:], AF.Ln))
        sln = ss.n

        @SE
        def _(s):
            ss.wait(s, sln)
            ss.bump(s.activation(tb[:], tb[:], AF.Exp, scale=-0.5))
        srs = ss.n

        @VE
        def _(v):
            ss.wait(v, srs)
            sv.bump(v.tensor_scalar(tc[:], aggr[:, 0:1], tb[:], -1.0,
                                    OP.mult, OP.mult))
        vtc = sv.n

        if store_affine is not None:
            sca, bia = store_affine

            @VE
            def _(v):
                sv.wait(v, vtc)
                v.tensor_scalar(sca, tb[:], 1.0, None, OP.add)
                sv.bump(v.tensor_scalar(bia, tc[:], 1.0, None, OP.mult))

        if gb is not None:
            gbc, bbc, zn0, z2 = gb
            tgt = zn0
        else:
            tgt = z_out if z_out is not None else out_bf

        @SE
        def _(s):
            sv.wait(s, vtc)
            ss.bump(s.activation(tgt[:], xin[:], AF.Identity,
                                 scale=tb[:], bias=tc[:]))
        szn = ss.n

        if gb is not None:
            @VE
            def _(v):
                ss.wait(v, szn)
                v.scalar_tensor_tensor(z2[:], zn0[:], 1.0, gbc[:],
                                       OP.mult, OP.mult)
                sv.bump(v.tensor_tensor(out_bf[:], z2[:], bbc[:], OP.add))
        return szn, ss.n, sv.n

    def tile_to_xT(i, src_bf, src_ctr, src_v):
        @TE
        def _(t):
            src_ctr.wait(t, src_v)
            sv.wait(t, tp_sv[0])
            ss.wait(t, tp_ss[0])
            for cc in range(8):
                ins = t.transpose(ps_tp_bf[:, cc * 128:(cc + 1) * 128],
                                  src_bf[:, cc * 128:(cc + 1) * 128], eye[:])
                if cc == 7:
                    sm.bump(ins)
        tpv = sm.n

        @VE
        def _(v):
            sm.wait(v, tpv)
            sv.bump(v.tensor_copy(
                xT[:, :, i * 128:(i + 1) * 128],
                ps_tp_bf.rearrange("p (c t) -> p c t", c=8)))
        tp_sv[0] = sv.n
        return sv.n

    def finish_tile(i, mmv, xld, buf, dst, affine_z, next_ln, tap, xguard):
        if affine_z is not None:
            @SE
            def _(s):
                sx.wait(s, xld)
                ss.bump(s.activation(affine_z[:], xtile[:, buf, :],
                                     AF.Identity, scale=rsc[:, i:i + 1],
                                     bias=rbi[:, i:i + 1]))
            rv = (ss, ss.n)
            xguard[i] = (ss, ss.n)
        else:
            rv = (sx, xld)

        yss, yso = ytg["ss"], ytg["so"]

        @VE
        def _(v):
            sm.wait(v, mmv)
            rv[0].wait(v, rv[1])
            ss.wait(v, yss)
            so.wait(v, yso)
            src = affine_z[:] if affine_z is not None else xtile[:, buf, :]
            sv.bump(v.tensor_tensor(
                ytile[:], src,
                ps_all[:, 4:6, :].rearrange("p a b -> p (a b)"), OP.add))
        yv = sv.n
        if affine_z is None:
            xguard[i] = (sv, yv)

        @SY
        def _(s):
            sv.wait(s, yv)
            s.dma_start(dst[i * 128:(i + 1) * 128, :],
                        ytile[:]).then_inc(so.h, 16)
            if tap is not None and tap in tap_hbm:
                s.dma_start(tap_hbm[tap][i * 128:(i + 1) * 128, :],
                            ytile[:]).then_inc(so.h, 16)
        so.n += 16 * (2 if (tap in tap_hbm) else 1)
        ytg["so"] = so.n

        if next_ln:
            szn, sdv, vdv = ln_chain(ytile, ntile, wait_before=[(sv, yv)])
            tile_to_xT(i, ntile, ss, sdv)
            ytg["ss"] = szn
        return yv

    @SY
    def _(s):
        for dsts, srcs in ((eye, eye_h), (eyef, eyef_h), (eye16f, eye16f_h),
                           (ones_r, ones_r_h), (ones_rf, ones_rf_h),
                           (ones_c, ones_c_h), (emat, emat_h)):
            s.dma_start(dsts[:], srcs).then_inc(sx.h, 16)
    sx.n += 7 * 16
    barrier()

    def phase_a():
        with ExitStack() as phx:
            pec = phx.enter_context
            gr_row = pec(st("a_gr", [1, D], F32))
            br_row = pec(st("a_br", [1, D], F32))
            gbc = pec(st("a_gbc", [128, D], F32))
            bbc = pec(st("a_bbc", [128, D], F32))
            ztile = pec(st("a_ztile", [128, D], F32))
            zn0 = pec(st("a_zn0", [128, D], F32))
            z2 = pec(st("a_z2", [128, D], F32))
            wqkvT = pec(st("a_wqkvT", [128, 8, 768], BF16))
            bqk_s = pec(st("a_bqk", [128, NH], F32))
            bv_s = pec(st("a_bv", [64, NH], F32))
            woutT = pec(st("a_woutT", [128, 8, D], BF16))
            bouta_s = pec(st("a_bouta", [1, D], BF16))
            qT = pec(st("a_qT", [64, S], BF16))
            kT = pec(st("a_kT", [64, S], BF16))
            vT = pec(st("a_vT", [64, S], BF16))
            vb = pec(st("a_vb", [128, NT, 66], BF16))
            pT = pec(st("a_pT", [128, 6, 512], BF16))
            recb = pec(st("a_recb", [1, 512], BF16))
            recf = pec(st("a_recf", [64, 512], F32))
            cstg = pec(st("a_cstg", [64, 512], BF16))
            ctile = pec(st("a_ctile", [128, 2, 8, 128], BF16))

            @SY
            def _(s):
                s.dma_start(bqk_s[:], bqk_h).then_inc(sx.h, 16)
                s.dma_start(bv_s[:], bv_h).then_inc(sx.h, 16)
                s.dma_start(gr_row[:], gamma_row_h).then_inc(sx.h, 16)
                s.dma_start(br_row[:], beta_row_h).then_inc(sx.h, 16)
            sx.n += 64
            gbrow = sx.n

            @GE
            def _(g):
                g.dma_start(bouta_s[:], bouta_h).then_inc(sw.h, 16)
            sw.n += 16

            @TE
            def _(t):
                sx.wait(t, gbrow)
                for half in range(2):
                    t.matmul(ps_all[:, 2 * half, :], ones_rf[:],
                             gr_row[:, half * 512:(half + 1) * 512],
                             start=True, stop=True)
                    ins = t.matmul(ps_all[:, 2 * half + 1, :], ones_rf[:],
                                   br_row[:, half * 512:(half + 1) * 512],
                                   start=True, stop=True)
                    if half == 1:
                        sm.bump(ins)
            gmmv = sm.n

            @VE
            def _(v):
                sm.wait(v, gmmv)
                v.tensor_copy(gbc[:, 0:512], ps_all[:, 0, :])
                v.tensor_copy(bbc[:, 0:512], ps_all[:, 1, :])
                v.tensor_copy(gbc[:, 512:1024], ps_all[:, 2, :])
                sv.bump(v.tensor_copy(bbc[:, 512:1024], ps_all[:, 3, :]))
            gb_done = sv.n

            load_T(wouta_h, D, D, simple_evacT(woutT))

            @VE
            def _(v):
                v.memset(vb[:, :, 64:66], 0.0)
                sv.bump(v.memset(vb[:, :, 64:65], 1.0))
            vb_ones = sv.n
            barrier()

            _scopes["A1"] = nc.enter_named_scope("A1", False)[0]
            zdone = {}
            for i in range(NT):
                buf = i % 2

                @SY
                def _(s):
                    if i >= 2:
                        ss.wait(s, zdone[i - 2])
                    s.dma_start(xtile[:, buf, :],
                                x_h[i * 128:(i + 1) * 128, :]).then_inc(sx.h, 16)
                sx.n += 16
                xld = sx.n
                szn, _s, _v = ln_chain(
                    xtile[:, buf, :], None, z_out=ztile,
                    store_affine=(rsc[:, i:i + 1], rbi[:, i:i + 1]),
                    wait_before=[(sx, xld)])
                zdone[i] = szn
                szn2, sdv, vdv = ln_chain(ztile, ntile,
                                          gb=(gbc, bbc, zn0, z2),
                                          wait_before=[(ss, szn)])
                tile_to_xT(i, ntile, sv, vdv)
            barrier()
            if "xn" in tap_set:
                @GE
                def _(g):
                    g.dma_start(tap_hbm["xn"], xT[:]).then_inc(sw.h, 16)
                sw.n += 16
                barrier()
            nc.leave_named_scope("A1", _scopes["A1"], False)
            _scopes["A2"] = nc.enter_named_scope("A2", False)[0]

            for quarter in range(4):
                for band in range(3):
                    row0 = band * 1024 + quarter * 256

                    def wq_evac(j, tpv, band=band):
                        @VE
                        def _(v):
                            sm.wait(v, tpv)
                            sv.bump(v.tensor_copy(
                                wqkvT[:, :, band * 256 + j * 128:
                                      band * 256 + (j + 1) * 128],
                                ps_tp_bf.rearrange("p (c t) -> p c t", c=8)))
                    load_T(wqkv_h[row0:row0 + 256, :], 256, D, wq_evac)
                wq_done = sv.n
                for hh in range(4):
                    h = quarter * 4 + hh
                    a2_head(h, hh, wqkvT, bqk_s, bv_s, qT, kT, vT, vb, pT,
                            recb, recf, cstg, wq_done, vb_ones)
            barrier()
            nc.leave_named_scope("A2", _scopes["A2"], False)
            _scopes["A3"] = nc.enter_named_scope("A3", False)[0]

            xguard = {}
            cld = {}
            for i in range(NT):
                buf = i % 2
                cbuf = i % 2
                prev = a3_ctile_used.get(i - 2, 0)

                @SY
                def _(s):
                    sm.wait(s, prev)
                    s.dma_start(ctile[:, cbuf, :, :],
                                ctx_h[:, :, i * 128:(i + 1) * 128]).then_inc(sx.h, 16)
                sx.n += 16
                cldv = sx.n

                @SY
                def _(s):
                    g = xguard.get(i - 2)
                    if g is not None:
                        g[0].wait(s, g[1])
                    s.dma_start(xtile[:, buf, :],
                                x_h[i * 128:(i + 1) * 128, :]).then_inc(sx.h, 16)
                sx.n += 16
                xld = sx.n

                evv = a3_psfree[0]

                @TE
                def _(t):
                    sv.wait(t, evv)
                    sx.wait(t, cldv)
                    for ob in range(2):
                        for cc in range(8):
                            t.matmul(ps_all[:, 4 + ob, :],
                                     ctile[:, cbuf, cc, :],
                                     woutT[:, cc, ob * 512:(ob + 1) * 512],
                                     start=(cc == 0), stop=False)
                        ins = t.matmul(ps_all[:, 4 + ob, :], ones_r[:],
                                       bouta_s[:, ob * 512:(ob + 1) * 512],
                                       start=False, stop=True)
                        if ob == 1:
                            sm.bump(ins)
                mmv = sm.n
                a3_ctile_used[i] = mmv
                yv = finish_tile(i, mmv, xld, buf, ya_h, ztile, True, "y1",
                                 xguard)
                a3_psfree[0] = yv
            barrier()
            nc.leave_named_scope("A3", _scopes["A3"], False)

    a3_ctile_used = {}
    a3_psfree = [0]

    def a2_head(h, hh, wqkvT, bqk_s, bv_s, qT, kT, vT, vb, pT, recb, recf,
                cstg, wq_done, vb_ones):
        w3 = wqkvT.rearrange("p c (b o) -> p c b o", b=3)
        for tbk in range(8):
            bk = (tbk % 2) * 2
            evv = sv.n

            @TE
            def _(t):
                sv.wait(t, wq_done)
                sv.wait(t, evv)
                for cc in range(8):
                    t.matmul(ps_all[0:64, bk, :],
                             w3[:, cc, 0, hh * 64:(hh + 1) * 64],
                             xT[:, cc, tbk * 512:(tbk + 1) * 512],
                             start=(cc == 0), stop=(cc == 7))
                for cc in range(8):
                    t.matmul(ps_all[64:128, bk, :],
                             w3[:, cc, 1, hh * 64:(hh + 1) * 64],
                             xT[:, cc, tbk * 512:(tbk + 1) * 512],
                             start=(cc == 0), stop=(cc == 7),
                             tile_position=(0, 64))
                for cc in range(8):
                    ins = t.matmul(ps_all[0:64, bk + 1, :],
                                   w3[:, cc, 2, hh * 64:(hh + 1) * 64],
                                   xT[:, cc, tbk * 512:(tbk + 1) * 512],
                                   start=(cc == 0), stop=(cc == 7))
                    if cc == 7:
                        sm.bump(ins)
            mmv = sm.n

            @VE
            def _(v):
                sm.wait(v, mmv)
                sl = slice(tbk * 512, (tbk + 1) * 512)
                v.tensor_scalar(qT[:, sl], ps_all[0:64, bk, :],
                                bqk_s[0:64, h:h + 1], None, OP.add)
                v.tensor_scalar(kT[:, sl], ps_all[64:128, bk, :],
                                bqk_s[64:128, h:h + 1], None, OP.add)
                sv.bump(v.tensor_scalar(vT[:, sl], ps_all[0:64, bk + 1, :],
                                        bv_s[:, h:h + 1], None, OP.add))
        vT_done = sv.n
        if h == 0 and "q0" in tap_hbm:
            @GE
            def _(g):
                ss.wait(g, vT_done)
                g.dma_start(tap_hbm["q0"], qT[:]).then_inc(sw.h, 16)
                g.dma_start(tap_hbm["k0"], kT[:]).then_inc(sw.h, 16)
                g.dma_start(tap_hbm["v0"], vT[:]).then_inc(sw.h, 16)
            sw.n += 48
            barrier()

        for grp in range(4):
            @TE
            def _(t):
                sv.wait(t, vT_done)
                ss.wait(t, tp_ss[0])
                sv.wait(t, tp_sv[0])
                sv.wait(t, vb_ones)
                for k in range(8):
                    kc = grp * 8 + k
                    ins = t.transpose(ps_tp_bf[:, k * 64:(k + 1) * 64],
                                      vT[:, kc * 128:(kc + 1) * 128],
                                      eye[0:64, 0:64])
                    if k == 7:
                        sm.bump(ins)
            tpv = sm.n

            @VE
            def _(v):
                sm.wait(v, tpv)
                sv.bump(v.tensor_copy(
                    vb[:, grp * 8:(grp + 1) * 8, 0:64],
                    ps_tp_bf[:, 0:512].rearrange("p (c t) -> p c t", c=8)))
            tp_sv[0] = sv.n
        vb_done = sv.n

        groups = [list(range(g * 3, min(32, g * 3 + 3))) for g in range(11)]
        ng = len(groups)
        for qb in range(8):
            qsl = slice(qb * 512, (qb + 1) * 512)
            edone = {}
            cdone = {}
            sdone = {}

            def emit_s(g):
                chunks = groups[g]
                bs = (g % 2) * 3

                @TE
                def _(t):
                    if g >= 2:
                        ss.wait(t, edone[g - 2])
                    sv.wait(t, vb_done)
                    for idx, kc in enumerate(chunks):
                        ins = t.matmul(ps_all[:, bs + idx, :],
                                       kT[:, kc * 128:(kc + 1) * 128],
                                       qT[:, qsl], start=True, stop=True)
                        if idx == len(chunks) - 1:
                            sm.bump(ins)
                sdone[g] = sm.n

            def emit_e(g):
                chunks = groups[g]
                bs = (g % 2) * 3

                @SE
                def _(s):
                    sm.wait(s, sdone[g])
                    if g >= 2:
                        sm.wait(s, cdone[g - 2])
                    ss.bump(s.activation(pT[:, bs:bs + len(chunks), :],
                                         ps_all[:, bs:bs + len(chunks), :],
                                         AF.Exp, scale=0.125))
                edone[g] = ss.n

            def emit_c(g):
                chunks = groups[g]
                bs = (g % 2) * 3

                @TE
                def _(t):
                    ss.wait(t, edone[g])
                    if g == 0:
                        sv.wait(t, a2_ctx_free[0])
                    for idx, kc in enumerate(chunks):
                        ins = t.matmul(ps_c[0:65, :],
                                       vb[:, kc, 0:65],
                                       pT[:, bs + idx, :],
                                       start=(kc == 0), stop=(kc == 31))
                        if idx == len(chunks) - 1:
                            sm.bump(ins)
                cdone[g] = sm.n

            emit_s(0)
            emit_e(0)
            for g in range(1, ng):
                emit_s(g)
                emit_e(g)
                emit_c(g - 1)
            emit_c(ng - 1)
            ctx_done = sm.n

            @VE
            def _(v):
                sm.wait(v, ctx_done)
                with nc.allow_low_precision(reason="bf16 softmax recip bcast"):
                    sv.bump(v.reciprocal(recb[:], ps_c[64:65, :]))
            rcv = sv.n

            @TE
            def _(t):
                sv.wait(t, rcv)
                ss.wait(t, tp_ss[0])
                sv.wait(t, tp_sv[0])
                sm.bump(t.matmul(ps_tp[0:64, :], ones_r[0:1, 0:64], recb[:],
                                 start=True, stop=True))
            bcv = sm.n

            @VE
            def _(v):
                sm.wait(v, bcv)
                sv.wait(v, a2_ctx_free[0])
                sv.bump(v.tensor_copy(recf[:], ps_tp[0:64, :]))
            tp_sv[0] = sv.n
            rfv = sv.n

            @VE
            def _(v):
                sv.wait(v, rfv)
                so.wait(v, a2_cstg_free[0])
                sv.bump(v.tensor_tensor(cstg[:], ps_c[0:64, :], recf[:],
                                        OP.mult))
            csv = sv.n
            a2_ctx_free[0] = csv

            @SY
            def _(s):
                sv.wait(s, csv)
                s.dma_start(ctx_h[(h % 2) * 64:(h % 2) * 64 + 64, h // 2, qsl],
                            cstg[:]).then_inc(so.h, 16)
            so.n += 16
            a2_cstg_free[0] = so.n

    a2_ctx_free = [0]
    a2_cstg_free = [0]
    a2_recf_free = [0]

    def phase_b(resid_src, dst, next_ln, tap, tag):
        with ExitStack() as phx:
            pec = phx.enter_context
            winT = pec(st("b" + tag + "_winT", [128, 8, DI], BF16))
            woutfT = pec(st("b" + tag + "_woutfT", [128, 16, D], BF16))
            sxv = pec(st("b" + tag + "_sx", [128, PKH, NT], F32))
            ixv = pec(st("b" + tag + "_ix", [128, PKH, NT], U32))
            idxv = pec(st("b" + tag + "_idx", [128, PKH, NT], U32))
            gatev = pec(st("b" + tag + "_gate", [128, PKH, NT], F32))
            g_ina = pec(st("b" + tag + "_gina", [128, D], F32))
            g_inb = pec(st("b" + tag + "_ginb", [128, 2, DI], F32))
            g_outa = pec(st("b" + tag + "_gouta", [128, DI], F32))
            g_outb = pec(st("b" + tag + "_goutb", [128, D], F32))
            xftile = pec(st("b" + tag + "_xf", [128, D], BF16))
            linb = pec(st("b" + tag + "_lin", [128, PKH], F32))
            loutb = pec(st("b" + tag + "_lout", [128, PKH], F32))
            b1_scope = ExitStack()
            b1c = b1_scope.enter_context
            wqT = b1c(st("b" + tag + "_wqT", [128, 8, 512], BF16))
            qTb = b1c(st("b" + tag + "_qT", [128, 4, 512], F32))
            keysT = b1c(st("b" + tag + "_keysT", [128, 4, 128], F32))
            simt = b1c(st("b" + tag + "_simt", [128, 128], F32))
            mv8 = b1c(st("b" + tag + "_mv8", [128, 8], F32))
            mi8 = b1c(st("b" + tag + "_mi8", [128, 8], U32))

            load_T(win_h, DI, D, simple_evacT(winT))

            def woutf_evac(base):
                def f(j, tpv):
                    @VE
                    def _(v):
                        sm.wait(v, tpv)
                        sv.bump(v.tensor_copy(
                            woutfT[:, base:base + 8, j * 128:(j + 1) * 128],
                            ps_tp_bf.rearrange("p (c t) -> p c t", c=8)))
                return f
            load_T(wout_h[:, 0:1024], D, 1024, woutf_evac(0))
            load_T(wout_h[:, 1024:2048], D, 1024, woutf_evac(8))
            barrier()

            _scopes["B1q" + tag] = nc.enter_named_scope("B1q" + tag, False)[0]
            bank_free = [sv.n] * 4
            sim_free = sv.n
            key_free = sm.n
            for ph_i in range(16):
                p_, h_ = ph_i // 8, ph_i % 8
                row0 = (p_ * 8 + h_) * 512
                load_T(wq_h[row0:row0 + 512, :], 512, D, simple_evacT(wqT))
                wqv = sv.n

                kstage = qTb[:, 0, :]

                @SY
                def _(s):
                    sm.wait(s, key_free)
                    sv.wait(s, b1_qtb_writes[0])
                    s.dma_start(kstage,
                                keys_h[h_, :, p_, :]).then_inc(sx.h, 16)
                sx.n += 16
                kldv = sx.n

                @TE
                def _(t):
                    sx.wait(t, kldv)
                    sv.wait(t, tp_sv[0])
                    ss.wait(t, tp_ss[0])
                    for m in range(4):
                        ins = t.transpose(ps_tp[:, m * 128:(m + 1) * 128],
                                          kstage[:, m * 128:(m + 1) * 128],
                                          eyef[:])
                        if m == 3:
                            sm.bump(ins)
                ktp = sm.n

                @VE
                def _(v):
                    sm.wait(v, ktp)
                    sv.bump(v.tensor_copy(
                        keysT[:, :, :],
                        ps_tp[:, 0:512].rearrange("p (c t) -> p c t", c=4)))
                tp_sv[0] = sv.n
                kv = sv.n

                for tbk in range(8):
                    for m in range(4):
                        bk = m

                        @TE
                        def _(t):
                            sv.wait(t, wqv)
                            sv.wait(t, bank_free[bk])
                            for cc in range(8):
                                ins = t.matmul(
                                    ps_all[:, bk, :],
                                    wqT[:, cc, m * 128:(m + 1) * 128],
                                    xT[:, cc, tbk * 512:(tbk + 1) * 512],
                                    start=(cc == 0), stop=(cc == 7))
                                if cc == 7:
                                    sm.bump(ins)
                        mmv = sm.n

                        @VE
                        def _(v):
                            sm.wait(v, mmv)
                            sm.wait(v, qtb_free[0])
                            sm.wait(v, ktp)
                            sv.bump(v.tensor_copy(qTb[:, m, :],
                                                  ps_all[:, bk, :]))
                        bank_free[bk] = sv.n
                        b1_qtb_writes[0] = sv.n
                    qdv = ss.n

                    for q in range(4):
                        i = tbk * 4 + q

                        @TE
                        def _(t):
                            ss.wait(t, qdv)
                            sv.wait(t, sim_free)
                            sv.wait(t, kv)
                            for m in range(4):
                                ins = t.matmul(ps_c[:, 0:128],
                                               qTb[:, m, q * 128:(q + 1) * 128],
                                               keysT[:, m, :],
                                               start=(m == 0), stop=(m == 3))
                                if m == 3:
                                    sm.bump(ins)
                        smv = sm.n
                        if q == 3:
                            qtb_free[0] = smv
                        key_free = smv

                        @VE
                        def _(v):
                            sm.wait(v, smv)
                            sv.bump(v.tensor_copy(simt[:], ps_c[:, 0:128]))
                        sim_free = sv.n
                        cpv = sv.n

                        @VE
                        def _(v):
                            sv.wait(v, cpv)
                            sv.bump(v.max(mv8[:], simt[:]))
                        mxv = sv.n

                        @VE
                        def _(v):
                            sv.wait(v, mxv)
                            sv.bump(v.max_index(mi8[:], mv8[:], simt[:]))
                        miv = sv.n

                        if p_ == 0:
                            @VE
                            def _(v):
                                sv.wait(v, miv)
                                v.tensor_copy(sxv[:, h_, i:i + 1], mv8[:, 0:1])
                                sv.bump(v.tensor_copy(ixv[:, h_, i:i + 1],
                                                      mi8[:, 0:1]))
                        else:
                            @VE
                            def _(v):
                                sv.wait(v, miv)
                                v.scalar_tensor_tensor(
                                    idxv[:, h_, i:i + 1], ixv[:, h_, i:i + 1],
                                    128.0, mi8[:, 0:1], OP.mult, OP.add)
                                sv.bump(v.tensor_scalar(
                                    gatev[:, h_, i:i + 1],
                                    sxv[:, h_, i:i + 1], mv8[:, 0:1], 0.0,
                                    OP.add, OP.max))
            nc.leave_named_scope("B1q" + tag, _scopes["B1q" + tag], False)
            barrier()
            if "idx" in tap_set and tag == "1":
                @GE
                def _(g):
                    g.dma_start(tap_hbm["idx"], idxv[:]).then_inc(sw.h, 16)
                    g.dma_start(tap_hbm["gate"], gatev[:]).then_inc(sw.h, 16)
                sw.n += 32
                barrier()
            b1_scope.close()
            _scopes["B3_" + tag] = nc.enter_named_scope("B3_" + tag, False)[0]
            prod = pec(st("b" + tag + "_prod", [128, D], BF16))
            hidbf = pec(st("b" + tag + "_hidbf", [128, DI], BF16))
            hT = pec(st("b" + tag + "_hT", [128, 16, 128], BF16))
            hacc = pec(st("b" + tag + "_hacc", [128, DI], F32))
            lp = pec(st("b" + tag + "_lp", [128, PKH], F32))
            lq = pec(st("b" + tag + "_lq", [128, PKH], F32))

            scn0 = scn.n
            sco0 = sco.n
            sg0 = sg.n
            sg_in_base = {}
            sg_out_base = {}
            sgc = [sg.n]
            GQD = 4

            def emit_in(i):
                sg_in_base[i] = sgc[0]

                @GE
                def _(g):
                    for h_ in range(PKH):
                        ghc = i * PKH + h_
                        if ghc >= 1:
                            scn.wait(g, scn0 + max(0, 2 * ghc - 1))
                        sg.wait(g, sg.n - (GQD - 1) * 16)
                        g.indirect_dma_start(
                            g_ina[:, :], None, tbl["ina"],
                            bass.IndirectOffsetOnAxis(idxv[:, h_, i:i + 1], 0)
                        ).then_inc(sg.h, 16)
                        sg.n += 16
                        if ghc >= 2:
                            scn.wait(g, scn0 + 2 * ghc - 2)
                        sg.wait(g, sg.n - (GQD - 1) * 16)
                        g.indirect_dma_start(
                            g_inb[:, ghc % 2, :], None, tbl["inb"],
                            bass.IndirectOffsetOnAxis(idxv[:, h_, i:i + 1], 0)
                        ).then_inc(sg.h, 16)
                        sg.n += 16
                sgc[0] = sg.n

            def emit_out(i):
                sg_out_base[i] = sgc[0]

                @GE
                def _(g):
                    for h_ in range(PKH):
                        ghc = i * PKH + h_
                        if ghc >= 1:
                            sco.wait(g, sco0 + 2 * ghc)
                        sg.wait(g, sg.n - (GQD - 1) * 16)
                        g.indirect_dma_start(
                            g_outa[:, :], None, tbl["outa"],
                            bass.IndirectOffsetOnAxis(idxv[:, h_, i:i + 1], 0)
                        ).then_inc(sg.h, 16)
                        sg.n += 16
                        sg.wait(g, sg.n - (GQD - 1) * 16)
                        g.indirect_dma_start(
                            g_outb[:, :], None, tbl["outb"],
                            bass.IndirectOffsetOnAxis(idxv[:, h_, i:i + 1], 0)
                        ).then_inc(sg.h, 16)
                        sg.n += 16
                sgc[0] = sg.n

            for i in range(NT):
                emit_in(i)
                emit_out(i)

            def gwait(ghc, k):
                i, h_ = divmod(ghc, PKH)
                if k < 2:
                    return sg_in_base[i] + (2 * h_ + k + 1) * 16
                return sg_out_base[i] + (2 * h_ + (k - 2) + 1) * 16

            hid_free = sv.n
            out_free = sv.n
            xguard = {}
            b3_xf_rd = [0]
            for i in range(NT):
                buf = i % 2

                @TE
                def _(t):
                    sv.wait(t, tp_sv[0])
                    ss.wait(t, tp_ss[0])
                    sv.wait(t, b3_xf_rd[0])
                    for cc in range(8):
                        ins = t.transpose(ps_tp_bf[:, cc * 128:(cc + 1) * 128],
                                          xT[:, cc, i * 128:(i + 1) * 128],
                                          eye[:])
                        if cc == 7:
                            sm.bump(ins)
                tpv = sm.n

                @VE
                def _(v):
                    sm.wait(v, tpv)
                    sv.bump(v.tensor_copy(xftile[:], ps_tp_bf[:, 0:1024]))
                tp_sv[0] = sv.n
                xfv = sv.n

                @TE
                def _(t):
                    sv.wait(t, hid_free)
                    for nb in range(4):
                        for cc in range(8):
                            ins = t.matmul(ps_all[:, nb, :],
                                           xT[:, cc, i * 128:(i + 1) * 128],
                                           winT[:, cc, nb * 512:(nb + 1) * 512],
                                           start=(cc == 0), stop=(cc == 7))
                        if nb == 3:
                            sm.bump(ins)
                winv = sm.n

                for h_ in range(PKH):
                    ghc = i * PKH + h_

                    @VE
                    def _(v):
                        sg.wait(v, gwait(ghc, 0))
                        sv.wait(v, xfv)
                        v.scalar_tensor_tensor(
                            prod[:, 0:1024], g_ina[:, :], 1.0, xftile[:],
                            OP.mult, OP.mult,
                            accum_out=linb[:, h_:h_ + 1])
                        sv.bump(v.nop())
                        scn.bump(v.nop())
                    linv = sv.n
                    if h_ == PKH - 1:
                        b3_xf_rd[0] = sv.n

                    @VE
                    def _(v):
                        sv.wait(v, linv)
                        sv.bump(v.tensor_scalar(lp[:, h_:h_ + 1],
                                                linb[:, h_:h_ + 1],
                                                gatev[:, h_, i:i + 1], None,
                                                OP.mult))
                    lpv = sv.n

                    @VE
                    def _(v):
                        sv.wait(v, lpv)
                        sg.wait(v, gwait(ghc, 1))
                        if h_ == 0:
                            sm.wait(v, winv)
                            v.scalar_tensor_tensor(
                                hacc[:], g_inb[:, ghc % 2, :],
                                lp[:, h_:h_ + 1],
                                ps_all[:, 0:4, :].rearrange("p a b -> p (a b)"),
                                OP.mult, OP.add)
                        else:
                            v.scalar_tensor_tensor(
                                hacc[:], g_inb[:, ghc % 2, :],
                                lp[:, h_:h_ + 1], hacc[:],
                                OP.mult, OP.add)
                        sv.bump(v.nop())
                        scn.bump(v.nop())
                    if h_ == 0:
                        hid_free = sv.n
                hidv = sv.n

                @SE
                def _(s):
                    sv.wait(s, hidv)
                    ss.bump(s.activation(hidbf[:], hacc[:], AF.Gelu))
                gfv = ss.n

                for grp in range(2):
                    @TE
                    def _(t):
                        ss.wait(t, gfv)
                        sv.wait(t, tp_sv[0])
                        ss.wait(t, tp_ss[0])
                        for k in range(8):
                            cc = grp * 8 + k
                            ins = t.transpose(
                                ps_tp_bf[:, k * 128:(k + 1) * 128],
                                hidbf[:, cc * 128:(cc + 1) * 128], eye[:])
                            if k == 7:
                                sm.bump(ins)
                    tpv = sm.n

                    @VE
                    def _(v):
                        sm.wait(v, tpv)
                        sv.bump(v.tensor_copy(
                            hT[:, grp * 8:(grp + 1) * 8, :],
                            ps_tp_bf.rearrange("p (c t) -> p c t", c=8)))
                    tp_sv[0] = sv.n
                hTv = sv.n

                @TE
                def _(t):
                    sv.wait(t, hTv)
                    sv.wait(t, out_free)
                    for ob in range(2):
                        for cc in range(16):
                            ins = t.matmul(
                                ps_all[:, 4 + ob, :], hT[:, cc, :],
                                woutfT[:, cc, ob * 512:(ob + 1) * 512],
                                start=(cc == 0), stop=(cc == 15))
                        if ob == 1:
                            sm.bump(ins)
                woutv = sm.n

                @SY
                def _(s):
                    g = xguard.get(i - 2)
                    if g is not None:
                        g[0].wait(s, g[1])
                    s.dma_start(xtile[:, buf, :],
                                resid_src[i * 128:(i + 1) * 128, :]).then_inc(sx.h, 16)
                sx.n += 16
                xld = sx.n

                yss, yso = ytg["ss"], ytg["so"]
                for h_ in range(PKH):
                    ghc = i * PKH + h_

                    @VE
                    def _(v):
                        sg.wait(v, gwait(ghc, 2))
                        ss.wait(v, gfv)
                        v.scalar_tensor_tensor(
                            hacc.bitcast(BF16)[:, 0:DI], g_outa[:, :], 1.0,
                            hidbf[:], OP.mult, OP.mult,
                            accum_out=loutb[:, h_:h_ + 1])
                        sv.bump(v.nop())
                        sco.bump(v.nop())
                    lov = sv.n

                    @VE
                    def _(v):
                        sv.wait(v, lov)
                        sv.bump(v.tensor_scalar(lq[:, h_:h_ + 1],
                                                loutb[:, h_:h_ + 1],
                                                gatev[:, h_, i:i + 1], None,
                                                OP.mult))
                    lqv = sv.n

                    @VE
                    def _(v):
                        sv.wait(v, lqv)
                        sg.wait(v, gwait(ghc, 3))
                        if h_ == 0:
                            sm.wait(v, woutv)
                            ss.wait(v, yss)
                            so.wait(v, yso)
                            v.scalar_tensor_tensor(
                                ytile[:], g_outb[:, :], lq[:, h_:h_ + 1],
                                ps_all[:, 4:6, :].rearrange("p a b -> p (a b)"),
                                OP.mult, OP.add)
                        else:
                            v.scalar_tensor_tensor(
                                ytile[:], g_outb[:, :], lq[:, h_:h_ + 1],
                                ytile[:], OP.mult, OP.add)
                        sv.bump(v.nop())
                        sco.bump(v.nop())
                    if h_ == 0:
                        out_free = sv.n
                oaxv = sv.n

                @VE
                def _(v):
                    sv.wait(v, oaxv)
                    sx.wait(v, xld)
                    sv.bump(v.tensor_tensor(ytile[:], ytile[:],
                                            xtile[:, buf, :], OP.add))
                yv = sv.n
                xguard[i] = (sv, yv)

                @SY
                def _(s):
                    sv.wait(s, yv)
                    s.dma_start(dst[i * 128:(i + 1) * 128, :],
                                ytile[:]).then_inc(so.h, 16)
                    if tap is not None and tap in tap_hbm:
                        s.dma_start(tap_hbm[tap][i * 128:(i + 1) * 128, :],
                                    ytile[:]).then_inc(so.h, 16)
                so.n += 16 * (2 if (tap in tap_hbm) else 1)
                ytg["so"] = so.n

                if next_ln:
                    szn, sdv, vdv = ln_chain(ytile, ntile,
                                             wait_before=[(sv, yv)])
                    tile_to_xT(i, ntile, ss, sdv)
                    ytg["ss"] = szn
            barrier()
            nc.leave_named_scope("B3_" + tag, _scopes["B3_" + tag], False)

    qtb_free = [0]
    b1_qtb_writes = [0]

    def phase_c():
        with ExitStack() as phx:
            pec = phx.enter_context
            outb_s = pec(st("c_outb", [1, D], BF16))
            tempc = pec(st("c_tempc", [128, 8], F32))
            wT = pec(st("c_wT", [128, 8, S], BF16))
            norm2p = pec(st("c_norm2p", [128, 8, 8], F32))
            norm2 = pec(st("c_norm2", [128, 8], F32))
            gvec = pec(st("c_g", [128, 8], F32))
            gmat = pec(st("c_gmat", [128, 8, 16], BF16))
            sqsc = pec(st("c_sqsc", [128, 2, 512], BF16))
            sts = pec(st("c_sts", [16, 512], F32))
            sti = pec(st("c_sti", [128, 16], F32))
            e16 = pec(st("c_e16", [128, 16], F32))
            esc = pec(st("c_esc", [128, 16], F32))
            pibf = pec(st("c_pibf", [128, NT, 16], BF16))
            piT = pec(st("c_piT", [16, S], BF16))
            zrow = pec(st("c_zrow", [16, 1], F32))
            zrowT = pec(st("c_zrowT", [1, 16], F32))
            zc = pec(st("c_zc", [128, 8], F32))
            dotsc = pec(st("c_dotsc", [128, 8], F32))
            attnc = pec(st("c_attnc", [128, 8], F32))
            wtile = pec(st("c_wtile", [128, D], BF16))
            w2tile = pec(st("c_w2tile", [128, D], BF16))
            tmp1 = pec(st("c_tmp1", [128, 512], F32))
            opT = pec(st("c_opT", [128, 8, 512], BF16))

            @SY
            def _(s):
                s.dma_start(tempc[:], tempc_h).then_inc(sx.h, 16)
            sx.n += 16
            tcv = sx.n

            @GE
            def _(g):
                g.dma_start(outb_s[:], outb_h).then_inc(sw.h, 16)
            sw.n += 16

            with ExitStack() as ph2:
                qkvwT = ph2.enter_context(st("c_qkvwT", [128, 8, D], BF16))
                load_T(qkvw_h, D, D, simple_evacT(qkvwT))
                barrier()
                bank_free = [ss.n] * 4
                for tbk in range(8):
                    for oc in range(8):
                        bk = (tbk * 8 + oc) % 4
                        evv = bank_free[bk]

                        @TE
                        def _(t):
                            ss.wait(t, evv)
                            for cc in range(8):
                                ins = t.matmul(
                                    ps_all[:, bk, :],
                                    qkvwT[:, cc, oc * 128:(oc + 1) * 128],
                                    xT[:, cc, tbk * 512:(tbk + 1) * 512],
                                    start=(cc == 0), stop=(cc == 7))
                                if cc == 7:
                                    sm.bump(ins)
                        mmv = sm.n

                        @SE
                        def _(s):
                            sm.wait(s, mmv)
                            ss.bump(s.copy(
                                wT[:, oc, tbk * 512:(tbk + 1) * 512],
                                ps_all[:, bk, :]))
                        bank_free[bk] = ss.n
                barrier()
            wTv = ss.n

            outwT = pec(st("c_outwT", [128, 8, D], BF16))
            load_T(outw_h, D, D, simple_evacT(outwT))

            for cc in range(8):
                for blk in range(8):
                    par = blk % 2

                    @SE
                    def _(s):
                        ss.wait(s, wTv)
                        sv.wait(s, c_sq_free[0])
                        ss.bump(s.activation(
                            sqsc[:, par, :],
                            wT[:, cc, blk * 512:(blk + 1) * 512], AF.Square,
                            accum_out=norm2p[:, cc, blk:blk + 1]))
            nv = ss.n

            @VE
            def _(v):
                ss.wait(v, nv)
                sv.bump(v.tensor_reduce(norm2[:], norm2p[:],
                                        mybir.AxisListType.X, OP.add))
            n2r = sv.n

            @VE
            def _(v):
                sv.wait(v, n2r)
                sv.bump(v.tensor_scalar(norm2[:], norm2[:], 1e-24, None,
                                        OP.max))
            n2v = sv.n

            @VE
            def _(v):
                sv.wait(v, n2v)
                sv.bump(v.reciprocal(gvec[:], norm2[:]))
            rgv = sv.n

            @VE
            def _(v):
                sv.wait(v, rgv)
                sx.wait(v, tcv)
                sv.bump(v.tensor_tensor(gvec[:], gvec[:], tempc[:], OP.mult))
            gv = sv.n

            @VE
            def _(v):
                sv.wait(v, gv)
                sv.bump(v.memset(gmat[:], 0.0))
            gm0 = sv.n

            @VE
            def _(v):
                sv.wait(v, gm0)
                for j in range(8):
                    v.tensor_copy(gmat[0:64, j, 2 * j:2 * j + 1],
                                  gvec[0:64, j:j + 1])
                    ins = v.tensor_copy(gmat[64:128, j, 2 * j + 1:2 * j + 2],
                                        gvec[64:128, j:j + 1])
                sv.bump(ins)
            gmv = sv.n

            pst_free = sv.n
            for tbk in range(8):
                for cc in range(8):
                    par = cc % 2

                    k_glob = tbk * 8 + cc

                    @SE
                    def _(s):
                        sv.wait(s, gmv)
                        if k_glob >= 2:
                            sm.wait(s, sq_hist[k_glob - 2])
                        ss.bump(s.activation(
                            sqsc[:, par, :],
                            wT[:, cc, tbk * 512:(tbk + 1) * 512], AF.Square))
                    sqv = ss.n

                    @TE
                    def _(t):
                        ss.wait(t, sqv)
                        if tbk > 0 and cc == 0:
                            ss.wait(t, c_sts_done[0])
                        ins = t.matmul(ps_c[0:16, :], gmat[:, cc, :],
                                       sqsc[:, par, :],
                                       start=(cc == 0), stop=(cc == 7))
                        sm.bump(ins)
                    sq_hist.append(sm.n)
                stv = sm.n

                @SE
                def _(s):
                    sm.wait(s, stv)
                    ss.bump(s.copy(sts[:], ps_c[0:16, :]))
                c_sts_done[0] = ss.n
                stsv = ss.n

                for q in range(4):
                    i = tbk * 4 + q

                    @TE
                    def _(t):
                        ss.wait(t, stsv)
                        sv.wait(t, tp_sv[0])
                        ss.wait(t, tp_ss[0])
                        sm.bump(t.transpose(ps_tp[:, 0:16],
                                            sts[:, q * 128:(q + 1) * 128],
                                            eye16f[:]))
                    tpv = sm.n

                    @VE
                    def _(v):
                        sm.wait(v, tpv)
                        sv.bump(v.tensor_copy(sti[:], ps_tp[:, 0:16]))
                    tp_sv[0] = sv.n
                    stiv = sv.n

                    @SE
                    def _(s):
                        sv.wait(s, stiv)
                        ss.bump(s.activation(e16[:], sti[:], AF.Exp))
                    ev = ss.n

                    @VE
                    def _(v):
                        ss.wait(v, ev)
                        sv.bump(v.tensor_scalar(esc[:], e16[:], 1.0, None,
                                                OP.mult, OP.add,
                                                accum_out=tb[:]))
                    sev = sv.n

                    @VE
                    def _(v):
                        sv.wait(v, sev)
                        sv.bump(v.reciprocal(tc[:], tb[:]))
                    rcv = sv.n

                    @VE
                    def _(v):
                        sv.wait(v, rcv)
                        sv.bump(v.tensor_scalar(pibf[:, i, :], e16[:], tc[:],
                                                None, OP.mult))
                    piv = sv.n

                    @TE
                    def _(t):
                        sv.wait(t, piv)
                        sv.wait(t, tp_sv[0])
                        ss.wait(t, tp_ss[0])
                        sm.bump(t.transpose(ps_tp.bitcast(BF16)[0:16, 0:128],
                                            pibf[:, i, :], eye[:]))
                    ptv = sm.n

                    @VE
                    def _(v):
                        sm.wait(v, ptv)
                        sv.bump(v.tensor_copy(piT[:, i * 128:(i + 1) * 128],
                                              ps_tp.bitcast(BF16)[0:16, 0:128]))
                    tp_sv[0] = sv.n
                    pst_free = sv.n

            @TE
            def _(t):
                sv.wait(t, pst_free)
                ins = None
                for i in range(NT):
                    ins = t.matmul(ps_c[0:16, 0:1], pibf[:, i, :], ones_c[:],
                                   start=(i == 0), stop=(i == NT - 1))
                sm.bump(ins)
            zmv = sm.n

            @SE
            def _(s):
                sm.wait(s, zmv)
                ss.bump(s.copy(zrow[:], ps_c[0:16, 0:1]))
            zev = ss.n

            @VE
            def _(v):
                ss.wait(v, zev)
                sv.bump(v.tensor_scalar(zrow[:], zrow[:], 1e-8, None, OP.add))
            zadd = sv.n

            @TE
            def _(t):
                sv.wait(t, zadd)
                sv.wait(t, tp_sv[0])
                ss.wait(t, tp_ss[0])
                sm.bump(t.transpose(ps_tp[0:1, 0:16], zrow[:], eye16f[:]))
            ztv = sm.n

            @SE
            def _(s):
                sm.wait(s, ztv)
                ss.bump(s.copy(zrowT[:], ps_tp[0:1, 0:16]))
            tp_ss[0] = ss.n
            zrv = ss.n

            @TE
            def _(t):
                ss.wait(t, zrv)
                sv.wait(t, tp_sv[0])
                sm.bump(t.matmul(ps_tp[:, 0:16], ones_rf[:], zrowT[:],
                                 start=True, stop=True))
            zbv = sm.n

            @VE
            def _(v):
                sm.wait(v, zbv)
                for j in range(8):
                    v.tensor_copy(zc[0:64, j:j + 1],
                                  ps_tp[0:64, 2 * j:2 * j + 1])
                    ins = v.tensor_copy(zc[64:128, j:j + 1],
                                        ps_tp[64:128, 2 * j + 1:2 * j + 2])
                sv.bump(ins)
            tp_sv[0] = sv.n
            zcv = sv.n

            @VE
            def _(v):
                sv.wait(v, zcv)
                sv.bump(v.reciprocal(zc[:], zc[:]))
            zrcv = sv.n

            c_wtile_rd = [0]
            for i in range(NT):
                @TE
                def _(t):
                    sv.wait(t, tp_sv[0])
                    ss.wait(t, tp_ss[0])
                    ss.wait(t, c_wtile_rd[0])
                    if i == 0:
                        sv.wait(t, zrcv)
                    for cc in range(8):
                        ins = t.transpose(ps_tp_bf[:, cc * 128:(cc + 1) * 128],
                                          wT[:, cc, i * 128:(i + 1) * 128],
                                          eye[:])
                        if cc == 7:
                            sm.bump(ins)
                tpv = sm.n

                @VE
                def _(v):
                    sm.wait(v, tpv)
                    sv.bump(v.tensor_copy(wtile[:], ps_tp_bf[:, 0:1024]))
                tp_sv[0] = sv.n
                wtv = sv.n

                @SE
                def _(s):
                    sv.wait(s, wtv)
                    ss.bump(s.activation(w2tile[:], wtile[:], AF.Square))
                w2v = ss.n
                c_wtile_rd[0] = ss.n

                @TE
                def _(t):
                    ss.wait(t, w2v)
                    for cc in range(8):
                        ins = t.matmul(ps_c[:, cc * 16:(cc + 1) * 16],
                                       w2tile[:, cc * 128:(cc + 1) * 128],
                                       pibf[:, i, :],
                                       start=(i == 0), stop=(i == NT - 1))
                        if cc == 7:
                            sm.bump(ins)
            dmv = sm.n

            @VE
            def _(v):
                sm.wait(v, dmv)
                for j in range(8):
                    v.tensor_copy(dotsc[0:64, j:j + 1],
                                  ps_c[0:64, j * 16 + 2 * j:j * 16 + 2 * j + 1])
                    ins = v.tensor_copy(
                        dotsc[64:128, j:j + 1],
                        ps_c[64:128, j * 16 + 2 * j + 1:j * 16 + 2 * j + 2])
                sv.bump(ins)
            dcv = sv.n

            @VE
            def _(v):
                sv.wait(v, dcv)
                sv.bump(v.tensor_tensor(attnc[:], dotsc[:], zc[:], OP.mult))
            acv0 = sv.n

            @VE
            def _(v):
                sv.wait(v, acv0)
                sv.bump(v.tensor_scalar(attnc[:], attnc[:], 1.0, None,
                                        OP.add))
            acv = sv.n

            @VE
            def _(v):
                sv.wait(v, acv)
                sv.bump(v.reciprocal(attnc[:], attnc[:]))
            afv = sv.n

            pp_free = sv.n
            xguard = {}
            for tbk in range(8):
                for cc in range(8):
                    evv = pp_free

                    @TE
                    def _(t):
                        sv.wait(t, afv)
                        sv.wait(t, evv)
                        sm.bump(t.matmul(ps_c[:, 0:512],
                                         emat[0:16, cc * 128:(cc + 1) * 128],
                                         piT[:, tbk * 512:(tbk + 1) * 512],
                                         start=True, stop=True))
                    pev = sm.n

                    @VE
                    def _(v):
                        sm.wait(v, pev)
                        v.tensor_scalar(tmp1[:],
                                        wT[:, cc, tbk * 512:(tbk + 1) * 512],
                                        attnc[:, cc:cc + 1], -1.0,
                                        OP.mult, OP.mult)
                        sv.bump(v.tensor_tensor(opT[:, cc, :], tmp1[:],
                                                ps_c[:, 0:512], OP.mult))
                    pp_free = sv.n
                opv = sv.n

                for q in range(4):
                    i = tbk * 4 + q
                    buf = i % 2

                    @SY
                    def _(s):
                        g = xguard.get(i - 2)
                        if g is not None:
                            g[0].wait(s, g[1])
                        s.dma_start(xtile[:, buf, :],
                                    yb_h[i * 128:(i + 1) * 128, :]).then_inc(sx.h, 16)
                    sx.n += 16
                    xld = sx.n
                    oevv = c_out_free[0]

                    @TE
                    def _(t):
                        sv.wait(t, opv)
                        sv.wait(t, oevv)
                        for ob in range(2):
                            for cc in range(8):
                                t.matmul(ps_all[:, 4 + ob, :],
                                         opT[:, cc, q * 128:(q + 1) * 128],
                                         outwT[:, cc, ob * 512:(ob + 1) * 512],
                                         start=(cc == 0), stop=False)
                            ins = t.matmul(ps_all[:, 4 + ob, :], ones_r[:],
                                           outb_s[:, ob * 512:(ob + 1) * 512],
                                           start=False, stop=True)
                            if ob == 1:
                                sm.bump(ins)
                    mmv = sm.n
                    yv = finish_tile(i, mmv, xld, buf, ya_h, None, True,
                                     "y3", xguard)
                    c_out_free[0] = yv
            barrier()

    c_out_free = [0]
    c_sq_free = [0]
    c_sts_done = [0]
    sq_hist = []

    if "A" in phases:
        with nc.named_scope("phaseA"):
            phase_a()
    if "B1" in phases:
        with nc.named_scope("phaseB1"):
            phase_b(resid_src=ya_h, dst=yb_h, next_ln=("C" in phases), tap="y2", tag="1")
    if "C" in phases:
        with nc.named_scope("phaseC"):
            phase_c()
    if "B2" in phases:
        with nc.named_scope("phaseB2"):
            phase_b(resid_src=ya_h, dst=out_h, next_ln=False, tap=None, tag="2")

    @SY
    def _(s):
        so.wait(s, so.n)
        sg.wait(s, sg.n)
        sw.wait(s, sw.n)
        sx.wait(s, sx.n)

    es.close()
    return nc


def _consts():
    c = {}
    c["c_eye_bf"] = np.eye(128, dtype=ml_dtypes.bfloat16)
    c["c_eye_f"] = np.eye(128, dtype=np.float32)
    c["c_eye16_f"] = np.eye(16, dtype=np.float32)
    c["c_ones_1x128_bf"] = np.ones((1, 128), dtype=ml_dtypes.bfloat16)
    c["c_ones_1x128_f"] = np.ones((1, 128), dtype=np.float32)
    c["c_ones_128x1_bf"] = np.ones((128, 1), dtype=ml_dtypes.bfloat16)
    em = np.zeros((16, D), dtype=ml_dtypes.bfloat16)
    for h in range(16):
        em[h, h * 64:(h + 1) * 64] = 1
    c["c_emat_bf"] = em
    return c


def _in_map(core, inputs):
    m = {}
    m["x"] = inputs["L0_attn_a_hidden_states"][core]
    m["gamma_row"] = inputs["L0_attn_a_ln_gamma"].reshape(1, D)
    m["beta_row"] = inputs["L0_attn_a_ln_beta"].reshape(1, D)
    m["W_qkv"] = inputs["L0_attn_a_W_qkv"]
    b = inputs["L0_attn_a_b_qkv"]
    bqk = np.zeros((128, NH), np.float32)
    bv = np.zeros((64, NH), np.float32)
    for h in range(NH):
        bqk[0:64, h] = b[h * 64:(h + 1) * 64]
        bqk[64:128, h] = b[1024 + h * 64:1024 + (h + 1) * 64]
        bv[:, h] = b[2048 + h * 64:2048 + (h + 1) * 64]
    m["bqk_r"] = bqk
    m["bv_r"] = bv
    m["W_out_a"] = inputs["L0_attn_a_W_out"]
    m["b_out_a"] = inputs["L0_attn_a_b_out"].reshape(1, D)
    m["attb_qkv_w"] = inputs["L0_attn_b_qkv_w"]
    t = np.asarray(inputs["L0_attn_b_temp"]).reshape(16)
    tcm = np.zeros((128, 8), np.float32)
    for j in range(8):
        tcm[0:64, j] = t[2 * j]
        tcm[64:128, j] = t[2 * j + 1]
    m["attb_tempc"] = tcm
    m["attb_out_w"] = inputs["L0_attn_b_out_w"]
    m["attb_out_b"] = inputs["L0_attn_b_out_b"].reshape(1, D)
    m["ffn_Wq"] = inputs["L0_ffn_Wq"]
    m["ffn_keys"] = inputs["L0_ffn_keys"]
    m["ffn_W_in"] = inputs["L0_ffn_W_in"]
    m["ffn_W_out"] = inputs["L0_ffn_W_out"]
    m["ffn_in_a"] = inputs["L0_ffn_in_a"]
    m["ffn_in_b"] = inputs["L0_ffn_in_b"]
    m["ffn_out_a"] = inputs["L0_ffn_out_a"]
    m["ffn_out_b"] = inputs["L0_ffn_out_b"]
    m.update(_consts())
    return {k: np.ascontiguousarray(v) for k, v in m.items()}


def kernel(**inputs):
    nc = build_nc()
    in_maps = [_in_map(c, inputs) for c in range(8)]
    res = run_bass_kernel_spmd(nc, in_maps, core_ids=list(range(8)))
    out = np.stack([np.asarray(res.results[c]["y"]) for c in range(8)], axis=0)
    return out.astype(np.float32)


# version 07 (final)
# speedup vs baseline: 1.1642x; 1.0123x over previous
import numpy as np
import ml_dtypes
from contextlib import ExitStack

import concourse.bass as bass
import concourse.mybir as mybir
from concourse.bass_utils import run_bass_kernel_spmd
from concourse.alu_op_type import AluOpType as OP

F32 = mybir.dt.float32
BF16 = mybir.dt.bfloat16
U32 = mybir.dt.uint32
AF = mybir.ActivationFunctionType

S, D, NT = 4096, 1024, 32
NH = 16
PKH, DI, NE = 8, 2048, 16384
EPS = 1e-5


class Ctr:
    def __init__(self, es, nc, name):
        self.h = es.enter_context(nc.semaphore(name))
        self.n = 0
        self.maxw = 0
        self.seen = {}

    def bump(self, ins, k=1):
        ins.then_inc(self.h, k)
        self.n += k
        return self.n

    def wait(self, eng, v=None):
        v = self.n if v is None else v
        if v <= 0:
            return
        key = id(eng)
        if self.seen.get(key, -1) >= v:
            return
        self.seen[key] = v
        self.maxw = max(self.maxw, v)
        eng.wait_ge(self.h, v)


def build_nc(phases=("A", "B1", "C", "B2"), taps=()):
    nc = bass.Bass("TRN2", target_bir_lowering=False)
    es = ExitStack()
    tap_set = set(taps)

    dt = nc.dram_tensor
    x_h = dt("x", [S, D], F32, kind="ExternalInput").ap()
    gamma_row_h = dt("gamma_row", [1, D], F32, kind="ExternalInput").ap()
    beta_row_h = dt("beta_row", [1, D], F32, kind="ExternalInput").ap()
    wqkv_h = dt("W_qkv", [3 * D, D], F32, kind="ExternalInput").ap()
    bqk_h = dt("bqk_r", [128, NH], F32, kind="ExternalInput").ap()
    bv_h = dt("bv_r", [64, NH], F32, kind="ExternalInput").ap()
    wouta_h = dt("W_out_a", [D, D], F32, kind="ExternalInput").ap()
    bouta_h = dt("b_out_a", [1, D], F32, kind="ExternalInput").ap()
    qkvw_h = dt("attb_qkv_w", [D, D], F32, kind="ExternalInput").ap()
    tempc_h = dt("attb_tempc", [128, 8], F32, kind="ExternalInput").ap()
    outw_h = dt("attb_out_w", [D, D], F32, kind="ExternalInput").ap()
    outb_h = dt("attb_out_b", [1, D], F32, kind="ExternalInput").ap()
    wq_h = dt("ffn_Wq", [8 * D, D], F32, kind="ExternalInput").ap()
    keys_h = dt("ffn_keys", [PKH, 128, 2, 512], F32, kind="ExternalInput").ap()
    win_h = dt("ffn_W_in", [DI, D], F32, kind="ExternalInput").ap()
    wout_h = dt("ffn_W_out", [D, DI], F32, kind="ExternalInput").ap()
    tbl = {"ina": dt("ffn_in_a", [NE, D], F32, kind="ExternalInput").ap(),
           "inb": dt("ffn_in_b", [NE, DI], F32, kind="ExternalInput").ap(),
           "outa": dt("ffn_out_a", [NE, DI], F32, kind="ExternalInput").ap(),
           "outb": dt("ffn_out_b", [NE, D], F32, kind="ExternalInput").ap()}
    eye_h = dt("c_eye_bf", [128, 128], BF16, kind="ExternalInput").ap()
    eyef_h = dt("c_eye_f", [128, 128], F32, kind="ExternalInput").ap()
    ones_r_h = dt("c_ones_1x128_bf", [1, 128], BF16, kind="ExternalInput").ap()
    ones_rf_h = dt("c_ones_1x128_f", [1, 128], F32, kind="ExternalInput").ap()
    ones_c_h = dt("c_ones_128x1_bf", [128, 1], BF16, kind="ExternalInput").ap()
    emat_h = dt("c_emat_bf", [16, D], BF16, kind="ExternalInput").ap()
    eye16f_h = dt("c_eye16_f", [16, 16], F32, kind="ExternalInput").ap()

    out_h = dt("y", [S, D], F32, kind="ExternalOutput").ap()
    ya_h = dt("ya", [S, D], F32, kind="Internal").ap()
    yb_h = dt("yb", [S, D], F32, kind="Internal").ap()
    ctx_h = dt("ctxs", [128, 8, S], BF16,
               kind="ExternalOutput" if "ctx" in tap_set else "Internal").ap()
    tap_hbm = {t: dt(f"tap_{t}", [S, D], F32, kind="ExternalOutput").ap()
               for t in tap_set if t.startswith("y")}
    if "xn" in tap_set:
        tap_hbm["xn"] = dt("tap_xn", [128, 8, S], F32, kind="ExternalOutput").ap()
    if "idx" in tap_set:
        tap_hbm["idx"] = dt("tap_idx", [128, PKH, NT], U32, kind="ExternalOutput").ap()
        tap_hbm["gate"] = dt("tap_gate", [128, PKH, NT], F32, kind="ExternalOutput").ap()
    if "qkv0" in tap_set:
        tap_hbm["q0"] = dt("tap_q0", [64, S], F32, kind="ExternalOutput").ap()
        tap_hbm["k0"] = dt("tap_k0", [64, S], F32, kind="ExternalOutput").ap()
        tap_hbm["v0"] = dt("tap_v0", [64, S], F32, kind="ExternalOutput").ap()

    ec = es.enter_context
    st = nc.sbuf_tensor

    eye = ec(st("eye", [128, 128], BF16))
    eyef = ec(st("eyef", [128, 128], F32))
    eye16f = ec(st("eye16f", [16, 16], F32))
    ones_r = ec(st("ones_r", [1, 128], BF16))
    ones_rf = ec(st("ones_rf", [1, 128], F32))
    ones_c = ec(st("ones_c", [128, 1], BF16))
    emat = ec(st("emat", [16, D], BF16))
    xT = ec(st("xT", [128, 8, S], BF16))
    rsc = ec(st("rsc", [128, NT], F32))
    rbi = ec(st("rbi", [128, NT], F32))
    st6 = ec(st("st6", [128, 12], F32))
    aggr = ec(st("aggr", [128, 2], F32))
    ta = ec(st("ta", [128, 1], F32))
    tb = ec(st("tb", [128, 1], F32))
    tc = ec(st("tc", [128, 1], F32))
    xtile = ec(st("xtile", [128, 2, D], F32))
    ytile = ec(st("ytile", [128, D], F32))
    ntile = ec(st("ntile", [128, D], BF16))
    wstage = ec(st("wstage", [128, 2, D], BF16))

    ps_all = ec(nc.psum_tensor("ps_all", [128, 6, 512], F32))
    ps_c = ec(nc.psum_tensor("ps_c", [128, 512], F32))
    ps_tp = ec(nc.psum_tensor("ps_tp", [128, 512], F32))
    ps_tp_bf = ps_tp.bitcast(BF16)

    sx = Ctr(es, nc, "sx")
    sw = Ctr(es, nc, "sw")
    sg = Ctr(es, nc, "sg")
    so = Ctr(es, nc, "so")
    sm = Ctr(es, nc, "sm")
    ss = Ctr(es, nc, "ss")
    sv = Ctr(es, nc, "sv")
    scn = Ctr(es, nc, "scn")
    sco = Ctr(es, nc, "sco")

    block = es.enter_context(nc.Block())
    TE, VE, SE, SY, GE = (block.tensor, block.vector, block.scalar,
                          block.sync, block.gpsimd)

    def barrier():
        snap = [(c, c.n) for c in (sx, sw, sg, so, sm, ss, sv, scn, sco)]
        for sec in (TE, VE, SE, SY, GE):
            @sec
            def _(e):
                for c, v in snap:
                    c.wait(e, v)

    ytg = {"ss": 0, "so": 0}
    _scopes = {}
    tp_ss = [0]
    tp_sv = [0]
    wst_rd = [0, 0]

    def load_T(src_h, rows, cols, write_evac):
        ncc = cols // 128
        for j in range(rows // 128):
            buf = j % 2

            @GE
            def _(g):
                sm.wait(g, wst_rd[buf])
                g.dma_start(wstage[:, buf, :cols],
                            src_h[j * 128:(j + 1) * 128, :]).then_inc(sw.h, 16)
            sw.n += 16
            ldv = sw.n

            @TE
            def _(t):
                sw.wait(t, ldv)
                sv.wait(t, tp_sv[0])
                ss.wait(t, tp_ss[0])
                for cc in range(ncc):
                    ins = t.transpose(ps_tp_bf[:, cc * 128:(cc + 1) * 128],
                                      wstage[:, buf, cc * 128:(cc + 1) * 128],
                                      eye[:])
                    if cc == ncc - 1:
                        sm.bump(ins)
            wst_rd[buf] = sm.n
            write_evac(j, sm.n)
            tp_sv[0] = sv.n

    def simple_evacT(dstT):
        def f(j, tpv):
            @VE
            def _(v):
                sm.wait(v, tpv)
                sv.bump(v.tensor_copy(
                    dstT[:, :, j * 128:(j + 1) * 128],
                    ps_tp_bf.rearrange("p (c t) -> p c t", c=8)))
        return f

    def ln_chain(xin, out_bf, gb=None, z_out=None, store_affine=None,
                 wait_before=None):
        @VE
        def _(v):
            for c, val in (wait_before or []):
                c.wait(v, val)
            v.bn_stats(st6[:, 0:6], xin[:, 0:512])
            sv.bump(v.bn_stats(st6[:, 6:12], xin[:, 512:1024]))
        vst = sv.n

        @VE
        def _(v):
            sv.wait(v, vst)
            sv.bump(v.bn_aggr(aggr[:], st6[:]))
        vag = sv.n

        @VE
        def _(v):
            sv.wait(v, vag)
            sv.bump(v.tensor_scalar(ta[:], aggr[:, 1:2], EPS, None, OP.add))
        vta = sv.n

        @SE
        def _(s):
            sv.wait(s, vta)
            ss.bump(s.activation(tb[:], ta[:], AF.Ln))
        sln = ss.n

        @SE
        def _(s):
            ss.wait(s, sln)
            ss.bump(s.activation(tb[:], tb[:], AF.Exp, scale=-0.5))
        srs = ss.n

        @VE
        def _(v):
            ss.wait(v, srs)
            sv.bump(v.tensor_scalar(tc[:], aggr[:, 0:1], tb[:], -1.0,
                                    OP.mult, OP.mult))
        vtc = sv.n

        if store_affine is not None:
            sca, bia = store_affine

            @VE
            def _(v):
                sv.wait(v, vtc)
                v.tensor_scalar(sca, tb[:], 1.0, None, OP.add)
                sv.bump(v.tensor_scalar(bia, tc[:], 1.0, None, OP.mult))

        if gb is not None:
            gbc, bbc, zn0, z2 = gb
            tgt = zn0
        else:
            tgt = z_out if z_out is not None else out_bf

        @SE
        def _(s):
            sv.wait(s, vtc)
            ss.bump(s.activation(tgt[:], xin[:], AF.Identity,
                                 scale=tb[:], bias=tc[:]))
        szn = ss.n

        if gb is not None:
            @VE
            def _(v):
                ss.wait(v, szn)
                v.scalar_tensor_tensor(z2[:], zn0[:], 1.0, gbc[:],
                                       OP.mult, OP.mult)
                sv.bump(v.tensor_tensor(out_bf[:], z2[:], bbc[:], OP.add))
        return szn, ss.n, sv.n

    def tile_to_xT(i, src_bf, src_ctr, src_v):
        @TE
        def _(t):
            src_ctr.wait(t, src_v)
            sv.wait(t, tp_sv[0])
            ss.wait(t, tp_ss[0])
            for cc in range(8):
                ins = t.transpose(ps_tp_bf[:, cc * 128:(cc + 1) * 128],
                                  src_bf[:, cc * 128:(cc + 1) * 128], eye[:])
                if cc == 7:
                    sm.bump(ins)
        tpv = sm.n

        @VE
        def _(v):
            sm.wait(v, tpv)
            sv.bump(v.tensor_copy(
                xT[:, :, i * 128:(i + 1) * 128],
                ps_tp_bf.rearrange("p (c t) -> p c t", c=8)))
        tp_sv[0] = sv.n
        return sv.n

    def finish_tile(i, mmv, xld, buf, dst, affine_z, next_ln, tap, xguard):
        if affine_z is not None:
            @SE
            def _(s):
                sx.wait(s, xld)
                ss.bump(s.activation(affine_z[:], xtile[:, buf, :],
                                     AF.Identity, scale=rsc[:, i:i + 1],
                                     bias=rbi[:, i:i + 1]))
            rv = (ss, ss.n)
            xguard[i] = (ss, ss.n)
        else:
            rv = (sx, xld)

        yss, yso = ytg["ss"], ytg["so"]

        @VE
        def _(v):
            sm.wait(v, mmv)
            rv[0].wait(v, rv[1])
            ss.wait(v, yss)
            so.wait(v, yso)
            src = affine_z[:] if affine_z is not None else xtile[:, buf, :]
            sv.bump(v.tensor_tensor(
                ytile[:], src,
                ps_all[:, 4:6, :].rearrange("p a b -> p (a b)"), OP.add))
        yv = sv.n
        if affine_z is None:
            xguard[i] = (sv, yv)

        @SY
        def _(s):
            sv.wait(s, yv)
            s.dma_start(dst[i * 128:(i + 1) * 128, :],
                        ytile[:]).then_inc(so.h, 16)
            if tap is not None and tap in tap_hbm:
                s.dma_start(tap_hbm[tap][i * 128:(i + 1) * 128, :],
                            ytile[:]).then_inc(so.h, 16)
        so.n += 16 * (2 if (tap in tap_hbm) else 1)
        ytg["so"] = so.n

        if next_ln:
            szn, sdv, vdv = ln_chain(ytile, ntile, wait_before=[(sv, yv)])
            tile_to_xT(i, ntile, ss, sdv)
            ytg["ss"] = szn
        return yv

    @SY
    def _(s):
        for dsts, srcs in ((eye, eye_h), (eyef, eyef_h), (eye16f, eye16f_h),
                           (ones_r, ones_r_h), (ones_rf, ones_rf_h),
                           (ones_c, ones_c_h), (emat, emat_h)):
            s.dma_start(dsts[:], srcs).then_inc(sx.h, 16)
    sx.n += 7 * 16
    barrier()

    def phase_a():
        with ExitStack() as phx:
            pec = phx.enter_context
            gr_row = pec(st("a_gr", [1, D], F32))
            br_row = pec(st("a_br", [1, D], F32))
            gbc = pec(st("a_gbc", [128, D], F32))
            bbc = pec(st("a_bbc", [128, D], F32))
            ztile = pec(st("a_ztile", [128, D], F32))
            zn0 = pec(st("a_zn0", [128, D], F32))
            z2 = pec(st("a_z2", [128, D], F32))
            wqkvT = pec(st("a_wqkvT", [128, 8, 768], BF16))
            bqk_s = pec(st("a_bqk", [128, NH], F32))
            bv_s = pec(st("a_bv", [64, NH], F32))
            woutT = pec(st("a_woutT", [128, 8, D], BF16))
            bouta_s = pec(st("a_bouta", [1, D], BF16))
            qT = pec(st("a_qT", [64, S], BF16))
            kT = pec(st("a_kT", [64, S], BF16))
            vT = pec(st("a_vT", [64, S], BF16))
            vb = pec(st("a_vb", [128, NT, 66], BF16))
            pT = pec(st("a_pT", [128, 6, 512], BF16))
            recb = pec(st("a_recb", [1, 512], BF16))
            recf = pec(st("a_recf", [64, 512], F32))
            cstg = pec(st("a_cstg", [64, 512], BF16))
            ctile = pec(st("a_ctile", [128, 2, 8, 128], BF16))

            @SY
            def _(s):
                s.dma_start(bqk_s[:], bqk_h).then_inc(sx.h, 16)
                s.dma_start(bv_s[:], bv_h).then_inc(sx.h, 16)
                s.dma_start(gr_row[:], gamma_row_h).then_inc(sx.h, 16)
                s.dma_start(br_row[:], beta_row_h).then_inc(sx.h, 16)
            sx.n += 64
            gbrow = sx.n

            @GE
            def _(g):
                g.dma_start(bouta_s[:], bouta_h).then_inc(sw.h, 16)
            sw.n += 16

            @TE
            def _(t):
                sx.wait(t, gbrow)
                for half in range(2):
                    t.matmul(ps_all[:, 2 * half, :], ones_rf[:],
                             gr_row[:, half * 512:(half + 1) * 512],
                             start=True, stop=True)
                    ins = t.matmul(ps_all[:, 2 * half + 1, :], ones_rf[:],
                                   br_row[:, half * 512:(half + 1) * 512],
                                   start=True, stop=True)
                    if half == 1:
                        sm.bump(ins)
            gmmv = sm.n

            @VE
            def _(v):
                sm.wait(v, gmmv)
                v.tensor_copy(gbc[:, 0:512], ps_all[:, 0, :])
                v.tensor_copy(bbc[:, 0:512], ps_all[:, 1, :])
                v.tensor_copy(gbc[:, 512:1024], ps_all[:, 2, :])
                sv.bump(v.tensor_copy(bbc[:, 512:1024], ps_all[:, 3, :]))
            gb_done = sv.n

            load_T(wouta_h, D, D, simple_evacT(woutT))

            @VE
            def _(v):
                v.memset(vb[:, :, 64:66], 0.0)
                sv.bump(v.memset(vb[:, :, 64:65], 1.0))
            vb_ones = sv.n
            barrier()

            _scopes["A1"] = nc.enter_named_scope("A1", False)[0]
            zdone = {}
            for i in range(NT):
                buf = i % 2

                @SY
                def _(s):
                    if i >= 2:
                        ss.wait(s, zdone[i - 2])
                    s.dma_start(xtile[:, buf, :],
                                x_h[i * 128:(i + 1) * 128, :]).then_inc(sx.h, 16)
                sx.n += 16
                xld = sx.n
                szn, _s, _v = ln_chain(
                    xtile[:, buf, :], None, z_out=ztile,
                    store_affine=(rsc[:, i:i + 1], rbi[:, i:i + 1]),
                    wait_before=[(sx, xld)])
                zdone[i] = szn
                szn2, sdv, vdv = ln_chain(ztile, ntile,
                                          gb=(gbc, bbc, zn0, z2),
                                          wait_before=[(ss, szn)])
                tile_to_xT(i, ntile, sv, vdv)
            barrier()
            if "xn" in tap_set:
                @GE
                def _(g):
                    g.dma_start(tap_hbm["xn"], xT[:]).then_inc(sw.h, 16)
                sw.n += 16
                barrier()
            nc.leave_named_scope("A1", _scopes["A1"], False)
            _scopes["A2"] = nc.enter_named_scope("A2", False)[0]

            for quarter in range(4):
                for band in range(3):
                    row0 = band * 1024 + quarter * 256

                    def wq_evac(j, tpv, band=band):
                        @VE
                        def _(v):
                            sm.wait(v, tpv)
                            sv.bump(v.tensor_copy(
                                wqkvT[:, :, band * 256 + j * 128:
                                      band * 256 + (j + 1) * 128],
                                ps_tp_bf.rearrange("p (c t) -> p c t", c=8)))
                    load_T(wqkv_h[row0:row0 + 256, :], 256, D, wq_evac)
                wq_done = sv.n
                for hh in range(4):
                    h = quarter * 4 + hh
                    a2_head(h, hh, wqkvT, bqk_s, bv_s, qT, kT, vT, vb, pT,
                            recb, recf, cstg, wq_done, vb_ones)
            barrier()
            nc.leave_named_scope("A2", _scopes["A2"], False)
            _scopes["A3"] = nc.enter_named_scope("A3", False)[0]

            xguard = {}
            cld = {}
            for i in range(NT):
                buf = i % 2
                cbuf = i % 2
                prev = a3_ctile_used.get(i - 2, 0)

                @SY
                def _(s):
                    sm.wait(s, prev)
                    s.dma_start(ctile[:, cbuf, :, :],
                                ctx_h[:, :, i * 128:(i + 1) * 128]).then_inc(sx.h, 16)
                sx.n += 16
                cldv = sx.n

                @SY
                def _(s):
                    g = xguard.get(i - 2)
                    if g is not None:
                        g[0].wait(s, g[1])
                    s.dma_start(xtile[:, buf, :],
                                x_h[i * 128:(i + 1) * 128, :]).then_inc(sx.h, 16)
                sx.n += 16
                xld = sx.n

                evv = a3_psfree[0]

                @TE
                def _(t):
                    sv.wait(t, evv)
                    sx.wait(t, cldv)
                    for ob in range(2):
                        for cc in range(8):
                            t.matmul(ps_all[:, 4 + ob, :],
                                     ctile[:, cbuf, cc, :],
                                     woutT[:, cc, ob * 512:(ob + 1) * 512],
                                     start=(cc == 0), stop=False)
                        ins = t.matmul(ps_all[:, 4 + ob, :], ones_r[:],
                                       bouta_s[:, ob * 512:(ob + 1) * 512],
                                       start=False, stop=True)
                        if ob == 1:
                            sm.bump(ins)
                mmv = sm.n
                a3_ctile_used[i] = mmv
                yv = finish_tile(i, mmv, xld, buf, ya_h, ztile, True, "y1",
                                 xguard)
                a3_psfree[0] = yv
            barrier()
            nc.leave_named_scope("A3", _scopes["A3"], False)

    a3_ctile_used = {}
    a3_psfree = [0]

    def a2_head(h, hh, wqkvT, bqk_s, bv_s, qT, kT, vT, vb, pT, recb, recf,
                cstg, wq_done, vb_ones):
        w3 = wqkvT.rearrange("p c (b o) -> p c b o", b=3)
        for tbk in range(8):
            bk = (tbk % 2) * 2
            evv = sv.n

            @TE
            def _(t):
                sv.wait(t, wq_done)
                sv.wait(t, evv)
                for cc in range(8):
                    t.matmul(ps_all[0:64, bk, :],
                             w3[:, cc, 0, hh * 64:(hh + 1) * 64],
                             xT[:, cc, tbk * 512:(tbk + 1) * 512],
                             start=(cc == 0), stop=(cc == 7))
                for cc in range(8):
                    t.matmul(ps_all[64:128, bk, :],
                             w3[:, cc, 1, hh * 64:(hh + 1) * 64],
                             xT[:, cc, tbk * 512:(tbk + 1) * 512],
                             start=(cc == 0), stop=(cc == 7),
                             tile_position=(0, 64))
                for cc in range(8):
                    ins = t.matmul(ps_all[0:64, bk + 1, :],
                                   w3[:, cc, 2, hh * 64:(hh + 1) * 64],
                                   xT[:, cc, tbk * 512:(tbk + 1) * 512],
                                   start=(cc == 0), stop=(cc == 7))
                    if cc == 7:
                        sm.bump(ins)
            mmv = sm.n

            @VE
            def _(v):
                sm.wait(v, mmv)
                sl = slice(tbk * 512, (tbk + 1) * 512)
                v.tensor_scalar(qT[:, sl], ps_all[0:64, bk, :],
                                bqk_s[0:64, h:h + 1], None, OP.add)
                v.tensor_scalar(kT[:, sl], ps_all[64:128, bk, :],
                                bqk_s[64:128, h:h + 1], None, OP.add)
                sv.bump(v.tensor_scalar(vT[:, sl], ps_all[0:64, bk + 1, :],
                                        bv_s[:, h:h + 1], None, OP.add))
        vT_done = sv.n
        if h == 0 and "q0" in tap_hbm:
            @GE
            def _(g):
                ss.wait(g, vT_done)
                g.dma_start(tap_hbm["q0"], qT[:]).then_inc(sw.h, 16)
                g.dma_start(tap_hbm["k0"], kT[:]).then_inc(sw.h, 16)
                g.dma_start(tap_hbm["v0"], vT[:]).then_inc(sw.h, 16)
            sw.n += 48
            barrier()

        for grp in range(4):
            @TE
            def _(t):
                sv.wait(t, vT_done)
                ss.wait(t, tp_ss[0])
                sv.wait(t, tp_sv[0])
                sv.wait(t, vb_ones)
                for k in range(8):
                    kc = grp * 8 + k
                    ins = t.transpose(ps_tp_bf[:, k * 64:(k + 1) * 64],
                                      vT[:, kc * 128:(kc + 1) * 128],
                                      eye[0:64, 0:64])
                    if k == 7:
                        sm.bump(ins)
            tpv = sm.n

            @VE
            def _(v):
                sm.wait(v, tpv)
                sv.bump(v.tensor_copy(
                    vb[:, grp * 8:(grp + 1) * 8, 0:64],
                    ps_tp_bf[:, 0:512].rearrange("p (c t) -> p c t", c=8)))
            tp_sv[0] = sv.n
        vb_done = sv.n

        groups = [list(range(g * 3, min(32, g * 3 + 3))) for g in range(11)]
        ng = len(groups)
        for qb in range(8):
            qsl = slice(qb * 512, (qb + 1) * 512)
            edone = {}
            cdone = {}
            sdone = {}

            def emit_s(g):
                chunks = groups[g]
                bs = (g % 2) * 3

                @TE
                def _(t):
                    if g >= 2:
                        ss.wait(t, edone[g - 2])
                    sv.wait(t, vb_done)
                    for idx, kc in enumerate(chunks):
                        ins = t.matmul(ps_all[:, bs + idx, :],
                                       kT[:, kc * 128:(kc + 1) * 128],
                                       qT[:, qsl], start=True, stop=True)
                        if idx == len(chunks) - 1:
                            sm.bump(ins)
                sdone[g] = sm.n

            def emit_e(g):
                chunks = groups[g]
                bs = (g % 2) * 3

                @SE
                def _(s):
                    sm.wait(s, sdone[g])
                    if g >= 2:
                        sm.wait(s, cdone[g - 2])
                    ss.bump(s.activation(pT[:, bs:bs + len(chunks), :],
                                         ps_all[:, bs:bs + len(chunks), :],
                                         AF.Exp, scale=0.125))
                edone[g] = ss.n

            def emit_c(g):
                chunks = groups[g]
                bs = (g % 2) * 3

                @TE
                def _(t):
                    ss.wait(t, edone[g])
                    if g == 0:
                        sv.wait(t, a2_ctx_free[0])
                    for idx, kc in enumerate(chunks):
                        ins = t.matmul(ps_c[0:65, :],
                                       vb[:, kc, 0:65],
                                       pT[:, bs + idx, :],
                                       start=(kc == 0), stop=(kc == 31))
                        if idx == len(chunks) - 1:
                            sm.bump(ins)
                cdone[g] = sm.n

            emit_s(0)
            emit_e(0)
            for g in range(1, ng):
                emit_s(g)
                emit_e(g)
                emit_c(g - 1)
            emit_c(ng - 1)
            ctx_done = sm.n

            @VE
            def _(v):
                sm.wait(v, ctx_done)
                with nc.allow_low_precision(reason="bf16 softmax recip bcast"):
                    sv.bump(v.reciprocal(recb[:], ps_c[64:65, :]))
            rcv = sv.n

            @TE
            def _(t):
                sv.wait(t, rcv)
                ss.wait(t, tp_ss[0])
                sv.wait(t, tp_sv[0])
                sm.bump(t.matmul(ps_tp[0:64, :], ones_r[0:1, 0:64], recb[:],
                                 start=True, stop=True))
            bcv = sm.n

            @SE
            def _(s):
                sm.wait(s, bcv)
                sv.wait(s, a2_ctx_free[0])
                ss.bump(s.copy(recf[:], ps_tp[0:64, :]))
            a2_recf_free[0] = ss.n
            tp_ss[0] = ss.n
            rfv = ss.n

            @VE
            def _(v):
                ss.wait(v, rfv)
                so.wait(v, a2_cstg_free[0])
                sv.bump(v.tensor_tensor(cstg[:], ps_c[0:64, :], recf[:],
                                        OP.mult))
            csv = sv.n
            a2_ctx_free[0] = csv

            @SY
            def _(s):
                sv.wait(s, csv)
                s.dma_start(ctx_h[(h % 2) * 64:(h % 2) * 64 + 64, h // 2, qsl],
                            cstg[:]).then_inc(so.h, 16)
            so.n += 16
            a2_cstg_free[0] = so.n

    a2_ctx_free = [0]
    a2_cstg_free = [0]
    a2_recf_free = [0]

    def phase_b(resid_src, dst, next_ln, tap, tag):
        with ExitStack() as phx:
            pec = phx.enter_context
            winT = pec(st("b" + tag + "_winT", [128, 8, DI], BF16))
            woutfT = pec(st("b" + tag + "_woutfT", [128, 16, D], BF16))
            idxv = pec(st("b" + tag + "_idx", [128, PKH, NT], U32))
            gatev = pec(st("b" + tag + "_gate", [128, PKH, NT], F32))
            xftile = pec(st("b" + tag + "_xf", [128, D], BF16))
            linb = pec(st("b" + tag + "_lin", [128, PKH], F32))
            loutb = pec(st("b" + tag + "_lout", [128, PKH], F32))
            b1_scope = ExitStack()
            b1c = b1_scope.enter_context
            sxv = b1c(st("b" + tag + "_sx", [128, PKH, NT], F32))
            wqT = b1c(st("b" + tag + "_wqT", [128, 8, 512], BF16))
            qTb = b1c(st("b" + tag + "_qT", [128, 4, 512], F32))
            keysT = b1c(st("b" + tag + "_keysT", [128, 4, 128], F32))
            simt = b1c(st("b" + tag + "_simt", [128, 128], F32))
            mv8 = b1c(st("b" + tag + "_mv8", [128, 8], F32))
            mi8 = b1c(st("b" + tag + "_mi8", [128, 8], U32))

            load_T(win_h, DI, D, simple_evacT(winT))

            def woutf_evac(base):
                def f(j, tpv):
                    @VE
                    def _(v):
                        sm.wait(v, tpv)
                        sv.bump(v.tensor_copy(
                            woutfT[:, base:base + 8, j * 128:(j + 1) * 128],
                            ps_tp_bf.rearrange("p (c t) -> p c t", c=8)))
                return f
            load_T(wout_h[:, 0:1024], D, 1024, woutf_evac(0))
            load_T(wout_h[:, 1024:2048], D, 1024, woutf_evac(8))
            barrier()

            _scopes["B1q" + tag] = nc.enter_named_scope("B1q" + tag, False)[0]
            bank_free = [sv.n] * 4
            sim_free = sv.n
            key_free = sm.n
            for ph_i in range(16):
                p_, h_ = ph_i // 8, ph_i % 8
                row0 = (p_ * 8 + h_) * 512
                load_T(wq_h[row0:row0 + 512, :], 512, D, simple_evacT(wqT))
                wqv = sv.n

                kstage = qTb[:, 0, :]

                @SY
                def _(s):
                    sm.wait(s, key_free)
                    sv.wait(s, b1_qtb_writes[0])
                    s.dma_start(kstage,
                                keys_h[h_, :, p_, :]).then_inc(sx.h, 16)
                sx.n += 16
                kldv = sx.n

                @TE
                def _(t):
                    sx.wait(t, kldv)
                    sv.wait(t, tp_sv[0])
                    ss.wait(t, tp_ss[0])
                    for m in range(4):
                        ins = t.transpose(ps_tp[:, m * 128:(m + 1) * 128],
                                          kstage[:, m * 128:(m + 1) * 128],
                                          eyef[:])
                        if m == 3:
                            sm.bump(ins)
                ktp = sm.n

                @VE
                def _(v):
                    sm.wait(v, ktp)
                    sv.bump(v.tensor_copy(
                        keysT[:, :, :],
                        ps_tp[:, 0:512].rearrange("p (c t) -> p c t", c=4)))
                tp_sv[0] = sv.n
                kv = sv.n

                for tbk in range(8):
                    for m in range(4):
                        bk = m

                        @TE
                        def _(t):
                            sv.wait(t, wqv)
                            sv.wait(t, bank_free[bk])
                            for cc in range(8):
                                ins = t.matmul(
                                    ps_all[:, bk, :],
                                    wqT[:, cc, m * 128:(m + 1) * 128],
                                    xT[:, cc, tbk * 512:(tbk + 1) * 512],
                                    start=(cc == 0), stop=(cc == 7))
                                if cc == 7:
                                    sm.bump(ins)
                        mmv = sm.n

                        @VE
                        def _(v):
                            sm.wait(v, mmv)
                            sm.wait(v, qtb_free[0])
                            sm.wait(v, ktp)
                            sv.bump(v.tensor_copy(qTb[:, m, :],
                                                  ps_all[:, bk, :]))
                        bank_free[bk] = sv.n
                        b1_qtb_writes[0] = sv.n
                    qdv = ss.n

                    for q in range(4):
                        i = tbk * 4 + q

                        @TE
                        def _(t):
                            ss.wait(t, qdv)
                            sv.wait(t, sim_free)
                            sv.wait(t, kv)
                            for m in range(4):
                                ins = t.matmul(ps_c[:, 0:128],
                                               qTb[:, m, q * 128:(q + 1) * 128],
                                               keysT[:, m, :],
                                               start=(m == 0), stop=(m == 3))
                                if m == 3:
                                    sm.bump(ins)
                        smv = sm.n
                        if q == 3:
                            qtb_free[0] = smv
                        key_free = smv

                        @VE
                        def _(v):
                            sm.wait(v, smv)
                            sv.bump(v.tensor_copy(simt[:], ps_c[:, 0:128]))
                        sim_free = sv.n
                        cpv = sv.n

                        @VE
                        def _(v):
                            sv.wait(v, cpv)
                            sv.bump(v.max(mv8[:], simt[:]))
                        mxv = sv.n

                        @VE
                        def _(v):
                            sv.wait(v, mxv)
                            sv.bump(v.max_index(mi8[:], mv8[:], simt[:]))
                        miv = sv.n

                        if p_ == 0:
                            @VE
                            def _(v):
                                sv.wait(v, miv)
                                v.tensor_copy(sxv[:, h_, i:i + 1], mv8[:, 0:1])
                                sv.bump(v.tensor_scalar(
                                    idxv[:, h_, i:i + 1], mi8[:, 0:1],
                                    128.0, None, OP.mult))
                        else:
                            @VE
                            def _(v):
                                sv.wait(v, miv)
                                v.tensor_tensor(
                                    idxv[:, h_, i:i + 1], idxv[:, h_, i:i + 1],
                                    mi8[:, 0:1], OP.add)
                                sv.bump(v.tensor_scalar(
                                    gatev[:, h_, i:i + 1],
                                    sxv[:, h_, i:i + 1], mv8[:, 0:1], 0.0,
                                    OP.add, OP.max))
            nc.leave_named_scope("B1q" + tag, _scopes["B1q" + tag], False)
            barrier()
            if "idx" in tap_set and tag == "1":
                @GE
                def _(g):
                    g.dma_start(tap_hbm["idx"], idxv[:]).then_inc(sw.h, 16)
                    g.dma_start(tap_hbm["gate"], gatev[:]).then_inc(sw.h, 16)
                sw.n += 32
                barrier()
            b1_scope.close()
            _scopes["B3_" + tag] = nc.enter_named_scope("B3_" + tag, False)[0]
            g_ina = pec(st("b" + tag + "_gina", [128, 2, D], F32))
            g_inb = pec(st("b" + tag + "_ginb", [128, 2, DI], F32))
            g_outa = pec(st("b" + tag + "_gouta", [128, DI], F32))
            g_outb = pec(st("b" + tag + "_goutb", [128, D], F32))
            hidbf = pec(st("b" + tag + "_hidbf", [128, DI], BF16))
            hT = pec(st("b" + tag + "_hT", [128, 16, 128], BF16))
            prod = hT[:, 0:8, :].rearrange("p c t -> p (c t)")
            hacc = pec(st("b" + tag + "_hacc", [128, DI], F32))
            lp = pec(st("b" + tag + "_lp", [128, PKH], F32))
            lq = pec(st("b" + tag + "_lq", [128, PKH], F32))

            scn0 = scn.n
            sco0 = sco.n
            sg0 = sg.n
            sg_in_base = {}
            sg_out_base = {}
            sgc = [sg.n]
            GQD = 4

            def emit_in(i):
                sg_in_base[i] = sgc[0]

                @GE
                def _(g):
                    for h_ in range(PKH):
                        ghc = i * PKH + h_
                        if ghc >= 2:
                            scn.wait(g, scn0 + 2 * ghc - 3)
                        sg.wait(g, sg.n - (GQD - 1) * 16)
                        g.indirect_dma_start(
                            g_ina[:, ghc % 2, :], None, tbl["ina"],
                            bass.IndirectOffsetOnAxis(idxv[:, h_, i:i + 1], 0)
                        ).then_inc(sg.h, 16)
                        sg.n += 16
                        if ghc >= 2:
                            scn.wait(g, scn0 + 2 * ghc - 2)
                        sg.wait(g, sg.n - (GQD - 1) * 16)
                        g.indirect_dma_start(
                            g_inb[:, ghc % 2, :], None, tbl["inb"],
                            bass.IndirectOffsetOnAxis(idxv[:, h_, i:i + 1], 0)
                        ).then_inc(sg.h, 16)
                        sg.n += 16
                sgc[0] = sg.n

            def emit_out(i):
                sg_out_base[i] = sgc[0]

                @GE
                def _(g):
                    for h_ in range(PKH):
                        ghc = i * PKH + h_
                        if ghc >= 1:
                            sco.wait(g, sco0 + 2 * ghc)
                        sg.wait(g, sg.n - (GQD - 1) * 16)
                        g.indirect_dma_start(
                            g_outa[:, :], None, tbl["outa"],
                            bass.IndirectOffsetOnAxis(idxv[:, h_, i:i + 1], 0)
                        ).then_inc(sg.h, 16)
                        sg.n += 16
                        sg.wait(g, sg.n - (GQD - 1) * 16)
                        g.indirect_dma_start(
                            g_outb[:, :], None, tbl["outb"],
                            bass.IndirectOffsetOnAxis(idxv[:, h_, i:i + 1], 0)
                        ).then_inc(sg.h, 16)
                        sg.n += 16
                sgc[0] = sg.n

            for i in range(NT):
                emit_in(i)
                emit_out(i)

            def gwait(ghc, k):
                i, h_ = divmod(ghc, PKH)
                if k < 2:
                    return sg_in_base[i] + (2 * h_ + k + 1) * 16
                return sg_out_base[i] + (2 * h_ + (k - 2) + 1) * 16

            hid_free = sv.n
            out_free = sv.n
            xguard = {}
            b3_xf_rd = [0]
            for i in range(NT):
                buf = i % 2

                @TE
                def _(t):
                    sv.wait(t, tp_sv[0])
                    ss.wait(t, tp_ss[0])
                    sv.wait(t, b3_xf_rd[0])
                    for cc in range(8):
                        ins = t.transpose(ps_tp_bf[:, cc * 128:(cc + 1) * 128],
                                          xT[:, cc, i * 128:(i + 1) * 128],
                                          eye[:])
                        if cc == 7:
                            sm.bump(ins)
                tpv = sm.n

                @VE
                def _(v):
                    sm.wait(v, tpv)
                    sv.bump(v.tensor_copy(xftile[:], ps_tp_bf[:, 0:1024]))
                tp_sv[0] = sv.n
                xfv = sv.n

                @TE
                def _(t):
                    sv.wait(t, hid_free)
                    for nb in range(4):
                        for cc in range(8):
                            ins = t.matmul(ps_all[:, nb, :],
                                           xT[:, cc, i * 128:(i + 1) * 128],
                                           winT[:, cc, nb * 512:(nb + 1) * 512],
                                           start=(cc == 0), stop=(cc == 7))
                        if nb == 3:
                            sm.bump(ins)
                winv = sm.n

                for h_ in range(PKH):
                    ghc = i * PKH + h_

                    @VE
                    def _(v):
                        sg.wait(v, gwait(ghc, 0))
                        sv.wait(v, xfv)
                        v.scalar_tensor_tensor(
                            prod[:, 0:1024], g_ina[:, ghc % 2, :], 1.0,
                            xftile[:],
                            OP.mult, OP.mult,
                            accum_out=linb[:, h_:h_ + 1])
                        sv.bump(v.nop())
                        scn.bump(v.nop())
                    linv = sv.n
                    if h_ == PKH - 1:
                        b3_xf_rd[0] = sv.n

                    @VE
                    def _(v):
                        sv.wait(v, linv)
                        sv.bump(v.tensor_scalar(lp[:, h_:h_ + 1],
                                                linb[:, h_:h_ + 1],
                                                gatev[:, h_, i:i + 1], None,
                                                OP.mult))
                    lpv = sv.n

                    @VE
                    def _(v):
                        sv.wait(v, lpv)
                        sg.wait(v, gwait(ghc, 1))
                        if h_ == 0:
                            sm.wait(v, winv)
                            v.scalar_tensor_tensor(
                                hacc[:], g_inb[:, ghc % 2, :],
                                lp[:, h_:h_ + 1],
                                ps_all[:, 0:4, :].rearrange("p a b -> p (a b)"),
                                OP.mult, OP.add)
                        else:
                            v.scalar_tensor_tensor(
                                hacc[:], g_inb[:, ghc % 2, :],
                                lp[:, h_:h_ + 1], hacc[:],
                                OP.mult, OP.add)
                        sv.bump(v.nop())
                        scn.bump(v.nop())
                    if h_ == 0:
                        hid_free = sv.n
                hidv = sv.n

                @SE
                def _(s):
                    sv.wait(s, hidv)
                    ss.bump(s.activation(hidbf[:], hacc[:], AF.Gelu))
                gfv = ss.n

                for grp in range(2):
                    @TE
                    def _(t):
                        ss.wait(t, gfv)
                        sv.wait(t, tp_sv[0])
                        ss.wait(t, tp_ss[0])
                        for k in range(8):
                            cc = grp * 8 + k
                            ins = t.transpose(
                                ps_tp_bf[:, k * 128:(k + 1) * 128],
                                hidbf[:, cc * 128:(cc + 1) * 128], eye[:])
                            if k == 7:
                                sm.bump(ins)
                    tpv = sm.n

                    @VE
                    def _(v):
                        sm.wait(v, tpv)
                        sv.bump(v.tensor_copy(
                            hT[:, grp * 8:(grp + 1) * 8, :],
                            ps_tp_bf.rearrange("p (c t) -> p c t", c=8)))
                    tp_sv[0] = sv.n
                hTv = sv.n

                @TE
                def _(t):
                    sv.wait(t, hTv)
                    sv.wait(t, out_free)
                    for ob in range(2):
                        for cc in range(16):
                            ins = t.matmul(
                                ps_all[:, 4 + ob, :], hT[:, cc, :],
                                woutfT[:, cc, ob * 512:(ob + 1) * 512],
                                start=(cc == 0), stop=(cc == 15))
                        if ob == 1:
                            sm.bump(ins)
                woutv = sm.n

                @SY
                def _(s):
                    g = xguard.get(i - 2)
                    if g is not None:
                        g[0].wait(s, g[1])
                    s.dma_start(xtile[:, buf, :],
                                resid_src[i * 128:(i + 1) * 128, :]).then_inc(sx.h, 16)
                sx.n += 16
                xld = sx.n

                yss, yso = ytg["ss"], ytg["so"]
                for h_ in range(PKH):
                    ghc = i * PKH + h_

                    @VE
                    def _(v):
                        sg.wait(v, gwait(ghc, 2))
                        ss.wait(v, gfv)
                        v.scalar_tensor_tensor(
                            hacc.bitcast(BF16)[:, 0:DI], g_outa[:, :],
                            1.0, hidbf[:], OP.mult, OP.mult,
                            accum_out=loutb[:, h_:h_ + 1])
                        sv.bump(v.nop())
                        sco.bump(v.nop())
                    lov = sv.n

                    @VE
                    def _(v):
                        sv.wait(v, lov)
                        sv.bump(v.tensor_scalar(lq[:, h_:h_ + 1],
                                                loutb[:, h_:h_ + 1],
                                                gatev[:, h_, i:i + 1], None,
                                                OP.mult))
                    lqv = sv.n

                    @VE
                    def _(v):
                        sv.wait(v, lqv)
                        sg.wait(v, gwait(ghc, 3))
                        if h_ == 0:
                            sm.wait(v, woutv)
                            ss.wait(v, yss)
                            so.wait(v, yso)
                            v.scalar_tensor_tensor(
                                ytile[:], g_outb[:, :], lq[:, h_:h_ + 1],
                                ps_all[:, 4:6, :].rearrange("p a b -> p (a b)"),
                                OP.mult, OP.add)
                        else:
                            v.scalar_tensor_tensor(
                                ytile[:], g_outb[:, :], lq[:, h_:h_ + 1],
                                ytile[:], OP.mult, OP.add)
                        sv.bump(v.nop())
                        sco.bump(v.nop())
                    if h_ == 0:
                        out_free = sv.n
                oaxv = sv.n

                @VE
                def _(v):
                    sv.wait(v, oaxv)
                    sx.wait(v, xld)
                    sv.bump(v.tensor_tensor(ytile[:], ytile[:],
                                            xtile[:, buf, :], OP.add))
                yv = sv.n
                xguard[i] = (sv, yv)

                @SY
                def _(s):
                    sv.wait(s, yv)
                    s.dma_start(dst[i * 128:(i + 1) * 128, :],
                                ytile[:]).then_inc(so.h, 16)
                    if tap is not None and tap in tap_hbm:
                        s.dma_start(tap_hbm[tap][i * 128:(i + 1) * 128, :],
                                    ytile[:]).then_inc(so.h, 16)
                so.n += 16 * (2 if (tap in tap_hbm) else 1)
                ytg["so"] = so.n

                if next_ln:
                    szn, sdv, vdv = ln_chain(ytile, ntile,
                                             wait_before=[(sv, yv)])
                    tile_to_xT(i, ntile, ss, sdv)
                    ytg["ss"] = szn
            barrier()
            nc.leave_named_scope("B3_" + tag, _scopes["B3_" + tag], False)

    qtb_free = [0]
    b1_qtb_writes = [0]

    def phase_c():
        with ExitStack() as phx:
            pec = phx.enter_context
            outb_s = pec(st("c_outb", [1, D], BF16))
            tempc = pec(st("c_tempc", [128, 8], F32))
            wT = pec(st("c_wT", [128, 8, S], BF16))
            norm2p = pec(st("c_norm2p", [128, 8, 8], F32))
            norm2 = pec(st("c_norm2", [128, 8], F32))
            gvec = pec(st("c_g", [128, 8], F32))
            gmat = pec(st("c_gmat", [128, 8, 16], BF16))
            sqsc = pec(st("c_sqsc", [128, 2, 512], BF16))
            sts = pec(st("c_sts", [16, 512], F32))
            sti = pec(st("c_sti", [128, 16], F32))
            e16 = pec(st("c_e16", [128, 16], F32))
            esc = pec(st("c_esc", [128, 16], F32))
            pibf = pec(st("c_pibf", [128, NT, 16], BF16))
            piT = pec(st("c_piT", [16, S], BF16))
            zrow = pec(st("c_zrow", [16, 1], F32))
            zrowT = pec(st("c_zrowT", [1, 16], F32))
            zc = pec(st("c_zc", [128, 8], F32))
            dotsc = pec(st("c_dotsc", [128, 8], F32))
            attnc = pec(st("c_attnc", [128, 8], F32))
            wtile = pec(st("c_wtile", [128, D], BF16))
            w2tile = pec(st("c_w2tile", [128, D], BF16))
            tmp1 = pec(st("c_tmp1", [128, 512], F32))
            opT = pec(st("c_opT", [128, 8, 512], BF16))

            @SY
            def _(s):
                s.dma_start(tempc[:], tempc_h).then_inc(sx.h, 16)
            sx.n += 16
            tcv = sx.n

            @GE
            def _(g):
                g.dma_start(outb_s[:], outb_h).then_inc(sw.h, 16)
            sw.n += 16

            with ExitStack() as ph2:
                qkvwT = ph2.enter_context(st("c_qkvwT", [128, 8, D], BF16))
                load_T(qkvw_h, D, D, simple_evacT(qkvwT))
                barrier()
                bank_free = [ss.n] * 4
                for tbk in range(8):
                    for oc in range(8):
                        bk = (tbk * 8 + oc) % 4
                        evv = bank_free[bk]

                        @TE
                        def _(t):
                            ss.wait(t, evv)
                            for cc in range(8):
                                ins = t.matmul(
                                    ps_all[:, bk, :],
                                    qkvwT[:, cc, oc * 128:(oc + 1) * 128],
                                    xT[:, cc, tbk * 512:(tbk + 1) * 512],
                                    start=(cc == 0), stop=(cc == 7))
                                if cc == 7:
                                    sm.bump(ins)
                        mmv = sm.n

                        @SE
                        def _(s):
                            sm.wait(s, mmv)
                            ss.bump(s.copy(
                                wT[:, oc, tbk * 512:(tbk + 1) * 512],
                                ps_all[:, bk, :]))
                        bank_free[bk] = ss.n
                barrier()
            wTv = ss.n

            outwT = pec(st("c_outwT", [128, 8, D], BF16))
            load_T(outw_h, D, D, simple_evacT(outwT))

            for cc in range(8):
                for blk in range(8):
                    par = blk % 2

                    @SE
                    def _(s):
                        ss.wait(s, wTv)
                        sv.wait(s, c_sq_free[0])
                        ss.bump(s.activation(
                            sqsc[:, par, :],
                            wT[:, cc, blk * 512:(blk + 1) * 512], AF.Square,
                            accum_out=norm2p[:, cc, blk:blk + 1]))
            nv = ss.n

            @VE
            def _(v):
                ss.wait(v, nv)
                sv.bump(v.tensor_reduce(norm2[:], norm2p[:],
                                        mybir.AxisListType.X, OP.add))
            n2r = sv.n

            @VE
            def _(v):
                sv.wait(v, n2r)
                sv.bump(v.tensor_scalar(norm2[:], norm2[:], 1e-24, None,
                                        OP.max))
            n2v = sv.n

            @VE
            def _(v):
                sv.wait(v, n2v)
                sv.bump(v.reciprocal(gvec[:], norm2[:]))
            rgv = sv.n

            @VE
            def _(v):
                sv.wait(v, rgv)
                sx.wait(v, tcv)
                sv.bump(v.tensor_tensor(gvec[:], gvec[:], tempc[:], OP.mult))
            gv = sv.n

            @VE
            def _(v):
                sv.wait(v, gv)
                sv.bump(v.memset(gmat[:], 0.0))
            gm0 = sv.n

            @VE
            def _(v):
                sv.wait(v, gm0)
                for j in range(8):
                    v.tensor_copy(gmat[0:64, j, 2 * j:2 * j + 1],
                                  gvec[0:64, j:j + 1])
                    ins = v.tensor_copy(gmat[64:128, j, 2 * j + 1:2 * j + 2],
                                        gvec[64:128, j:j + 1])
                sv.bump(ins)
            gmv = sv.n

            pst_free = sv.n
            for tbk in range(8):
                for cc in range(8):
                    par = cc % 2

                    k_glob = tbk * 8 + cc

                    @SE
                    def _(s):
                        sv.wait(s, gmv)
                        if k_glob >= 2:
                            sm.wait(s, sq_hist[k_glob - 2])
                        ss.bump(s.activation(
                            sqsc[:, par, :],
                            wT[:, cc, tbk * 512:(tbk + 1) * 512], AF.Square))
                    sqv = ss.n

                    @TE
                    def _(t):
                        ss.wait(t, sqv)
                        if tbk > 0 and cc == 0:
                            ss.wait(t, c_sts_done[0])
                        ins = t.matmul(ps_c[0:16, :], gmat[:, cc, :],
                                       sqsc[:, par, :],
                                       start=(cc == 0), stop=(cc == 7))
                        sm.bump(ins)
                    sq_hist.append(sm.n)
                stv = sm.n

                @SE
                def _(s):
                    sm.wait(s, stv)
                    ss.bump(s.copy(sts[:], ps_c[0:16, :]))
                c_sts_done[0] = ss.n
                stsv = ss.n

                for q in range(4):
                    i = tbk * 4 + q

                    @TE
                    def _(t):
                        ss.wait(t, stsv)
                        sv.wait(t, tp_sv[0])
                        ss.wait(t, tp_ss[0])
                        sm.bump(t.transpose(ps_tp[:, 0:16],
                                            sts[:, q * 128:(q + 1) * 128],
                                            eye16f[:]))
                    tpv = sm.n

                    @VE
                    def _(v):
                        sm.wait(v, tpv)
                        sv.bump(v.tensor_copy(sti[:], ps_tp[:, 0:16]))
                    tp_sv[0] = sv.n
                    stiv = sv.n

                    @SE
                    def _(s):
                        sv.wait(s, stiv)
                        ss.bump(s.activation(e16[:], sti[:], AF.Exp))
                    ev = ss.n

                    @VE
                    def _(v):
                        ss.wait(v, ev)
                        sv.bump(v.tensor_scalar(esc[:], e16[:], 1.0, None,
                                                OP.mult, OP.add,
                                                accum_out=tb[:]))
                    sev = sv.n

                    @VE
                    def _(v):
                        sv.wait(v, sev)
                        sv.bump(v.reciprocal(tc[:], tb[:]))
                    rcv = sv.n

                    @VE
                    def _(v):
                        sv.wait(v, rcv)
                        sv.bump(v.tensor_scalar(pibf[:, i, :], e16[:], tc[:],
                                                None, OP.mult))
                    piv = sv.n

                    @TE
                    def _(t):
                        sv.wait(t, piv)
                        sv.wait(t, tp_sv[0])
                        ss.wait(t, tp_ss[0])
                        sm.bump(t.transpose(ps_tp.bitcast(BF16)[0:16, 0:128],
                                            pibf[:, i, :], eye[:]))
                    ptv = sm.n

                    @VE
                    def _(v):
                        sm.wait(v, ptv)
                        sv.bump(v.tensor_copy(piT[:, i * 128:(i + 1) * 128],
                                              ps_tp.bitcast(BF16)[0:16, 0:128]))
                    tp_sv[0] = sv.n
                    pst_free = sv.n

            @TE
            def _(t):
                sv.wait(t, pst_free)
                ins = None
                for i in range(NT):
                    ins = t.matmul(ps_c[0:16, 0:1], pibf[:, i, :], ones_c[:],
                                   start=(i == 0), stop=(i == NT - 1))
                sm.bump(ins)
            zmv = sm.n

            @SE
            def _(s):
                sm.wait(s, zmv)
                ss.bump(s.copy(zrow[:], ps_c[0:16, 0:1]))
            zev = ss.n

            @VE
            def _(v):
                ss.wait(v, zev)
                sv.bump(v.tensor_scalar(zrow[:], zrow[:], 1e-8, None, OP.add))
            zadd = sv.n

            @TE
            def _(t):
                sv.wait(t, zadd)
                sv.wait(t, tp_sv[0])
                ss.wait(t, tp_ss[0])
                sm.bump(t.transpose(ps_tp[0:1, 0:16], zrow[:], eye16f[:]))
            ztv = sm.n

            @SE
            def _(s):
                sm.wait(s, ztv)
                ss.bump(s.copy(zrowT[:], ps_tp[0:1, 0:16]))
            tp_ss[0] = ss.n
            zrv = ss.n

            @TE
            def _(t):
                ss.wait(t, zrv)
                sv.wait(t, tp_sv[0])
                sm.bump(t.matmul(ps_tp[:, 0:16], ones_rf[:], zrowT[:],
                                 start=True, stop=True))
            zbv = sm.n

            @VE
            def _(v):
                sm.wait(v, zbv)
                for j in range(8):
                    v.tensor_copy(zc[0:64, j:j + 1],
                                  ps_tp[0:64, 2 * j:2 * j + 1])
                    ins = v.tensor_copy(zc[64:128, j:j + 1],
                                        ps_tp[64:128, 2 * j + 1:2 * j + 2])
                sv.bump(ins)
            tp_sv[0] = sv.n
            zcv = sv.n

            @VE
            def _(v):
                sv.wait(v, zcv)
                sv.bump(v.reciprocal(zc[:], zc[:]))
            zrcv = sv.n

            c_wtile_rd = [0]
            for i in range(NT):
                @TE
                def _(t):
                    sv.wait(t, tp_sv[0])
                    ss.wait(t, tp_ss[0])
                    ss.wait(t, c_wtile_rd[0])
                    if i == 0:
                        sv.wait(t, zrcv)
                    for cc in range(8):
                        ins = t.transpose(ps_tp_bf[:, cc * 128:(cc + 1) * 128],
                                          wT[:, cc, i * 128:(i + 1) * 128],
                                          eye[:])
                        if cc == 7:
                            sm.bump(ins)
                tpv = sm.n

                @VE
                def _(v):
                    sm.wait(v, tpv)
                    sv.bump(v.tensor_copy(wtile[:], ps_tp_bf[:, 0:1024]))
                tp_sv[0] = sv.n
                wtv = sv.n

                @SE
                def _(s):
                    sv.wait(s, wtv)
                    ss.bump(s.activation(w2tile[:], wtile[:], AF.Square))
                w2v = ss.n
                c_wtile_rd[0] = ss.n

                @TE
                def _(t):
                    ss.wait(t, w2v)
                    for cc in range(8):
                        ins = t.matmul(ps_c[:, cc * 16:(cc + 1) * 16],
                                       w2tile[:, cc * 128:(cc + 1) * 128],
                                       pibf[:, i, :],
                                       start=(i == 0), stop=(i == NT - 1))
                        if cc == 7:
                            sm.bump(ins)
            dmv = sm.n

            @VE
            def _(v):
                sm.wait(v, dmv)
                for j in range(8):
                    v.tensor_copy(dotsc[0:64, j:j + 1],
                                  ps_c[0:64, j * 16 + 2 * j:j * 16 + 2 * j + 1])
                    ins = v.tensor_copy(
                        dotsc[64:128, j:j + 1],
                        ps_c[64:128, j * 16 + 2 * j + 1:j * 16 + 2 * j + 2])
                sv.bump(ins)
            dcv = sv.n

            @VE
            def _(v):
                sv.wait(v, dcv)
                sv.bump(v.tensor_tensor(attnc[:], dotsc[:], zc[:], OP.mult))
            acv0 = sv.n

            @VE
            def _(v):
                sv.wait(v, acv0)
                sv.bump(v.tensor_scalar(attnc[:], attnc[:], 1.0, None,
                                        OP.add))
            acv = sv.n

            @VE
            def _(v):
                sv.wait(v, acv)
                sv.bump(v.reciprocal(attnc[:], attnc[:]))
            afv = sv.n

            pp_free = sv.n
            xguard = {}
            for tbk in range(8):
                for cc in range(8):
                    evv = pp_free

                    @TE
                    def _(t):
                        sv.wait(t, afv)
                        sv.wait(t, evv)
                        sm.bump(t.matmul(ps_c[:, 0:512],
                                         emat[0:16, cc * 128:(cc + 1) * 128],
                                         piT[:, tbk * 512:(tbk + 1) * 512],
                                         start=True, stop=True))
                    pev = sm.n

                    @VE
                    def _(v):
                        sm.wait(v, pev)
                        v.tensor_scalar(tmp1[:],
                                        wT[:, cc, tbk * 512:(tbk + 1) * 512],
                                        attnc[:, cc:cc + 1], -1.0,
                                        OP.mult, OP.mult)
                        sv.bump(v.tensor_tensor(opT[:, cc, :], tmp1[:],
                                                ps_c[:, 0:512], OP.mult))
                    pp_free = sv.n
                opv = sv.n

                for q in range(4):
                    i = tbk * 4 + q
                    buf = i % 2

                    @SY
                    def _(s):
                        g = xguard.get(i - 2)
                        if g is not None:
                            g[0].wait(s, g[1])
                        s.dma_start(xtile[:, buf, :],
                                    yb_h[i * 128:(i + 1) * 128, :]).then_inc(sx.h, 16)
                    sx.n += 16
                    xld = sx.n
                    oevv = c_out_free[0]

                    @TE
                    def _(t):
                        sv.wait(t, opv)
                        sv.wait(t, oevv)
                        for ob in range(2):
                            for cc in range(8):
                                t.matmul(ps_all[:, 4 + ob, :],
                                         opT[:, cc, q * 128:(q + 1) * 128],
                                         outwT[:, cc, ob * 512:(ob + 1) * 512],
                                         start=(cc == 0), stop=False)
                            ins = t.matmul(ps_all[:, 4 + ob, :], ones_r[:],
                                           outb_s[:, ob * 512:(ob + 1) * 512],
                                           start=False, stop=True)
                            if ob == 1:
                                sm.bump(ins)
                    mmv = sm.n
                    yv = finish_tile(i, mmv, xld, buf, ya_h, None, True,
                                     "y3", xguard)
                    c_out_free[0] = yv
            barrier()

    c_out_free = [0]
    c_sq_free = [0]
    c_sts_done = [0]
    sq_hist = []

    if "A" in phases:
        with nc.named_scope("phaseA"):
            phase_a()
    if "B1" in phases:
        with nc.named_scope("phaseB1"):
            phase_b(resid_src=ya_h, dst=yb_h, next_ln=("C" in phases), tap="y2", tag="1")
    if "C" in phases:
        with nc.named_scope("phaseC"):
            phase_c()
    if "B2" in phases:
        with nc.named_scope("phaseB2"):
            phase_b(resid_src=ya_h, dst=out_h, next_ln=False, tap=None, tag="2")

    @SY
    def _(s):
        so.wait(s, so.n)
        sg.wait(s, sg.n)
        sw.wait(s, sw.n)
        sx.wait(s, sx.n)

    es.close()
    return nc


def _consts():
    c = {}
    c["c_eye_bf"] = np.eye(128, dtype=ml_dtypes.bfloat16)
    c["c_eye_f"] = np.eye(128, dtype=np.float32)
    c["c_eye16_f"] = np.eye(16, dtype=np.float32)
    c["c_ones_1x128_bf"] = np.ones((1, 128), dtype=ml_dtypes.bfloat16)
    c["c_ones_1x128_f"] = np.ones((1, 128), dtype=np.float32)
    c["c_ones_128x1_bf"] = np.ones((128, 1), dtype=ml_dtypes.bfloat16)
    em = np.zeros((16, D), dtype=ml_dtypes.bfloat16)
    for h in range(16):
        em[h, h * 64:(h + 1) * 64] = 1
    c["c_emat_bf"] = em
    return c


def _in_map(core, inputs):
    m = {}
    m["x"] = inputs["L0_attn_a_hidden_states"][core]
    m["gamma_row"] = inputs["L0_attn_a_ln_gamma"].reshape(1, D)
    m["beta_row"] = inputs["L0_attn_a_ln_beta"].reshape(1, D)
    m["W_qkv"] = inputs["L0_attn_a_W_qkv"]
    b = inputs["L0_attn_a_b_qkv"]
    bqk = np.zeros((128, NH), np.float32)
    bv = np.zeros((64, NH), np.float32)
    for h in range(NH):
        bqk[0:64, h] = b[h * 64:(h + 1) * 64]
        bqk[64:128, h] = b[1024 + h * 64:1024 + (h + 1) * 64]
        bv[:, h] = b[2048 + h * 64:2048 + (h + 1) * 64]
    m["bqk_r"] = bqk
    m["bv_r"] = bv
    m["W_out_a"] = inputs["L0_attn_a_W_out"]
    m["b_out_a"] = inputs["L0_attn_a_b_out"].reshape(1, D)
    m["attb_qkv_w"] = inputs["L0_attn_b_qkv_w"]
    t = np.asarray(inputs["L0_attn_b_temp"]).reshape(16)
    tcm = np.zeros((128, 8), np.float32)
    for j in range(8):
        tcm[0:64, j] = t[2 * j]
        tcm[64:128, j] = t[2 * j + 1]
    m["attb_tempc"] = tcm
    m["attb_out_w"] = inputs["L0_attn_b_out_w"]
    m["attb_out_b"] = inputs["L0_attn_b_out_b"].reshape(1, D)
    m["ffn_Wq"] = inputs["L0_ffn_Wq"]
    m["ffn_keys"] = inputs["L0_ffn_keys"]
    m["ffn_W_in"] = inputs["L0_ffn_W_in"]
    m["ffn_W_out"] = inputs["L0_ffn_W_out"]
    m["ffn_in_a"] = inputs["L0_ffn_in_a"]
    m["ffn_in_b"] = inputs["L0_ffn_in_b"]
    m["ffn_out_a"] = inputs["L0_ffn_out_a"]
    m["ffn_out_b"] = inputs["L0_ffn_out_b"]
    m.update(_consts())
    return {k: np.ascontiguousarray(v) for k, v in m.items()}


def kernel(**inputs):
    nc = build_nc()
    in_maps = [_in_map(c, inputs) for c in range(8)]
    res = run_bass_kernel_spmd(nc, in_maps, core_ids=list(range(8)))
    out = np.stack([np.asarray(res.results[c]["y"]) for c in range(8)], axis=0)
    return out.astype(np.float32)
